# version 06
# baseline (speedup 1.0000x reference)
import math
import numpy as np
from contextlib import ExitStack

import concourse.bass as bass
import concourse.mybir as mybir
from concourse.bass_utils import run_bass_kernel_spmd

F32 = mybir.dt.float32
BF16 = mybir.dt.bfloat16
AF = mybir.ActivationFunctionType
ALU = mybir.AluOpType
AX = mybir.AxisListType.X

B, S, D = 2, 4096, 4096
NH, DH = 32, 128
FC = D // 128
T_OWN = 1024
T_SLAB = 1536
NT_OWN = 8
NT_SLAB = 12
SEGS = 4
SCALE = 1.0 / math.sqrt(DH)
EPS = 1e-5

RA = 0
RB = 65536
RC = 131072
RE = 147456
RF = 163840
ARENA_BYTES = 180224


class Ctx:
    def __init__(self):
        self.cnt = {}

    def bump(self, name, k=1):
        self.cnt[name] = self.cnt.get(name, 0) + k
        return self.cnt[name]

    def val(self, name):
        return self.cnt.get(name, 0)


def build_program(tap=None):
    nc = bass.Bass("TRN2", target_bir_lowering=False)

    xslab = nc.dram_tensor("xslab", [T_SLAB, D], F32, kind="ExternalInput").ap()
    wq_t = nc.dram_tensor("wq_t", [FC, 128, FC, 128], F32, kind="ExternalInput").ap()
    wo_t = nc.dram_tensor("wo_t", [FC, 128, FC, 128], F32, kind="ExternalInput").ap()
    wup_t = nc.dram_tensor("wup_t", [FC, 128, FC, 128], F32, kind="ExternalInput").ap()
    wk_t = nc.dram_tensor("wk_t", [128, FC, 128], F32, kind="ExternalInput").ap()
    wv_t = nc.dram_tensor("wv_t", [128, FC, 128], F32, kind="ExternalInput").ap()
    sh_t = nc.dram_tensor("sh_t", [128, FC, 64], F32, kind="ExternalInput").ap()
    bflat = nc.dram_tensor("bflat", [32, D], F32, kind="ExternalInput").ap()
    ebias = nc.dram_tensor("ebias", [4, 1], F32, kind="ExternalInput").ap()
    eexp = nc.dram_tensor("eexp", [4, 32], F32, kind="ExternalInput").ap()
    identf = nc.dram_tensor("identf", [128, 128], F32, kind="ExternalInput").ap()
    ptmat = nc.dram_tensor("ptmat", [128, 128], F32, kind="ExternalInput").ap()
    cosT = nc.dram_tensor("cosT", [128, T_SLAB], F32, kind="ExternalInput").ap()
    sinT = nc.dram_tensor("sinT", [128, T_SLAB], F32, kind="ExternalInput").ap()
    alphaA = nc.dram_tensor("alphaA", [128, 2 * SEGS], F32, kind="ExternalInput").ap()
    alphaB = nc.dram_tensor("alphaB", [128, 2 * SEGS], F32, kind="ExternalInput").ap()
    maskb = nc.dram_tensor("maskb", [128, 2], F32, kind="ExternalInput").ap()

    out = nc.dram_tensor("out", [T_OWN, D], F32, kind="ExternalOutput").ap()
    xres = nc.dram_tensor("xres", [T_OWN, D], F32, kind="Internal").ap()
    tap_dram = [None]

    ctx = Ctx()
    st = ExitStack()
    e = st.enter_context

    with st:
        s_x = e(nc.semaphore("s_x"))
        s_w = e(nc.semaphore("s_w"))
        s_tr = e(nc.semaphore("s_tr"))
        s_mm = e(nc.semaphore("s_mm"))
        s_v = e(nc.semaphore("s_v"))
        s_s = e(nc.semaphore("s_s"))
        s_res = e(nc.semaphore("s_res"))
        s_out = e(nc.semaphore("s_out"))
        s_bar = e(nc.semaphore("s_bar"))
        s_g = e(nc.semaphore("s_g"))

        cos_sb = e(nc.sbuf_tensor("cos_sb", [128, T_SLAB], F32))
        sin_sb = e(nc.sbuf_tensor("sin_sb", [128, T_SLAB], F32))
        aA_sb = e(nc.sbuf_tensor("aA_sb", [128, 2 * SEGS], F32))
        aB_sb = e(nc.sbuf_tensor("aB_sb", [128, 2 * SEGS], F32))
        mb_sb = e(nc.sbuf_tensor("mb_sb", [128, 2], F32))
        idb_sb = e(nc.sbuf_tensor("idb_sb", [128, 128], BF16))
        idf_sb = e(nc.sbuf_tensor("idf_sb", [128, 128], F32))
        pt_sb = e(nc.sbuf_tensor("pt_sb", [128, 128], BF16))
        eb_sb = e(nc.sbuf_tensor("eb_sb", [36, 1], F32))
        ee_sb = e(nc.sbuf_tensor("ee_sb", [4, 32], BF16))
        sh_sb = e(nc.sbuf_tensor("sh_sb", [128, FC, 64], BF16))
        bfl_sb = e(nc.sbuf_tensor("bfl_sb", [32, D], BF16))
        krT_sb = e(nc.sbuf_tensor("krT_sb", [128, T_SLAB], BF16))
        vaug_sb = e(nc.sbuf_tensor("vaug_sb", [128, NT_SLAB, 132], BF16))
        st_sum = e(nc.sbuf_tensor("st_sum", [128, 8], F32))
        st_ssq = e(nc.sbuf_tensor("st_ssq", [128, 8], F32))
        st_mu = e(nc.sbuf_tensor("st_mu", [128, 8], F32))
        st_var = e(nc.sbuf_tensor("st_var", [128, 8], F32))
        st_rstd = e(nc.sbuf_tensor("st_rstd", [128, 8], F32))
        st_nb = e(nc.sbuf_tensor("st_nb", [128, 8], F32))
        st_nmu = e(nc.sbuf_tensor("st_nmu", [128, 8], F32))
        st_musq = e(nc.sbuf_tensor("st_musq", [128, 8], F32))
        st_var2 = e(nc.sbuf_tensor("st_var2", [128, 8], F32))
        st_rvar = e(nc.sbuf_tensor("st_rvar", [128, 8], F32))
        jnk_v = e(nc.sbuf_tensor("jnk_v", [128, 4], F32))
        jnk_s = e(nc.sbuf_tensor("jnk_s", [128, 4], F32))
        bl_r = e(nc.sbuf_tensor("bl_r", [128, 8], F32))

        arena = e(nc.sbuf_tensor("arena", [128, ARENA_BYTES // 2], BF16))

        def vb(off, *dims):
            n = 1
            for d_ in dims:
                n *= d_
            ap = arena[:, off // 2: off // 2 + n]
            if len(dims) == 2:
                ap = ap.rearrange("p (a b) -> p a b", a=dims[0], b=dims[1])
            return ap

        def vf(off, *dims):
            n = 1
            for d_ in dims:
                n *= d_
            ap = arena[:, off // 2: off // 2 + 2 * n].bitcast(F32)
            if len(dims) == 2:
                ap = ap.rearrange("p (a b) -> p a b", a=dims[0], b=dims[1])
            return ap

        ln1T = vb(RA, FC, T_OWN)
        attnT = vb(RA, NH, T_OWN)
        nat8 = vb(RA, NT_OWN, D)
        ffnT = vb(RA, NH, T_OWN)
        wk_sb = vb(RB, FC, 128)
        wv_sb = vb(RB + 8192, FC, 128)
        xload2 = [vf(RB + 16384, D), vf(RB + 32768, D)]
        qrT = vb(RB, NH, T_OWN)
        projT = vb(RB, NH, T_OWN)
        ln2T = vb(RB, FC, T_OWN)
        natG = vb(RB, NT_OWN, D)
        lnb = [vb(RC, D), vb(RC + 8192, D)]
        wbuf = [vb(RC, FC, 128), vb(RC + 8192, FC, 128)]
        anat = [vb(RC, D), vb(RC + 8192, D)]
        halotmp = [vb(RE, FC, 128), vb(RE + 8192, FC, 128)]
        rstage = vb(RE, T_OWN)
        sigAB = vb(RE, 2 * T_OWN)
        sigA = sigAB[:, 0:1024]
        sigB = sigAB[:, 1024:2048]
        xnf = vf(RE, D)
        lgT = vf(RE, T_OWN)
        ohT_sb = vb(RE + 4096, T_OWN)
        ex_sb = vf(RE + 6144, T_OWN)
        zmT = vb(RE + 10240, T_OWN)
        mcol = vf(RE + 12288, NT_OWN)
        fflg = vf(RE + 12320, 4)
        ntmp = vf(RE + 12336, 4)
        ohn = vf(RE + 12352, NT_OWN, 4)
        lgN = vf(RE + 12480, NT_OWN, 4)
        fflg8 = vf(RE + 12608, NT_OWN, 4)
        ntA = vf(RE + 12736, NT_OWN)
        ntB = vf(RE + 12768, NT_OWN)
        ntC = vf(RE + 12800, NT_OWN)
        bl_t = vf(RE + 4096, 256)
        dumpA = vb(RF, D)
        vstage = [vb(RF + 8192, 128), vb(RF + 8448, 128)]
        kstage = [vb(RF + 8704, 128), vb(RF + 8960, 128)]
        ropesc = vf(RF + 9216, T_OWN)
        t1f = vf(RF, D)

        block = e(nc.Block())
        ENGS = ("tensor", "vector", "scalar", "sync", "gpsimd")

        def barrier():
            tgt = ctx.bump("s_bar", len(ENGS))
            for en in ENGS:
                def _f(eng, tgt=tgt):
                    eng.sem_inc(s_bar, 1)
                    eng.wait_ge(s_bar, tgt)
                getattr(block, en)(_f)

        def finish_tap(src_ap, shape, dtype):
            tap_dram[0] = nc.dram_tensor("tap", shape, dtype, kind="ExternalOutput").ap()
            barrier()

            def _f(sy):
                sy.dma_start(tap_dram[0], src_ap).then_inc(s_out, 16)
                sy.wait_ge(s_out, ctx.bump("s_out", 16))
            block.sync(_f)

        def ts1(eng, o, i0, s1, op):
            return eng.tensor_scalar(o, i0, s1, 0.0, op, ALU.add)

        def vsync(v):
            v.tensor_copy(jnk_v[:, 2:3], jnk_v[:, 2:3]).then_inc(s_v, 1)
            v.wait_ge(s_v, ctx.bump("s_v"))

        def ssync(s):
            s.activation(jnk_s[:, 0:1], jnk_s[:, 0:1], AF.Identity).then_inc(s_s, 1)
            s.wait_ge(s_s, ctx.bump("s_s"))

        def p0_sync(sy):
            for dst, src in ((cos_sb[:], cosT), (sin_sb[:], sinT), (aA_sb[:], alphaA),
                             (aB_sb[:], alphaB), (mb_sb[:], maskb), (idf_sb[:], identf),
                             (eb_sb[32:36, :], ebias)):
                sy.dma_start(dst, src).then_inc(s_x, 16)
                ctx.bump("s_x", 16)
        block.sync(p0_sync)

        def p0_g(g):
            for dst, src in ((idb_sb[:], identf), (pt_sb[:], ptmat), (ee_sb[:], eexp),
                             (wk_sb, wk_t), (wv_sb, wv_t), (sh_sb[:], sh_t),
                             (bfl_sb[:], bflat)):
                g.dma_start(dst, src).then_inc(s_w, 16)
                ctx.bump("s_w", 16)
            g.memset(vaug_sb[:, :, 128:129], 1.0)
            g.memset(jnk_v[:], 0.0)
            g.memset(jnk_s[:], 0.0).then_inc(s_g, 1)
            ctx.bump("s_g", 1)
        block.gpsimd(p0_g)
        P0_W = ctx.val("s_w")
        P0_G = ctx.val("s_g")

        def ln_tile(src_ap, col, dst_ap, dump_ap, pre_s=None, pre_v=None, pre_s3=None):
            c = slice(col, col + 1)

            def _sc(s):
                if pre_s:
                    for sm, v in pre_s:
                        s.wait_ge(sm, v)
                s.activation(dump_ap, src_ap, AF.Square,
                             accum_out=st_ssq[:, c]).then_inc(s_s, 1)
                s.wait_ge(s_s, ctx.bump("s_s"))
                s.activation(jnk_s[:, 0:1], st_ssq[:, c], AF.Identity).then_inc(s_s, 1)
            block.scalar(_sc)
            ev_ssq = ctx.bump("s_s")

            def _v(v, ev_ssq=ev_ssq):
                if pre_v:
                    for sm, vv in pre_v:
                        v.wait_ge(sm, vv)
                v.reduce_sum(st_sum[:, c], src_ap, axis=AX)
                vsync(v)
                ts1(v, st_mu[:, c], st_sum[:, c], 1.0 / D, ALU.mult)
                ts1(v, st_nmu[:, c], st_sum[:, c], -1.0 / D, ALU.mult)
                v.wait_ge(s_s, ev_ssq)
                v.tensor_scalar(st_var[:, c], st_ssq[:, c], 1.0 / D, EPS,
                                ALU.mult, ALU.add)
                v.tensor_tensor(st_musq[:, c], st_mu[:, c], st_mu[:, c], ALU.mult)
                vsync(v)
                v.scalar_tensor_tensor(st_var2[:, c], st_musq[:, c], -1.0,
                                       st_var[:, c], ALU.mult, ALU.add)
                vsync(v)
                v.reciprocal(st_rvar[:, c], st_var2[:, c])
                v.tensor_copy(jnk_v[:, 0:1], st_rvar[:, c]).then_inc(s_v, 1)
            block.vector(_v)
            ev_rv = ctx.bump("s_v")

            def _sc3(s, ev=ev_rv):
                s.wait_ge(s_v, ev)
                if pre_s3:
                    for sm, vv in pre_s3:
                        s.wait_ge(sm, vv)
                s.activation(st_rstd[:, c], st_rvar[:, c], AF.Sqrt)
                ssync(s)
                s.activation(st_nb[:, c], st_rstd[:, c], AF.Identity,
                             scale=st_nmu[:, c])
                ssync(s)
                s.activation(dst_ap, src_ap, AF.Identity,
                             bias=st_nb[:, c], scale=st_rstd[:, c])
                s.activation(jnk_s[:, 3:4], st_nb[:, c], AF.Identity).then_inc(s_s, 1)
            block.scalar(_sc3)
            return ctx.bump("s_s")

        def _pa_v(v):
            v.wait_ge(s_g, P0_G)
        block.vector(_pa_v)

        def _pa_s(s2):
            s2.wait_ge(s_g, P0_G)
        block.scalar(_pa_s)

        with ExitStack() as ps_st:
            pe = ps_st.enter_context
            ps_k = pe(nc.psum_tensor("ps_k", [128, 128], F32))
            ps_vv = pe(nc.psum_tensor("ps_vv", [128, 128], F32))
            ps_r = pe(nc.psum_tensor("ps_r", [128, 128], F32))
            ps_tp = pe(nc.psum_tensor("ps_tp", [128, 128], BF16))

            HALO = {0: 0, 1: 1, 10: 0, 11: 1}
            halo_kv_done = {}
            norm_prev = 0
            norm_prev2 = [0, 0]
            tr_prev = {}
            kevac = 0
            vevac = 0
            rope_prev = 0
            vcp_prev = 0
            ln_tr_ev = [0] * NT_SLAB
            for i in range(NT_SLAB):
                slot = i % 2

                def _ld(sy, i=i, np_=norm_prev2[i % 2]):
                    if np_:
                        sy.wait_ge(s_s, np_)
                    sy.dma_start(xload2[i % 2],
                                 xslab[128 * i:128 * (i + 1), :]).then_inc(s_x, 16)
                block.sync(_ld)
                ev_x = ctx.bump("s_x", 16)
                xload = xload2[i % 2]

                pre3 = [(s_tr, tr_prev[i - 2])] if i >= 2 else None
                ev_norm = ln_tile(xload, i % 4, lnb[slot], dumpA,
                                  pre_s=[(s_x, ev_x)], pre_v=[(s_x, ev_x)], pre_s3=pre3)
                norm_prev = ev_norm
                norm_prev2[i % 2] = ev_norm

                if i in HALO:
                    hs = HALO[i]
                    dest = halotmp[hs]
                else:
                    dest = ln1T[:, :, 128 * (i - 2):128 * (i - 1)]
                    hs = None

                def _tr(s, i=i, slot=slot, ev=ev_norm, hs=hs, dest=dest):
                    s.wait_ge(s_s, ev)
                    if hs is not None and hs in halo_kv_done:
                        s.wait_ge(s_mm, halo_kv_done[hs])
                    s.dma_start_transpose(dest, lnb[slot]).then_inc(s_tr, 16)
                block.scalar(_tr)
                ln_tr_ev[i] = ctx.bump("s_tr", 16)
                tr_prev[i] = ln_tr_ev[i]

                def _t(t, i=i, dest=dest, ev_tr=ln_tr_ev[i], ke=kevac, ve=vevac):
                    if i == 0:
                        t.wait_ge(s_w, P0_W)
                        t.wait_ge(s_g, P0_G)
                    t.wait_ge(s_tr, ev_tr)
                    if ke:
                        t.wait_ge(s_s, ke)
                    for fc in range(FC):
                        m = t.matmul(ps_k[:], wk_sb[:, fc, :], dest[:, fc, :],
                                     start=(fc == 0), stop=(fc == FC - 1))
                    m.then_inc(s_mm, 1)
                    if ve:
                        t.wait_ge(s_s, ve)
                    for fc in range(FC):
                        m = t.matmul(ps_vv[:], wv_sb[:, fc, :], dest[:, fc, :],
                                     start=(fc == 0), stop=(fc == FC - 1))
                    m.then_inc(s_mm, 1)
                block.tensor(_t)
                ev_k = ctx.bump("s_mm")
                ev_vmm = ctx.bump("s_mm")
                if i in HALO:
                    halo_kv_done[HALO[i]] = ev_vmm

                def _s(s, i=i, slot=slot, evk=ev_k, evv=ev_vmm, rp=rope_prev, vc=vcp_prev):
                    s.wait_ge(s_mm, evk)
                    if rp:
                        s.wait_ge(s_v, rp)
                    s.activation(kstage[slot], ps_k[:], AF.Identity)
                    s.activation(jnk_s[:, 1:2], kstage[slot][:, 0:1],
                                 AF.Identity).then_inc(s_s, 1)
                    s.wait_ge(s_mm, evv)
                    if vc:
                        s.wait_ge(s_v, vc)
                    s.activation(vstage[slot], ps_vv[:], AF.Identity)
                    s.activation(jnk_s[:, 2:3], vstage[slot][:, 0:1],
                                 AF.Identity).then_inc(s_s, 1)
                block.scalar(_s)
                kevac = ctx.bump("s_s")
                vevac = ctx.bump("s_s")

                def _t2(t, slot=slot, ke=kevac, ve=vevac, rp=rope_prev, vc=vcp_prev):
                    t.wait_ge(s_s, ke)
                    if rp:
                        t.wait_ge(s_v, rp)
                    t.matmul(ps_r[:], pt_sb[:], kstage[slot],
                             start=True, stop=True).then_inc(s_mm, 1)
                    t.wait_ge(s_s, ve)
                    if vc:
                        t.wait_ge(s_v, vc)
                    t.transpose(ps_tp[:], vstage[slot], idb_sb[:]).then_inc(s_mm, 1)
                block.tensor(_t2)
                ev_rot = ctx.bump("s_mm")
                ev_vtp = ctx.bump("s_mm")

                def _v(v, i=i, slot=slot, er=ev_rot, et=ev_vtp):
                    cols = slice(128 * i, 128 * (i + 1))
                    v.wait_ge(s_mm, er)
                    v.tensor_tensor(ropesc[:, 0:128], kstage[slot], cos_sb[:, cols], ALU.mult)
                    v.tensor_tensor(ropesc[:, 128:256], ps_r[:], sin_sb[:, cols], ALU.mult)
                    vsync(v)
                    v.tensor_tensor(krT_sb[:, cols], ropesc[:, 0:128], ropesc[:, 128:256],
                                    ALU.add)
                    v.tensor_copy(jnk_v[:, 3:4], krT_sb[:, cols][:, 0:1]).then_inc(s_v, 1)
                    v.wait_ge(s_mm, et)
                    v.tensor_copy(vaug_sb[:, i, 0:128], ps_tp[:])
                    v.tensor_copy(jnk_v[:, 3:4], vaug_sb[:, i, 0:1]).then_inc(s_v, 1)
                block.vector(_v)
                rope_prev = ctx.bump("s_v")
                vcp_prev = ctx.bump("s_v")

            if tap == "kst":
                kdbg = vb(RF + 13312, 1024)

                def _vdbg(v):
                    kf = kdbg.bitcast(F32)
                    v.tensor_copy(kf[:, 0:1], st_sum[:, 3:4])
                    v.tensor_copy(kf[:, 1:2], st_ssq[:, 3:4])
                    v.tensor_copy(kf[:, 2:3], st_mu[:, 3:4])
                    v.tensor_copy(kf[:, 3:4], st_nmu[:, 3:4])
                    v.tensor_copy(kf[:, 4:5], st_var[:, 3:4])
                    v.tensor_copy(kf[:, 5:6], st_musq[:, 3:4])
                    v.tensor_copy(kf[:, 6:7], st_var2[:, 3:4])
                    v.tensor_copy(kf[:, 7:8], st_rvar[:, 3:4])
                    v.tensor_copy(kf[:, 8:9], st_rstd[:, 3:4])
                    v.tensor_copy(kf[:, 9:10], st_nb[:, 3:4])
                    v.tensor_copy(kdbg[:, 512:1024], lnb[1][:, 0:512]).then_inc(s_v, 1)
                block.vector(_vdbg)
                ctx.bump("s_v")
                finish_tap(kdbg, [128, 1024], BF16)
                return nc, tap_dram[0]
            if tap == "ln1T":
                finish_tap(ln1T, [128, FC, T_OWN], BF16)
                return nc, tap_dram[0]
            if tap == "kv":
                finish_tap(krT_sb[:], [128, T_SLAB], BF16)
                return nc, tap_dram[0]
            if tap == "vaug":
                finish_tap(vaug_sb[:], [128, NT_SLAB, 132], BF16)
                return nc, tap_dram[0]
            barrier()

        with ExitStack() as ps_st:
            pe = ps_st.enter_context
            ps_q = [pe(nc.psum_tensor(f"ps_q{i}", [128, 512], F32)) for i in range(2)]
            ps_r2 = [pe(nc.psum_tensor(f"ps_r2{i}", [128, 512], F32)) for i in range(2)]

            wbuf_free = [0, 0]
            psq_free = [0, 0]
            psr_free = [0, 0]
            qst_free = [0, 0]
            for ot in range(NH):
                slot = ot % 2

                def _g(g, ot=ot, slot=slot, w=wbuf_free[slot]):
                    if w:
                        g.wait_ge(s_mm, w)
                    g.dma_start(wbuf[slot], wq_t[ot]).then_inc(s_w, 16)
                block.gpsimd(_g)
                ev_w = ctx.bump("s_w", 16)

                last_mm = None
                for th in range(2):
                    def _t(t, th=th, slot=slot, ev_w=ev_w, pf=psq_free[th]):
                        t.wait_ge(s_w, ev_w)
                        if pf:
                            t.wait_ge(s_s, pf)
                        for fc in range(FC):
                            m = t.matmul(ps_q[th][:], wbuf[slot][:, fc, :],
                                         ln1T[:, fc, 512 * th:512 * (th + 1)],
                                         start=(fc == 0), stop=(fc == FC - 1))
                        m.then_inc(s_mm, 1)
                    block.tensor(_t)
                    ev_mm = ctx.bump("s_mm")
                    last_mm = ev_mm

                    def _s(s, th=th, ev=ev_mm, qf=qst_free[th]):
                        s.wait_ge(s_mm, ev)
                        if qf:
                            s.wait_ge(s_v, qf)
                        s.activation(rstage[:, 512 * th:512 * (th + 1)], ps_q[th][:],
                                     AF.Identity)
                        s.activation(jnk_s[:, 1:2], rstage[:, 512 * th:512 * th + 1],
                                     AF.Identity).then_inc(s_s, 1)
                    block.scalar(_s)
                    ev_ev = ctx.bump("s_s")
                    psq_free[th] = ev_ev

                    def _t2(t, th=th, ev=ev_ev, rf=psr_free[th]):
                        t.wait_ge(s_s, ev)
                        if rf:
                            t.wait_ge(s_v, rf)
                        t.matmul(ps_r2[th][:], pt_sb[:], rstage[:, 512 * th:512 * (th + 1)],
                                 start=True, stop=True).then_inc(s_mm, 1)
                    block.tensor(_t2)
                    ev_rot = ctx.bump("s_mm")

                    def _v(v, ot=ot, th=th, ev=ev_rot):
                        v.wait_ge(s_mm, ev)
                        ccols = slice(256 + 512 * th, 256 + 512 * (th + 1))
                        v.tensor_tensor(ropesc[:, 0:512], rstage[:, 512 * th:512 * (th + 1)],
                                        cos_sb[:, ccols], ALU.mult)
                        v.tensor_tensor(ropesc[:, 512:1024], ps_r2[th][:],
                                        sin_sb[:, ccols], ALU.mult)
                        v.tensor_tensor(qrT[:, ot, 512 * th:512 * (th + 1)],
                                        ropesc[:, 0:512], ropesc[:, 512:1024],
                                        ALU.add)
                        v.tensor_copy(jnk_v[:, 3:4],
                                      qrT[:, ot, 512 * th:512 * th + 1]).then_inc(s_v, 1)
                    block.vector(_v)
                    ev_rope = ctx.bump("s_v")
                    psr_free[th] = ev_rope
                    qst_free[th] = ev_rope
                wbuf_free[slot] = last_mm

            if tap == "qrT":
                finish_tap(qrT, [128, NH, T_OWN], BF16)
                return nc, tap_dram[0]
            barrier()

        with ExitStack() as ps_st:
            pe = ps_st.enter_context
            ps_sc = pe(nc.psum_tensor("ps_sc", [128, 2048], F32))
            ps_scA = ps_sc[:, 0:1024]
            ps_scB = ps_sc[:, 1024:2048]
            ps_wA = pe(nc.psum_tensor("ps_wA", [128, 512], F32))
            ps_wB = pe(nc.psum_tensor("ps_wB", [128, 512], F32))

            sig_done = 0
            blend_done = 0
            anat_guard = 0
            anat_tr = {}
            for s_i in range(SEGS):
                for h in range(NH):
                    qcols = slice(256 * s_i, 256 * (s_i + 1))

                    def _t(t, s_i=s_i, h=h, sd=sig_done, qcols=qcols):
                        if sd:
                            t.wait_ge(s_s, sd)
                        for j in range(4):
                            t.matmul(ps_scA[:, 256 * j:256 * (j + 1)],
                                     krT_sb[:, 128 * (2 * s_i + j):128 * (2 * s_i + j + 1)],
                                     qrT[:, h, qcols], start=True, stop=True)
                        for j in range(4):
                            m = t.matmul(ps_scB[:, 256 * j:256 * (j + 1)],
                                         krT_sb[:, 128 * (2 * s_i + 2 + j):128 * (2 * s_i + 3 + j)],
                                         qrT[:, h, qcols], start=True, stop=True)
                        m.then_inc(s_mm, 1)
                    block.tensor(_t)
                    ev_sc = ctx.bump("s_mm")

                    def _s(s, s_i=s_i, ev=ev_sc):
                        s.wait_ge(s_mm, ev)
                        if s_i == SEGS - 1:
                            s.activation(sigAB[:, 0:1536], ps_sc[:, 0:1536],
                                         AF.Sigmoid, scale=SCALE)
                            s.activation(sigB[:, 512:768], ps_scB[:, 512:768], AF.Sigmoid,
                                         scale=SCALE, bias=mb_sb[:, 0:1])
                            s.activation(sigB[:, 768:1024], ps_scB[:, 768:1024], AF.Sigmoid,
                                         scale=SCALE, bias=mb_sb[:, 1:2])
                        else:
                            s.activation(sigAB, ps_sc[:], AF.Sigmoid, scale=SCALE)
                        s.activation(jnk_s[:, 1:2], sigB[:, 1023:1024],
                                     AF.Identity).then_inc(s_s, 1)
                    block.scalar(_s)
                    sig_done = ctx.bump("s_s")

                    def _t2(t, s_i=s_i, sd=sig_done, bd=blend_done):
                        t.wait_ge(s_s, sd)
                        if bd:
                            t.wait_ge(s_v, bd)
                        for qc in range(2):
                            off = 132 * qc
                            for j in range(4):
                                t.matmul(ps_wA[:, off:off + 129],
                                         sigA[:, 256 * j + 128 * qc:256 * j + 128 * (qc + 1)],
                                         vaug_sb[:, 2 * s_i + j, 0:129],
                                         start=(j == 0), stop=(j == 3))
                            for j in range(4):
                                m = t.matmul(ps_wB[:, off:off + 129],
                                             sigB[:, 256 * j + 128 * qc:256 * j + 128 * (qc + 1)],
                                             vaug_sb[:, 2 * s_i + 2 + j, 0:129],
                                             start=(j == 0), stop=(j == 3))
                        m.then_inc(s_mm, 1)
                    block.tensor(_t2)
                    ev_w2 = ctx.bump("s_mm")

                    def _v(v, s_i=s_i, h=h, ev=ev_w2, ag=anat_guard):
                        v.wait_ge(s_mm, ev)
                        if ag and h == 0:
                            v.wait_ge(s_tr, ag)
                        for qc in range(2):
                            v.reciprocal(bl_r[:, 4 * qc:4 * qc + 1],
                                         ps_wA[:, 132 * qc + 128:132 * qc + 129])
                            v.reciprocal(bl_r[:, 4 * qc + 2:4 * qc + 3],
                                         ps_wB[:, 132 * qc + 128:132 * qc + 129])
                        vsync(v)
                        for qc in range(2):
                            acol = slice(2 * s_i + qc, 2 * s_i + qc + 1)
                            v.tensor_tensor(bl_r[:, 4 * qc + 1:4 * qc + 2],
                                            bl_r[:, 4 * qc:4 * qc + 1],
                                            aA_sb[:, acol], ALU.mult)
                            v.tensor_tensor(bl_r[:, 4 * qc + 3:4 * qc + 4],
                                            bl_r[:, 4 * qc + 2:4 * qc + 3],
                                            aB_sb[:, acol], ALU.mult)
                        vsync(v)
                        for qc in range(2):
                            ts1(v, bl_t[:, 128 * qc:128 * (qc + 1)],
                                ps_wA[:, 132 * qc:132 * qc + 128],
                                bl_r[:, 4 * qc + 1:4 * qc + 2], ALU.mult)
                        vsync(v)
                        for qc in range(2):
                            v.scalar_tensor_tensor(
                                anat[qc][:, 128 * h:128 * (h + 1)],
                                ps_wB[:, 132 * qc:132 * qc + 128],
                                bl_r[:, 4 * qc + 3:4 * qc + 4],
                                bl_t[:, 128 * qc:128 * (qc + 1)],
                                ALU.mult, ALU.add)
                        v.tensor_copy(jnk_v[:, 3:4],
                                      anat[1][:, 128 * h:128 * h + 1]).then_inc(s_v, 1)
                    block.vector(_v)
                    blend_done = ctx.bump("s_v")

                for qc in range(2):
                    ti = 2 * s_i + qc

                    def _tr(s, ti=ti, qc=qc, bd=blend_done):
                        s.wait_ge(s_v, bd)
                        s.dma_start_transpose(
                            attnT[:, :, 128 * ti:128 * (ti + 1)],
                            anat[qc]).then_inc(s_tr, 16)
                    block.scalar(_tr)
                    anat_tr[ti] = ctx.bump("s_tr", 16)
                anat_guard = anat_tr[2 * s_i + 1]

            if tap == "wlast":
                wdbg = vf(RE + 8192, T_OWN)

                def _vdbg(v, bd=blend_done):
                    v.wait_ge(s_v, bd)
                    v.tensor_copy(wdbg[:, 0:264], ps_wA.bitcast(F32)[:, 0:264])
                    v.tensor_copy(wdbg[:, 264:528], ps_wB.bitcast(F32)[:, 0:264]).then_inc(s_v, 1)
                block.vector(_vdbg)
                ctx.bump("s_v")
                finish_tap(wdbg, [128, T_OWN], F32)
                return nc, tap_dram[0]
            if tap == "sclast":
                wdbg2 = vf(RE + 8192, T_OWN)

                def _vdbg2(v, ev=ev_sc):
                    v.wait_ge(s_mm, ev)
                    v.tensor_copy(wdbg2[:], ps_scB[:]).then_inc(s_v, 1)
                block.vector(_vdbg2)
                ctx.bump("s_v")
                finish_tap(wdbg2, [128, T_OWN], F32)
                return nc, tap_dram[0]
            if tap == "siglast":
                finish_tap(sigB, [128, T_OWN], BF16)
                return nc, tap_dram[0]
            if tap == "attnT":
                finish_tap(attnT, [128, NH, T_OWN], BF16)
                return nc, tap_dram[0]
            attn_tr_last = anat_tr[2 * SEGS - 1]
            barrier()

        def big_proj(wdram, rhs_fn, dst, delta_fn=None, first_wait=None):
            with ExitStack() as ps_st2:
                pe2 = ps_st2.enter_context
                ps_p = [pe2(nc.psum_tensor(f"ps_p{ctx.bump('psn')}", [128, 512], F32))
                        for _ in range(2)]
                wb_free = [0, 0]
                ps_free = [0, 0]
                for ot in range(FC):
                    slot = ot % 2

                    def _g(g, ot=ot, slot=slot, w=wb_free[slot]):
                        if w:
                            g.wait_ge(s_mm, w)
                        g.dma_start(wbuf[slot], wdram[ot]).then_inc(s_w, 16)
                    block.gpsimd(_g)
                    ev_w = ctx.bump("s_w", 16)

                    last = None
                    for th in range(2):
                        def _t(t, ot=ot, th=th, slot=slot, ev_w=ev_w, pf=ps_free[th]):
                            if ot == 0 and th == 0 and first_wait:
                                for sm, vv in first_wait:
                                    t.wait_ge(sm, vv)
                            t.wait_ge(s_w, ev_w)
                            if pf:
                                t.wait_ge(s_s, pf)
                            first = True
                            if delta_fn is not None:
                                delta_fn(t, ps_p[th], ot, th)
                                first = False
                            for fc in range(FC):
                                m = t.matmul(ps_p[th][:], wbuf[slot][:, fc, :],
                                             rhs_fn(fc, th),
                                             start=(first and fc == 0),
                                             stop=(fc == FC - 1))
                            m.then_inc(s_mm, 1)
                        block.tensor(_t)
                        ev_mm = ctx.bump("s_mm")
                        last = ev_mm

                        def _s(s, ot=ot, th=th, ev=ev_mm):
                            s.wait_ge(s_mm, ev)
                            s.activation(dst[:, ot, 512 * th:512 * (th + 1)],
                                         ps_p[th][:], AF.Identity)
                            s.activation(jnk_s[:, 1:2],
                                         dst[:, ot, 512 * th:512 * th + 1],
                                         AF.Identity).then_inc(s_s, 1)
                        block.scalar(_s)
                        ps_free[th] = ctx.bump("s_s")
                    wb_free[slot] = last

        big_proj(wo_t,
                 lambda fc, th: attnT[:, fc, 512 * th:512 * (th + 1)],
                 projT, first_wait=[(s_tr, attn_tr_last)])
        if tap == "projT":
            finish_tap(projT, [128, NH, T_OWN], BF16)
            return nc, tap_dram[0]
        barrier()

        def resid_phase(srcT, natv, is_final):
            ptr = 0
            for ot in range(FC):
                def _tr(s, ot=ot):
                    s.dma_start_transpose(natv[:, :, 128 * ot:128 * (ot + 1)],
                                          srcT[:, ot, :]).then_inc(s_tr, 16)
                block.scalar(_tr)
                ptr = ctx.bump("s_tr", 16)

            xn_free = None
            prev_add = 0
            ln2_tr_local = {}
            dumpR = vb(RC + 8192, D) if not is_final else vb(RC, D)
            for i in range(NT_OWN):
                slot = 0

                def _ld(sy, i=i, ptr=ptr, xe=xn_free):
                    if i == 0:
                        sy.wait_ge(s_tr, ptr)
                        if is_final:
                            sy.wait_ge(s_res, ctx.val("s_res"))
                    if xe is not None:
                        if is_final:
                            sy.wait_ge(s_out, xe)
                        else:
                            sy.wait_ge(s_res, xe[0])
                            sy.wait_ge(s_s, xe[1])
                    src = (xres[128 * i:128 * (i + 1), :] if is_final
                           else xslab[256 + 128 * i:256 + 128 * (i + 1), :])
                    sy.dma_start(xnf, src).then_inc(s_x, 16)
                block.sync(_ld)
                ev_x = ctx.bump("s_x", 16)

                pre = [(s_tr, ptr)] if i == 0 else None
                pre3 = [(s_v, prev_add)] if prev_add else None
                ev_t1 = ln_tile(natv[:, i, :], i % 4, t1f, dumpR,
                                pre_s=pre, pre_v=pre, pre_s3=pre3)

                def _v3(v, ev_t1=ev_t1, ev_x=ev_x):
                    v.wait_ge(s_s, ev_t1)
                    v.wait_ge(s_x, ev_x)
                    v.tensor_tensor(xnf, xnf, t1f, ALU.add)
                    v.tensor_copy(jnk_v[:, 3:4], xnf[:, 0:1]).then_inc(s_v, 1)
                block.vector(_v3)
                ev_xn = ctx.bump("s_v")
                prev_add = ev_xn

                if is_final:
                    def _st(sy, i=i, ev=ev_xn):
                        sy.wait_ge(s_v, ev)
                        sy.dma_start(out[128 * i:128 * (i + 1), :],
                                     xnf).then_inc(s_out, 16)
                    block.sync(_st)
                    xn_free = ctx.bump("s_out", 16)
                else:
                    def _st(sy, i=i, ev=ev_xn):
                        sy.wait_ge(s_v, ev)
                        sy.dma_start(xres[128 * i:128 * (i + 1), :],
                                     xnf).then_inc(s_res, 16)
                    block.sync(_st)
                    ev_res = ctx.bump("s_res", 16)

                    pre3b = [(s_tr, ln2_tr_local[i - 1])] if i >= 1 else None
                    ev_n2 = ln_tile(xnf, 4 + i % 4, lnb[slot], dumpR,
                                    pre_s=[(s_v, ev_xn)], pre_v=[(s_v, ev_xn)],
                                    pre_s3=pre3b)

                    def _tr2(s, i=i, slot=slot, ev=ev_n2):
                        s.wait_ge(s_s, ev)
                        s.dma_start_transpose(ln2T[:, :, 128 * i:128 * (i + 1)],
                                              lnb[slot]).then_inc(s_tr, 16)
                    block.scalar(_tr2)
                    ln2_tr_local[i] = ctx.bump("s_tr", 16)
                    xn_free = (ev_res, ev_n2)
            return ln2_tr_local.get(NT_OWN - 1, 0)

        ln2_tr_last = resid_phase(projT, nat8, is_final=False)
        if tap == "ln2T":
            finish_tap(ln2T, [128, FC, T_OWN], BF16)
            return nc, tap_dram[0]
        barrier()

        with ExitStack() as ps_st:
            pe = ps_st.enter_context
            ps_sh = pe(nc.psum_tensor("ps_sh", [64, T_OWN], F32))
            ps_tpf = pe(nc.psum_tensor("ps_tpf", [128, 128], F32))
            ps_oh = pe(nc.psum_tensor("ps_oh", [4, T_OWN], F32))
            ps_ex = pe(nc.psum_tensor("ps_ex", [32, T_OWN], F32))

            def _gms(g):
                g.memset(lgT, 0.0).then_inc(s_g, 1)
            block.gpsimd(_gms)
            ev_ms = ctx.bump("s_g", 1)

            def _tf1(t):
                t.wait_ge(s_tr, ln2_tr_last)
                for th in range(2):
                    for fc in range(FC):
                        m = t.matmul(ps_sh[:, 512 * th:512 * (th + 1)], sh_sb[:, fc, :],
                                     ln2T[:, fc, 512 * th:512 * (th + 1)],
                                     start=(fc == 0), stop=(fc == FC - 1))
                m.then_inc(s_mm, 1)
            block.tensor(_tf1)
            ev_sh = ctx.bump("s_mm")

            def _vf2(v, ev=ev_sh, evm=ev_ms):
                v.wait_ge(s_mm, ev)
                v.wait_ge(s_g, evm)
                ts1(v, lgT[32:36, :], ps_sh[32:36, :], eb_sb[32:36, :], ALU.add)
                v.tensor_copy(jnk_v[0:4, 3:4], lgT[32:36, 0:1]).then_inc(s_v, 1)
            block.vector(_vf2)
            ev_lg = ctx.bump("s_v")

            cp_ev = 0
            for j in range(NT_OWN):
                def _t(t, j=j, ev=ev_lg, ce=cp_ev):
                    if j == 0:
                        t.wait_ge(s_v, ev)
                    if ce:
                        t.wait_ge(s_v, ce)
                    t.transpose(ps_tpf[:], lgT[:, 128 * j:128 * (j + 1)],
                                idf_sb[:]).then_inc(s_mm, 1)
                block.tensor(_t)
                ev_tp = ctx.bump("s_mm")

                def _v(v, j=j, ev=ev_tp):
                    v.wait_ge(s_mm, ev)
                    v.tensor_copy(lgN[:, j, :], ps_tpf[:, 32:36])
                    v.tensor_copy(jnk_v[:, 3:4], lgN[:, j, 0:1]).then_inc(s_v, 1)
                block.vector(_v)
                cp_ev = ctx.bump("s_v")

            def _vcas(v, ce=cp_ev):
                v.wait_ge(s_v, ce)
                for j in range(NT_OWN):
                    v.reduce_max(mcol[:, j:j + 1], lgN[:, j, :], axis=AX)
                vsync(v)
                for j in range(NT_OWN):
                    ts1(v, fflg8[:, j, :], lgN[:, j, :], mcol[:, j:j + 1], ALU.is_equal)
                vsync(v)
                for j in range(NT_OWN):
                    v.tensor_copy(ohn[:, j, 0:1], fflg8[:, j, 0:1])
                for j in range(NT_OWN):
                    v.tensor_scalar(ntA[:, j:j + 1], fflg8[:, j, 0:1], -1.0, 1.0,
                                    ALU.mult, ALU.add)
                vsync(v)
                for j in range(NT_OWN):
                    v.tensor_tensor(ohn[:, j, 1:2], fflg8[:, j, 1:2], ntA[:, j:j + 1],
                                    ALU.mult)
                for j in range(NT_OWN):
                    v.tensor_scalar(ntB[:, j:j + 1], fflg8[:, j, 1:2], -1.0, 1.0,
                                    ALU.mult, ALU.add)
                vsync(v)
                for j in range(NT_OWN):
                    v.tensor_tensor(ntC[:, j:j + 1], ntA[:, j:j + 1], ntB[:, j:j + 1],
                                    ALU.mult)
                vsync(v)
                for j in range(NT_OWN):
                    v.tensor_tensor(ohn[:, j, 2:3], fflg8[:, j, 2:3], ntC[:, j:j + 1],
                                    ALU.mult)
                for j in range(NT_OWN):
                    v.tensor_scalar(ntB[:, j:j + 1], fflg8[:, j, 2:3], -1.0, 1.0,
                                    ALU.mult, ALU.add)
                vsync(v)
                for j in range(NT_OWN):
                    v.tensor_tensor(ntA[:, j:j + 1], ntC[:, j:j + 1], ntB[:, j:j + 1],
                                    ALU.mult)
                vsync(v)
                for j in range(NT_OWN):
                    v.tensor_tensor(ohn[:, j, 3:4], fflg8[:, j, 3:4], ntA[:, j:j + 1],
                                    ALU.mult)
                v.tensor_copy(jnk_v[:, 1:2], ohn[:, NT_OWN - 1, 3:4]).then_inc(s_v, 1)
            block.vector(_vcas)
            oh_ev = ctx.bump("s_v")

            tpb_last = 0
            for j in range(NT_OWN):
                def _t(t, j=j, ev=oh_ev):
                    if j == 0:
                        t.wait_ge(s_v, ev)
                    t.transpose(ps_oh[:, 128 * j:128 * (j + 1)], ohn[:, j, :],
                                idf_sb[:]).then_inc(s_mm, 1)
                block.tensor(_t)
                tpb_last = ctx.bump("s_mm")

            def _vf3(v, ev=tpb_last):
                v.wait_ge(s_mm, ev)
                v.tensor_copy(ohT_sb[0:4, :], ps_oh[:])
                v.tensor_copy(jnk_v[0:4, 3:4], ohT_sb[0:4, 0:1]).then_inc(s_v, 1)
            block.vector(_vf3)
            ev_ohT = ctx.bump("s_v")

            def _tf4(t, ev=ev_ohT):
                t.wait_ge(s_v, ev)
                for th in range(2):
                    m = t.matmul(ps_ex[:, 512 * th:512 * (th + 1)], ee_sb[:],
                                 ohT_sb[0:4, 512 * th:512 * (th + 1)], start=True, stop=True)
                m.then_inc(s_mm, 1)
            block.tensor(_tf4)
            ev_ex = ctx.bump("s_mm")

            def _vf5(v, ev=ev_ex):
                v.wait_ge(s_mm, ev)
                v.tensor_copy(ex_sb[0:32, :], ps_ex[:])
                vsync(v)
                v.tensor_tensor(zmT[0:32, :], ps_sh[0:32, :], ex_sb[0:32, :], ALU.mult)
                v.tensor_copy(jnk_v[0:32, 3:4], zmT[0:32, 0:1]).then_inc(s_v, 1)
            block.vector(_vf5)
            ev_zm = ctx.bump("s_v")

            def _tf6(t, ev=ev_zm):
                t.wait_ge(s_v, ev)
            block.tensor(_tf6)

            if tap == "zmT":
                finish_tap(zmT[0:32, :], [32, T_OWN], BF16)
                return nc, tap_dram[0]

        def delta_fn(t, ps, ot, th):
            t.matmul(ps[:], bfl_sb[:, 128 * ot:128 * (ot + 1)],
                     zmT[0:32, 512 * th:512 * (th + 1)], start=True, stop=False)

        big_proj(wup_t,
                 lambda fc, th: ln2T[:, fc, 512 * th:512 * (th + 1)],
                 ffnT, delta_fn=delta_fn, first_wait=[(s_tr, ln2_tr_last)])

        if tap == "ffnT":
            finish_tap(ffnT, [128, NH, T_OWN], BF16)
            return nc, tap_dram[0]
        barrier()

        resid_phase(ffnT, natG, is_final=True)

        def _fin(sy):
            sy.wait_ge(s_out, ctx.val("s_out"))
        block.sync(_fin)

    return nc, tap_dram[0]



def _rope_cache(S_, D_, base=10000.0):
    inv_freq = 1.0 / base ** (np.arange(0, D_, 2, dtype=np.float32) / D_)
    t = np.arange(S_, dtype=np.float32)
    f = np.outer(t, inv_freq)
    cos = np.zeros((S_, D_), dtype=np.float32)
    sin = np.zeros((S_, D_), dtype=np.float32)
    cos[:, 0::2] = np.cos(f)
    cos[:, 1::2] = np.cos(f)
    sin[:, 0::2] = np.sin(f)
    sin[:, 1::2] = np.sin(f)
    return cos, sin


def _tile_w(wT):
    f, o = wT.shape
    if o <= 128:
        return np.ascontiguousarray(wT.reshape(FC, 128, o).transpose(1, 0, 2))
    return np.ascontiguousarray(
        wT.reshape(FC, 128, o // 128, 128).transpose(2, 1, 0, 3))


def make_inputs(x, Wq, Wk, Wv, Wo, W_up, gate_w, expert_bias, lora_A, lora_B):
    cos, sin = _rope_cache(S, DH)
    wq_t = _tile_w(np.ascontiguousarray(Wq.T))
    wo_t = _tile_w(np.ascontiguousarray(Wo.T))
    wup_t = _tile_w(np.ascontiguousarray(W_up.T))
    wk_t = _tile_w(np.ascontiguousarray(Wk.T))
    wv_t = _tile_w(np.ascontiguousarray(Wv.T))
    shW = np.zeros((D, 64), np.float32)
    shW[:, 0:32] = lora_A.reshape(32, D).T
    shW[:, 32:36] = gate_w.T
    sh_t = _tile_w(shW)
    bflat = np.ascontiguousarray(lora_B.transpose(0, 2, 1).reshape(32, D))
    eexp = np.zeros((4, 32), np.float32)
    for ei in range(4):
        eexp[ei, ei * 8:(ei + 1) * 8] = 1.0
    identf = np.eye(128, dtype=np.float32)
    pmat = np.zeros((128, 128), np.float32)
    for d_ in range(64):
        pmat[d_, d_ + 64] = -1.0
        pmat[d_ + 64, d_] = 1.0
    ptmat = np.ascontiguousarray(pmat.T)
    lin = np.linspace(0.0, 1.0, 256, dtype=np.float32)

    maps = []
    for c in range(8):
        b, j = divmod(c, 4)
        g0 = 1024 * j - 256
        xs = np.zeros((T_SLAB, D), np.float32)
        lo, hi = max(0, g0), min(S, g0 + T_SLAB)
        xs[lo - g0:hi - g0, :] = x[b, lo:hi, :]
        ct = np.zeros((128, T_SLAB), np.float32)
        sn = np.zeros((128, T_SLAB), np.float32)
        ct[:, lo - g0:hi - g0] = cos[lo:hi].T
        sn[:, lo - g0:hi - g0] = sin[lo:hi].T
        aA = np.zeros((128, 2 * SEGS), np.float32)
        aB = np.zeros((128, 2 * SEGS), np.float32)
        for s_i in range(SEGS):
            g = 4 * j + s_i
            ab = np.ones(256, np.float32) if g == 0 else lin
            for qc in range(2):
                aB[:, 2 * s_i + qc] = ab[128 * qc:128 * (qc + 1)]
                aA[:, 2 * s_i + qc] = np.float32(1.0) - ab[128 * qc:128 * (qc + 1)]
        mb = np.zeros((128, 2), np.float32)
        if j == 3:
            mb[:] = -200.0
        maps.append(dict(
            xslab=xs, wq_t=wq_t, wo_t=wo_t, wup_t=wup_t, wk_t=wk_t, wv_t=wv_t,
            sh_t=sh_t, bflat=bflat,
            ebias=np.asarray(expert_bias, np.float32).reshape(4, 1),
            eexp=eexp, identf=identf, ptmat=ptmat, cosT=ct, sinT=sn,
            alphaA=aA, alphaB=aB, maskb=mb))
    return maps


def kernel(L0_attn_x, L0_attn_Wq, L0_attn_Wk, L0_attn_Wv, L0_attn_Wo,
           L0_ffn_W_up, L0_ffn_gate_w, L0_ffn_expert_bias, L0_ffn_lora_A,
           L0_ffn_lora_B):
    x = np.asarray(L0_attn_x, np.float32)
    maps = make_inputs(x, np.asarray(L0_attn_Wq, np.float32),
                       np.asarray(L0_attn_Wk, np.float32),
                       np.asarray(L0_attn_Wv, np.float32),
                       np.asarray(L0_attn_Wo, np.float32),
                       np.asarray(L0_ffn_W_up, np.float32),
                       np.asarray(L0_ffn_gate_w, np.float32),
                       np.asarray(L0_ffn_expert_bias, np.float32),
                       np.asarray(L0_ffn_lora_A, np.float32),
                       np.asarray(L0_ffn_lora_B, np.float32))
    nc, _ = build_program()
    res = run_bass_kernel_spmd(nc, maps, core_ids=list(range(8)))
    full = np.empty((B, S, D), np.float32)
    for c in range(8):
        b, j = divmod(c, 4)
        full[b, 1024 * j:1024 * (j + 1), :] = res.results[c]["out"]
    return full


# version 08
# speedup vs baseline: 1.3458x; 1.3458x over previous
import math
import numpy as np
from contextlib import ExitStack

import concourse.bass as bass
import concourse.mybir as mybir
from concourse.bass_utils import run_bass_kernel_spmd

F32 = mybir.dt.float32
BF16 = mybir.dt.bfloat16
AF = mybir.ActivationFunctionType
ALU = mybir.AluOpType
AX = mybir.AxisListType.X

B, S, D = 2, 4096, 4096
NH, DH = 32, 128
FC = D // 128
T_OWN = 1024
T_SLAB = 1536
NT_OWN = 8
NT_SLAB = 12
SEGS = 4
SCALE = 1.0 / math.sqrt(DH)
EPS = 1e-5

RA = 0
RB = 65536
RC = 131072
RE = 147456
RF = 163840
ARENA_BYTES = 180224


class Ctx:
    def __init__(self):
        self.cnt = {}

    def bump(self, name, k=1):
        self.cnt[name] = self.cnt.get(name, 0) + k
        return self.cnt[name]

    def val(self, name):
        return self.cnt.get(name, 0)


def build_program(tap=None):
    nc = bass.Bass("TRN2", target_bir_lowering=False)

    xslab = nc.dram_tensor("xslab", [T_SLAB, D], F32, kind="ExternalInput").ap()
    wq_t = nc.dram_tensor("wq_t", [FC, 128, FC, 128], F32, kind="ExternalInput").ap()
    wo_t = nc.dram_tensor("wo_t", [FC, 128, FC, 128], F32, kind="ExternalInput").ap()
    wup_t = nc.dram_tensor("wup_t", [FC, 128, FC, 128], F32, kind="ExternalInput").ap()
    wk_t = nc.dram_tensor("wk_t", [128, FC, 128], F32, kind="ExternalInput").ap()
    wv_t = nc.dram_tensor("wv_t", [128, FC, 128], F32, kind="ExternalInput").ap()
    sh_t = nc.dram_tensor("sh_t", [128, FC, 64], F32, kind="ExternalInput").ap()
    bflat = nc.dram_tensor("bflat", [32, D], F32, kind="ExternalInput").ap()
    ebias = nc.dram_tensor("ebias", [4, 1], F32, kind="ExternalInput").ap()
    eexp = nc.dram_tensor("eexp", [4, 32], F32, kind="ExternalInput").ap()
    identf = nc.dram_tensor("identf", [128, 128], F32, kind="ExternalInput").ap()
    ptmat = nc.dram_tensor("ptmat", [128, 128], F32, kind="ExternalInput").ap()
    cosT = nc.dram_tensor("cosT", [128, T_SLAB], F32, kind="ExternalInput").ap()
    sinT = nc.dram_tensor("sinT", [128, T_SLAB], F32, kind="ExternalInput").ap()
    alphaA = nc.dram_tensor("alphaA", [128, 2 * SEGS], F32, kind="ExternalInput").ap()
    alphaB = nc.dram_tensor("alphaB", [128, 2 * SEGS], F32, kind="ExternalInput").ap()
    maskb = nc.dram_tensor("maskb", [128, 2], F32, kind="ExternalInput").ap()

    out = nc.dram_tensor("out", [T_OWN, D], F32, kind="ExternalOutput").ap()
    xres = nc.dram_tensor("xres", [T_OWN, D], F32, kind="Internal").ap()
    tap_dram = [None]

    ctx = Ctx()
    st = ExitStack()
    e = st.enter_context

    with st:
        s_x = e(nc.semaphore("s_x"))
        s_w = e(nc.semaphore("s_w"))
        s_tr = e(nc.semaphore("s_tr"))
        s_mm = e(nc.semaphore("s_mm"))
        s_v = e(nc.semaphore("s_v"))
        s_s = e(nc.semaphore("s_s"))
        s_res = e(nc.semaphore("s_res"))
        s_out = e(nc.semaphore("s_out"))
        s_bar = e(nc.semaphore("s_bar"))
        s_g = e(nc.semaphore("s_g"))

        cos_sb = e(nc.sbuf_tensor("cos_sb", [128, T_SLAB], F32))
        sin_sb = e(nc.sbuf_tensor("sin_sb", [128, T_SLAB], F32))
        aA_sb = e(nc.sbuf_tensor("aA_sb", [128, 2 * SEGS], F32))
        aB_sb = e(nc.sbuf_tensor("aB_sb", [128, 2 * SEGS], F32))
        mb_sb = e(nc.sbuf_tensor("mb_sb", [128, 2], F32))
        idb_sb = e(nc.sbuf_tensor("idb_sb", [128, 128], BF16))
        idf_sb = e(nc.sbuf_tensor("idf_sb", [128, 128], F32))
        pt_sb = e(nc.sbuf_tensor("pt_sb", [128, 128], BF16))
        eb_sb = e(nc.sbuf_tensor("eb_sb", [36, 1], F32))
        ee_sb = e(nc.sbuf_tensor("ee_sb", [4, 32], BF16))
        sh_sb = e(nc.sbuf_tensor("sh_sb", [128, FC, 64], BF16))
        bfl_sb = e(nc.sbuf_tensor("bfl_sb", [32, D], BF16))
        krT_sb = e(nc.sbuf_tensor("krT_sb", [128, T_SLAB], BF16))
        vaug_sb = e(nc.sbuf_tensor("vaug_sb", [128, NT_SLAB, 132], BF16))
        st_sum = e(nc.sbuf_tensor("st_sum", [128, 8], F32))
        st_ssq = e(nc.sbuf_tensor("st_ssq", [128, 8], F32))
        st_mu = e(nc.sbuf_tensor("st_mu", [128, 8], F32))
        st_var = e(nc.sbuf_tensor("st_var", [128, 8], F32))
        st_rstd = e(nc.sbuf_tensor("st_rstd", [128, 8], F32))
        st_nb = e(nc.sbuf_tensor("st_nb", [128, 8], F32))
        st_nmu = e(nc.sbuf_tensor("st_nmu", [128, 8], F32))
        st_musq = e(nc.sbuf_tensor("st_musq", [128, 8], F32))
        st_var2 = e(nc.sbuf_tensor("st_var2", [128, 8], F32))
        st_rvar = e(nc.sbuf_tensor("st_rvar", [128, 8], F32))
        jnk_v = e(nc.sbuf_tensor("jnk_v", [128, 4], F32))
        jnk_s = e(nc.sbuf_tensor("jnk_s", [128, 4], F32))
        bl_r = e(nc.sbuf_tensor("bl_r", [128, 8], F32))

        arena = e(nc.sbuf_tensor("arena", [128, ARENA_BYTES // 2], BF16))

        def vb(off, *dims):
            n = 1
            for d_ in dims:
                n *= d_
            ap = arena[:, off // 2: off // 2 + n]
            if len(dims) == 2:
                ap = ap.rearrange("p (a b) -> p a b", a=dims[0], b=dims[1])
            return ap

        def vf(off, *dims):
            n = 1
            for d_ in dims:
                n *= d_
            ap = arena[:, off // 2: off // 2 + 2 * n].bitcast(F32)
            if len(dims) == 2:
                ap = ap.rearrange("p (a b) -> p a b", a=dims[0], b=dims[1])
            return ap

        ln1T = vb(RA, FC, T_OWN)
        attnT = vb(RA, NH, T_OWN)
        nat8 = vb(RA, NT_OWN, D)
        ffnT = vb(RA, NH, T_OWN)
        wk_sb = vb(RB, FC, 128)
        wv_sb = vb(RB + 8192, FC, 128)
        xload2 = [vf(RB + 16384, D), vf(RB + 32768, D)]
        qrT = vb(RB, NH, T_OWN)
        projT = vb(RB, NH, T_OWN)
        ln2T = vb(RB, FC, T_OWN)
        natG = vb(RB, NT_OWN, D)
        lnb = [vb(RC, D), vb(RC + 8192, D)]
        wbuf = [vb(RC, FC, 128), vb(RC + 8192, FC, 128)]
        anat = [vb(RC, D), vb(RC + 8192, D)]
        halotmp = [vb(RE, FC, 128), vb(RE + 8192, FC, 128)]
        rstage = vb(RE, T_OWN)
        sigAB = [vb(RE, 2 * T_OWN), vb(RE + 4096, 2 * T_OWN)]
        xnf = vf(RE, D)
        lgT = vf(RE, T_OWN)
        ohT_sb = vb(RE + 4096, T_OWN)
        ex_sb = vf(RE + 6144, T_OWN)
        zmT = vb(RE + 10240, T_OWN)
        mcol = vf(RE + 12288, NT_OWN)
        fflg = vf(RE + 12320, 4)
        ntmp = vf(RE + 12336, 4)
        ohn = vf(RE + 12352, NT_OWN, 4)
        lgN = vf(RE + 12480, NT_OWN, 4)
        fflg8 = vf(RE + 12608, NT_OWN, 4)
        ntA = vf(RE + 12736, NT_OWN)
        ntB = vf(RE + 12768, NT_OWN)
        ntC = vf(RE + 12800, NT_OWN)
        bl_t = vf(RE + 8192, 256)
        dumpA = vb(RF, D)
        vstage = [vb(RF + 8192, 128), vb(RF + 8448, 128)]
        kstage = [vb(RF + 8704, 128), vb(RF + 8960, 128)]
        ropesc = vf(RF + 9216, T_OWN)
        t1f = vf(RF, D)

        block = e(nc.Block())
        ENGS = ("tensor", "vector", "scalar", "sync", "gpsimd")

        def barrier():
            tgt = ctx.bump("s_bar", len(ENGS))
            for en in ENGS:
                def _f(eng, tgt=tgt):
                    eng.sem_inc(s_bar, 1)
                    eng.wait_ge(s_bar, tgt)
                getattr(block, en)(_f)

        def finish_tap(src_ap, shape, dtype):
            tap_dram[0] = nc.dram_tensor("tap", shape, dtype, kind="ExternalOutput").ap()
            barrier()

            def _f(sy):
                sy.dma_start(tap_dram[0], src_ap).then_inc(s_out, 16)
                sy.wait_ge(s_out, ctx.bump("s_out", 16))
            block.sync(_f)

        def ts1(eng, o, i0, s1, op):
            return eng.tensor_scalar(o, i0, s1, 0.0, op, ALU.add)

        def vsync(v):
            v.tensor_copy(jnk_v[:, 2:3], jnk_v[:, 2:3]).then_inc(s_v, 1)
            v.wait_ge(s_v, ctx.bump("s_v"))

        def ssync(s):
            s.activation(jnk_s[:, 0:1], jnk_s[:, 0:1], AF.Identity).then_inc(s_s, 1)
            s.wait_ge(s_s, ctx.bump("s_s"))

        def p0_sync(sy):
            for dst, src in ((cos_sb[:], cosT), (sin_sb[:], sinT), (aA_sb[:], alphaA),
                             (aB_sb[:], alphaB), (mb_sb[:], maskb), (idf_sb[:], identf),
                             (eb_sb[32:36, :], ebias)):
                sy.dma_start(dst, src).then_inc(s_x, 16)
                ctx.bump("s_x", 16)
        block.sync(p0_sync)

        def p0_g(g):
            for dst, src in ((idb_sb[:], identf), (pt_sb[:], ptmat), (ee_sb[:], eexp),
                             (wk_sb, wk_t), (wv_sb, wv_t), (sh_sb[:], sh_t),
                             (bfl_sb[:], bflat)):
                g.dma_start(dst, src).then_inc(s_w, 16)
                ctx.bump("s_w", 16)
            g.memset(vaug_sb[:, :, 128:129], 1.0)
            g.memset(jnk_v[:], 0.0)
            g.memset(jnk_s[:], 0.0).then_inc(s_g, 1)
            ctx.bump("s_g", 1)
        block.gpsimd(p0_g)
        P0_W = ctx.val("s_w")
        P0_G = ctx.val("s_g")

        def ln_tile(src_ap, col, dst_ap, dump_ap, pre_s=None, pre_v=None, pre_s3=None):
            c = slice(col, col + 1)

            def _sc(s):
                if pre_s:
                    for sm, v in pre_s:
                        s.wait_ge(sm, v)
                s.activation(dump_ap, src_ap, AF.Square,
                             accum_out=st_ssq[:, c]).then_inc(s_s, 1)
                s.wait_ge(s_s, ctx.bump("s_s"))
                s.activation(jnk_s[:, 0:1], st_ssq[:, c], AF.Identity).then_inc(s_s, 1)
            block.scalar(_sc)
            ev_ssq = ctx.bump("s_s")

            def _v(v, ev_ssq=ev_ssq):
                if pre_v:
                    for sm, vv in pre_v:
                        v.wait_ge(sm, vv)
                v.reduce_sum(st_sum[:, c], src_ap, axis=AX)
                vsync(v)
                ts1(v, st_mu[:, c], st_sum[:, c], 1.0 / D, ALU.mult)
                ts1(v, st_nmu[:, c], st_sum[:, c], -1.0 / D, ALU.mult)
                v.wait_ge(s_s, ev_ssq)
                v.tensor_scalar(st_var[:, c], st_ssq[:, c], 1.0 / D, EPS,
                                ALU.mult, ALU.add)
                v.tensor_tensor(st_musq[:, c], st_mu[:, c], st_mu[:, c], ALU.mult)
                vsync(v)
                v.scalar_tensor_tensor(st_var2[:, c], st_musq[:, c], -1.0,
                                       st_var[:, c], ALU.mult, ALU.add)
                vsync(v)
                v.reciprocal(st_rvar[:, c], st_var2[:, c])
                v.tensor_copy(jnk_v[:, 0:1], st_rvar[:, c]).then_inc(s_v, 1)
            block.vector(_v)
            ev_rv = ctx.bump("s_v")

            def _sc3(s, ev=ev_rv):
                s.wait_ge(s_v, ev)
                if pre_s3:
                    for sm, vv in pre_s3:
                        s.wait_ge(sm, vv)
                s.activation(st_rstd[:, c], st_rvar[:, c], AF.Sqrt)
                ssync(s)
                s.activation(st_nb[:, c], st_rstd[:, c], AF.Identity,
                             scale=st_nmu[:, c])
                ssync(s)
                s.activation(dst_ap, src_ap, AF.Identity,
                             bias=st_nb[:, c], scale=st_rstd[:, c])
                s.activation(jnk_s[:, 3:4], st_nb[:, c], AF.Identity).then_inc(s_s, 1)
            block.scalar(_sc3)
            return ctx.bump("s_s")

        def _pa_v(v):
            v.wait_ge(s_g, P0_G)
        block.vector(_pa_v)

        def _pa_s(s2):
            s2.wait_ge(s_g, P0_G)
        block.scalar(_pa_s)

        with ExitStack() as ps_st:
            pe = ps_st.enter_context
            ps_k = pe(nc.psum_tensor("ps_k", [128, 128], F32))
            ps_vv = pe(nc.psum_tensor("ps_vv", [128, 128], F32))
            ps_r = pe(nc.psum_tensor("ps_r", [128, 128], F32))
            ps_tp = pe(nc.psum_tensor("ps_tp", [128, 128], BF16))

            HALO = {0: 0, 1: 1, 10: 0, 11: 1}
            halo_kv_done = {}
            norm_prev = 0
            norm_prev2 = [0, 0]
            tr_prev = {}
            kevac = 0
            vevac = 0
            rope_prev = 0
            vcp_prev = 0
            ln_tr_ev = [0] * NT_SLAB
            for i in range(NT_SLAB):
                slot = i % 2

                def _ld(sy, i=i, np_=norm_prev2[i % 2]):
                    if np_:
                        sy.wait_ge(s_s, np_)
                    sy.dma_start(xload2[i % 2],
                                 xslab[128 * i:128 * (i + 1), :]).then_inc(s_x, 16)
                block.sync(_ld)
                ev_x = ctx.bump("s_x", 16)
                xload = xload2[i % 2]

                pre3 = [(s_tr, tr_prev[i - 2])] if i >= 2 else None
                ev_norm = ln_tile(xload, i % 4, lnb[slot], dumpA,
                                  pre_s=[(s_x, ev_x)], pre_v=[(s_x, ev_x)], pre_s3=pre3)
                norm_prev = ev_norm
                norm_prev2[i % 2] = ev_norm

                if i in HALO:
                    hs = HALO[i]
                    dest = halotmp[hs]
                else:
                    dest = ln1T[:, :, 128 * (i - 2):128 * (i - 1)]
                    hs = None

                def _tr(s, i=i, slot=slot, ev=ev_norm, hs=hs, dest=dest):
                    s.wait_ge(s_s, ev)
                    if hs is not None and hs in halo_kv_done:
                        s.wait_ge(s_mm, halo_kv_done[hs])
                    s.dma_start_transpose(dest, lnb[slot]).then_inc(s_tr, 16)
                block.scalar(_tr)
                ln_tr_ev[i] = ctx.bump("s_tr", 16)
                tr_prev[i] = ln_tr_ev[i]

                def _t(t, i=i, dest=dest, ev_tr=ln_tr_ev[i], ke=kevac, ve=vevac):
                    if i == 0:
                        t.wait_ge(s_w, P0_W)
                        t.wait_ge(s_g, P0_G)
                    t.wait_ge(s_tr, ev_tr)
                    if ke:
                        t.wait_ge(s_s, ke)
                    for fc in range(FC):
                        m = t.matmul(ps_k[:], wk_sb[:, fc, :], dest[:, fc, :],
                                     start=(fc == 0), stop=(fc == FC - 1))
                    m.then_inc(s_mm, 1)
                    if ve:
                        t.wait_ge(s_s, ve)
                    for fc in range(FC):
                        m = t.matmul(ps_vv[:], wv_sb[:, fc, :], dest[:, fc, :],
                                     start=(fc == 0), stop=(fc == FC - 1))
                    m.then_inc(s_mm, 1)
                block.tensor(_t)
                ev_k = ctx.bump("s_mm")
                ev_vmm = ctx.bump("s_mm")
                if i in HALO:
                    halo_kv_done[HALO[i]] = ev_vmm

                def _s(s, i=i, slot=slot, evk=ev_k, evv=ev_vmm, rp=rope_prev, vc=vcp_prev):
                    s.wait_ge(s_mm, evk)
                    if rp:
                        s.wait_ge(s_v, rp)
                    s.activation(kstage[slot], ps_k[:], AF.Identity)
                    s.activation(jnk_s[:, 1:2], kstage[slot][:, 0:1],
                                 AF.Identity).then_inc(s_s, 1)
                    s.wait_ge(s_mm, evv)
                    if vc:
                        s.wait_ge(s_v, vc)
                    s.activation(vstage[slot], ps_vv[:], AF.Identity)
                    s.activation(jnk_s[:, 2:3], vstage[slot][:, 0:1],
                                 AF.Identity).then_inc(s_s, 1)
                block.scalar(_s)
                kevac = ctx.bump("s_s")
                vevac = ctx.bump("s_s")

                def _t2(t, slot=slot, ke=kevac, ve=vevac, rp=rope_prev, vc=vcp_prev):
                    t.wait_ge(s_s, ke)
                    if rp:
                        t.wait_ge(s_v, rp)
                    t.matmul(ps_r[:], pt_sb[:], kstage[slot],
                             start=True, stop=True).then_inc(s_mm, 1)
                    t.wait_ge(s_s, ve)
                    if vc:
                        t.wait_ge(s_v, vc)
                    t.transpose(ps_tp[:], vstage[slot], idb_sb[:]).then_inc(s_mm, 1)
                block.tensor(_t2)
                ev_rot = ctx.bump("s_mm")
                ev_vtp = ctx.bump("s_mm")

                def _v(v, i=i, slot=slot, er=ev_rot, et=ev_vtp):
                    cols = slice(128 * i, 128 * (i + 1))
                    v.wait_ge(s_mm, er)
                    v.tensor_tensor(ropesc[:, 0:128], kstage[slot], cos_sb[:, cols], ALU.mult)
                    v.tensor_tensor(ropesc[:, 128:256], ps_r[:], sin_sb[:, cols], ALU.mult)
                    vsync(v)
                    v.tensor_tensor(krT_sb[:, cols], ropesc[:, 0:128], ropesc[:, 128:256],
                                    ALU.add)
                    v.tensor_copy(jnk_v[:, 3:4], krT_sb[:, cols][:, 0:1]).then_inc(s_v, 1)
                    v.wait_ge(s_mm, et)
                    v.tensor_copy(vaug_sb[:, i, 0:128], ps_tp[:])
                    v.tensor_copy(jnk_v[:, 3:4], vaug_sb[:, i, 0:1]).then_inc(s_v, 1)
                block.vector(_v)
                rope_prev = ctx.bump("s_v")
                vcp_prev = ctx.bump("s_v")

            if tap == "kst":
                kdbg = vb(RF + 13312, 1024)

                def _vdbg(v):
                    kf = kdbg.bitcast(F32)
                    v.tensor_copy(kf[:, 0:1], st_sum[:, 3:4])
                    v.tensor_copy(kf[:, 1:2], st_ssq[:, 3:4])
                    v.tensor_copy(kf[:, 2:3], st_mu[:, 3:4])
                    v.tensor_copy(kf[:, 3:4], st_nmu[:, 3:4])
                    v.tensor_copy(kf[:, 4:5], st_var[:, 3:4])
                    v.tensor_copy(kf[:, 5:6], st_musq[:, 3:4])
                    v.tensor_copy(kf[:, 6:7], st_var2[:, 3:4])
                    v.tensor_copy(kf[:, 7:8], st_rvar[:, 3:4])
                    v.tensor_copy(kf[:, 8:9], st_rstd[:, 3:4])
                    v.tensor_copy(kf[:, 9:10], st_nb[:, 3:4])
                    v.tensor_copy(kdbg[:, 512:1024], lnb[1][:, 0:512]).then_inc(s_v, 1)
                block.vector(_vdbg)
                ctx.bump("s_v")
                finish_tap(kdbg, [128, 1024], BF16)
                return nc, tap_dram[0]
            if tap == "ln1T":
                finish_tap(ln1T, [128, FC, T_OWN], BF16)
                return nc, tap_dram[0]
            if tap == "kv":
                finish_tap(krT_sb[:], [128, T_SLAB], BF16)
                return nc, tap_dram[0]
            if tap == "vaug":
                finish_tap(vaug_sb[:], [128, NT_SLAB, 132], BF16)
                return nc, tap_dram[0]
            barrier()

        with ExitStack() as ps_st:
            pe = ps_st.enter_context
            ps_q = [pe(nc.psum_tensor(f"ps_q{i}", [128, 512], F32)) for i in range(2)]
            ps_r2 = [pe(nc.psum_tensor(f"ps_r2{i}", [128, 512], F32)) for i in range(2)]

            wbuf_free = [0, 0]
            psq_free = [0, 0]
            psr_free = [0, 0]
            qst_free = [0, 0]
            for ot in range(NH):
                slot = ot % 2

                def _g(g, ot=ot, slot=slot, w=wbuf_free[slot]):
                    if w:
                        g.wait_ge(s_mm, w)
                    g.dma_start(wbuf[slot], wq_t[ot]).then_inc(s_w, 16)
                block.gpsimd(_g)
                ev_w = ctx.bump("s_w", 16)

                last_mm = None
                for th in range(2):
                    def _t(t, th=th, slot=slot, ev_w=ev_w, pf=psq_free[th]):
                        t.wait_ge(s_w, ev_w)
                        if pf:
                            t.wait_ge(s_s, pf)
                        for fc in range(FC):
                            m = t.matmul(ps_q[th][:], wbuf[slot][:, fc, :],
                                         ln1T[:, fc, 512 * th:512 * (th + 1)],
                                         start=(fc == 0), stop=(fc == FC - 1))
                        m.then_inc(s_mm, 1)
                    block.tensor(_t)
                    ev_mm = ctx.bump("s_mm")
                    last_mm = ev_mm

                    def _s(s, th=th, ev=ev_mm, qf=qst_free[th]):
                        s.wait_ge(s_mm, ev)
                        if qf:
                            s.wait_ge(s_v, qf)
                        s.activation(rstage[:, 512 * th:512 * (th + 1)], ps_q[th][:],
                                     AF.Identity)
                        s.activation(jnk_s[:, 1:2], rstage[:, 512 * th:512 * th + 1],
                                     AF.Identity).then_inc(s_s, 1)
                    block.scalar(_s)
                    ev_ev = ctx.bump("s_s")
                    psq_free[th] = ev_ev

                    def _t2(t, th=th, ev=ev_ev, rf=psr_free[th]):
                        t.wait_ge(s_s, ev)
                        if rf:
                            t.wait_ge(s_v, rf)
                        t.matmul(ps_r2[th][:], pt_sb[:], rstage[:, 512 * th:512 * (th + 1)],
                                 start=True, stop=True).then_inc(s_mm, 1)
                    block.tensor(_t2)
                    ev_rot = ctx.bump("s_mm")

                    def _v(v, ot=ot, th=th, ev=ev_rot):
                        v.wait_ge(s_mm, ev)
                        ccols = slice(256 + 512 * th, 256 + 512 * (th + 1))
                        v.tensor_tensor(ropesc[:, 0:512], rstage[:, 512 * th:512 * (th + 1)],
                                        cos_sb[:, ccols], ALU.mult)
                        v.tensor_tensor(ropesc[:, 512:1024], ps_r2[th][:],
                                        sin_sb[:, ccols], ALU.mult)
                        v.tensor_tensor(qrT[:, ot, 512 * th:512 * (th + 1)],
                                        ropesc[:, 0:512], ropesc[:, 512:1024],
                                        ALU.add)
                        v.tensor_copy(jnk_v[:, 3:4],
                                      qrT[:, ot, 512 * th:512 * th + 1]).then_inc(s_v, 1)
                    block.vector(_v)
                    ev_rope = ctx.bump("s_v")
                    psr_free[th] = ev_rope
                    qst_free[th] = ev_rope
                wbuf_free[slot] = last_mm

            if tap == "qrT":
                finish_tap(qrT, [128, NH, T_OWN], BF16)
                return nc, tap_dram[0]
            barrier()

        with ExitStack() as ps_st:
            pe = ps_st.enter_context
            ps_scA = pe(nc.psum_tensor("ps_scA", [128, 1024], F32))
            ps_scB = pe(nc.psum_tensor("ps_scB", [128, 1024], F32))
            ps_wA = pe(nc.psum_tensor("ps_wA", [128, 512], F32))
            ps_wB = pe(nc.psum_tensor("ps_wB", [128, 512], F32))

            sigA_done = 0
            sigB_done = 0
            blend_done = 0
            anat_guard = 0
            anat_tr = {}
            heads = [(s_i, h) for s_i in range(SEGS) for h in range(NH)]

            def emit_scores(idx):
                s_i, h = heads[idx]
                qcols = slice(256 * s_i, 256 * (s_i + 1))

                def _t(t, s_i=s_i, h=h, qcols=qcols, sa=sigA_done, sb=sigB_done):
                    if sa:
                        t.wait_ge(s_s, sa)
                    for j in range(4):
                        t.matmul(ps_scA[:, 256 * j:256 * (j + 1)],
                                 krT_sb[:, 128 * (2 * s_i + j):128 * (2 * s_i + j + 1)],
                                 qrT[:, h, qcols], start=True,
                                 stop=True).then_inc(s_mm, 1) if j == 3 else \
                        t.matmul(ps_scA[:, 256 * j:256 * (j + 1)],
                                 krT_sb[:, 128 * (2 * s_i + j):128 * (2 * s_i + j + 1)],
                                 qrT[:, h, qcols], start=True, stop=True)
                    if sb:
                        t.wait_ge(s_s, sb)
                    for j in range(4):
                        m = t.matmul(ps_scB[:, 256 * j:256 * (j + 1)],
                                     krT_sb[:, 128 * (2 * s_i + 2 + j):128 * (2 * s_i + 3 + j)],
                                     qrT[:, h, qcols], start=True, stop=True)
                    m.then_inc(s_mm, 1)
                block.tensor(_t)
                evA = ctx.bump("s_mm")
                evB = ctx.bump("s_mm")
                return evA, evB

            def emit_sig(idx, evA, evB):
                nonlocal sigA_done, sigB_done
                s_i, h = heads[idx]

                sl = sigAB[idx % 2]
                sigA = sl[:, 0:1024]
                sigB = sl[:, 1024:2048]

                def _s(s, s_i=s_i, evA=evA, evB=evB, sigA=sigA, sigB=sigB):
                    s.wait_ge(s_mm, evA)
                    s.activation(sigA, ps_scA[:], AF.Sigmoid,
                                 scale=SCALE).then_inc(s_s, 1)
                    s.wait_ge(s_mm, evB)
                    if s_i == SEGS - 1:
                        s.activation(sigB[:, 0:512], ps_scB[:, 0:512],
                                     AF.Sigmoid, scale=SCALE)
                        s.activation(sigB[:, 512:768], ps_scB[:, 512:768], AF.Sigmoid,
                                     scale=SCALE, bias=mb_sb[:, 0:1])
                        s.activation(sigB[:, 768:1024], ps_scB[:, 768:1024], AF.Sigmoid,
                                     scale=SCALE, bias=mb_sb[:, 1:2])
                    else:
                        s.activation(sigB, ps_scB[:], AF.Sigmoid, scale=SCALE)
                    s.activation(jnk_s[:, 1:2], sigB[:, 1023:1024],
                                 AF.Identity).then_inc(s_s, 1)
                block.scalar(_s)
                sigA_done = ctx.bump("s_s")
                sigB_done = ctx.bump("s_s")

            def emit_wout(idx, sd):
                s_i, h = heads[idx]
                sl = sigAB[idx % 2]
                sigA = sl[:, 0:1024]
                sigB = sl[:, 1024:2048]

                def _t2(t, s_i=s_i, sd=sd, bd=blend_done, sigA=sigA, sigB=sigB):
                    t.wait_ge(s_s, sd)
                    if bd:
                        t.wait_ge(s_v, bd)
                    for qc in range(2):
                        off = 132 * qc
                        for j in range(4):
                            t.matmul(ps_wA[:, off:off + 129],
                                     sigA[:, 256 * j + 128 * qc:256 * j + 128 * (qc + 1)],
                                     vaug_sb[:, 2 * s_i + j, 0:129],
                                     start=(j == 0), stop=(j == 3))
                        for j in range(4):
                            m = t.matmul(ps_wB[:, off:off + 129],
                                         sigB[:, 256 * j + 128 * qc:256 * j + 128 * (qc + 1)],
                                         vaug_sb[:, 2 * s_i + 2 + j, 0:129],
                                         start=(j == 0), stop=(j == 3))
                    m.then_inc(s_mm, 1)
                block.tensor(_t2)
                return ctx.bump("s_mm")

            def emit_blend(idx, ev_w2):
                nonlocal blend_done
                s_i, h = heads[idx]

                def _v(v, s_i=s_i, h=h, ev=ev_w2, ag=anat_guard):
                    v.wait_ge(s_mm, ev)
                    if ag and h == 0:
                        v.wait_ge(s_tr, ag)
                    sA2 = ps_wA[:, 0:264].rearrange("p (a b) -> p a b", a=2, b=132)[
                        :, :, 128:129].rearrange("p a b -> p (a b)")
                    sB2 = ps_wB[:, 0:264].rearrange("p (a b) -> p a b", a=2, b=132)[
                        :, :, 128:129].rearrange("p a b -> p (a b)")
                    v.reciprocal(bl_r[:, 0:2], sA2)
                    v.reciprocal(bl_r[:, 2:4], sB2)
                    vsync(v)
                    acols = slice(2 * s_i, 2 * s_i + 2)
                    v.tensor_tensor(bl_r[:, 4:6], bl_r[:, 0:2], aA_sb[:, acols], ALU.mult)
                    v.tensor_tensor(bl_r[:, 6:8], bl_r[:, 2:4], aB_sb[:, acols], ALU.mult)
                    vsync(v)
                    for qc in range(2):
                        ts1(v, bl_t[:, 128 * qc:128 * (qc + 1)],
                            ps_wA[:, 132 * qc:132 * qc + 128],
                            bl_r[:, 4 + qc:5 + qc], ALU.mult)
                    for qc in range(2):
                        v.scalar_tensor_tensor(
                            anat[qc][:, 128 * h:128 * (h + 1)],
                            ps_wB[:, 132 * qc:132 * qc + 128],
                            bl_r[:, 6 + qc:7 + qc],
                            bl_t[:, 128 * qc:128 * (qc + 1)],
                            ALU.mult, ALU.add)
                    v.tensor_copy(jnk_v[:, 3:4],
                                  anat[1][:, 128 * h:128 * h + 1]).then_inc(s_v, 1)
                block.vector(_v)
                blend_done = ctx.bump("s_v")

            pend = None
            for idx in range(len(heads)):
                evA, evB = emit_scores(idx)
                emit_sig(idx, evA, evB)
                my_sig = sigB_done
                if pend is not None:
                    ev_w2 = emit_wout(pend[0], pend[1])
                    emit_blend(pend[0], ev_w2)
                pend = (idx, my_sig)
                s_i, h = heads[idx]
                if h == NH - 1 and pend is not None:
                    ev_w2 = emit_wout(pend[0], pend[1])
                    emit_blend(pend[0], ev_w2)
                    pend = None
                    for qc in range(2):
                        ti = 2 * s_i + qc

                        def _tr(s, ti=ti, qc=qc, bd=blend_done):
                            s.wait_ge(s_v, bd)
                            s.dma_start_transpose(
                                attnT[:, :, 128 * ti:128 * (ti + 1)],
                                anat[qc]).then_inc(s_tr, 16)
                        block.scalar(_tr)
                        anat_tr[ti] = ctx.bump("s_tr", 16)
                    anat_guard = anat_tr[2 * s_i + 1]

            if tap == "wlast":
                wdbg = vf(RE + 8192, T_OWN)

                def _vdbg(v, bd=blend_done):
                    v.wait_ge(s_v, bd)
                    v.tensor_copy(wdbg[:, 0:264], ps_wA.bitcast(F32)[:, 0:264])
                    v.tensor_copy(wdbg[:, 264:528], ps_wB.bitcast(F32)[:, 0:264]).then_inc(s_v, 1)
                block.vector(_vdbg)
                ctx.bump("s_v")
                finish_tap(wdbg, [128, T_OWN], F32)
                return nc, tap_dram[0]
            if tap == "sclast":
                wdbg2 = vf(RE + 8192, T_OWN)

                def _vdbg2(v, ev=ev_sc):
                    v.wait_ge(s_mm, ev)
                    v.tensor_copy(wdbg2[:], ps_scB[:]).then_inc(s_v, 1)
                block.vector(_vdbg2)
                ctx.bump("s_v")
                finish_tap(wdbg2, [128, T_OWN], F32)
                return nc, tap_dram[0]
            if tap == "siglast":
                finish_tap(sigB, [128, T_OWN], BF16)
                return nc, tap_dram[0]
            if tap == "attnT":
                finish_tap(attnT, [128, NH, T_OWN], BF16)
                return nc, tap_dram[0]
            attn_tr_last = anat_tr[2 * SEGS - 1]
            barrier()

        def big_proj(wdram, rhs_fn, dst, delta_fn=None, first_wait=None):
            with ExitStack() as ps_st2:
                pe2 = ps_st2.enter_context
                ps_p = [pe2(nc.psum_tensor(f"ps_p{ctx.bump('psn')}", [128, 512], F32))
                        for _ in range(2)]
                wb_free = [0, 0]
                ps_free = [0, 0]
                for ot in range(FC):
                    slot = ot % 2

                    def _g(g, ot=ot, slot=slot, w=wb_free[slot]):
                        if w:
                            g.wait_ge(s_mm, w)
                        g.dma_start(wbuf[slot], wdram[ot]).then_inc(s_w, 16)
                    block.gpsimd(_g)
                    ev_w = ctx.bump("s_w", 16)

                    last = None
                    for th in range(2):
                        def _t(t, ot=ot, th=th, slot=slot, ev_w=ev_w, pf=ps_free[th]):
                            if ot == 0 and th == 0 and first_wait:
                                for sm, vv in first_wait:
                                    t.wait_ge(sm, vv)
                            t.wait_ge(s_w, ev_w)
                            if pf:
                                t.wait_ge(s_s, pf)
                            first = True
                            if delta_fn is not None:
                                delta_fn(t, ps_p[th], ot, th)
                                first = False
                            for fc in range(FC):
                                m = t.matmul(ps_p[th][:], wbuf[slot][:, fc, :],
                                             rhs_fn(fc, th),
                                             start=(first and fc == 0),
                                             stop=(fc == FC - 1))
                            m.then_inc(s_mm, 1)
                        block.tensor(_t)
                        ev_mm = ctx.bump("s_mm")
                        last = ev_mm

                        def _s(s, ot=ot, th=th, ev=ev_mm):
                            s.wait_ge(s_mm, ev)
                            s.activation(dst[:, ot, 512 * th:512 * (th + 1)],
                                         ps_p[th][:], AF.Identity)
                            s.activation(jnk_s[:, 1:2],
                                         dst[:, ot, 512 * th:512 * th + 1],
                                         AF.Identity).then_inc(s_s, 1)
                        block.scalar(_s)
                        ps_free[th] = ctx.bump("s_s")
                    wb_free[slot] = last

        big_proj(wo_t,
                 lambda fc, th: attnT[:, fc, 512 * th:512 * (th + 1)],
                 projT, first_wait=[(s_tr, attn_tr_last)])
        if tap == "projT":
            finish_tap(projT, [128, NH, T_OWN], BF16)
            return nc, tap_dram[0]
        barrier()

        def resid_phase(srcT, natv, is_final):
            ptr = 0
            for ot in range(FC):
                def _tr(s, ot=ot):
                    s.dma_start_transpose(natv[:, :, 128 * ot:128 * (ot + 1)],
                                          srcT[:, ot, :]).then_inc(s_tr, 16)
                block.scalar(_tr)
                ptr = ctx.bump("s_tr", 16)

            xn_free = None
            prev_add = 0
            ln2_tr_local = {}
            dumpR = vb(RC + 8192, D) if not is_final else vb(RC, D)
            for i in range(NT_OWN):
                slot = 0

                def _ld(sy, i=i, ptr=ptr, xe=xn_free):
                    if i == 0:
                        sy.wait_ge(s_tr, ptr)
                        if is_final:
                            sy.wait_ge(s_res, ctx.val("s_res"))
                    if xe is not None:
                        if is_final:
                            sy.wait_ge(s_out, xe)
                        else:
                            sy.wait_ge(s_res, xe[0])
                            sy.wait_ge(s_s, xe[1])
                    src = (xres[128 * i:128 * (i + 1), :] if is_final
                           else xslab[256 + 128 * i:256 + 128 * (i + 1), :])
                    sy.dma_start(xnf, src).then_inc(s_x, 16)
                block.sync(_ld)
                ev_x = ctx.bump("s_x", 16)

                pre = [(s_tr, ptr)] if i == 0 else None
                pre3 = [(s_v, prev_add)] if prev_add else None
                ev_t1 = ln_tile(natv[:, i, :], i % 4, t1f, dumpR,
                                pre_s=pre, pre_v=pre, pre_s3=pre3)

                def _v3(v, ev_t1=ev_t1, ev_x=ev_x):
                    v.wait_ge(s_s, ev_t1)
                    v.wait_ge(s_x, ev_x)
                    v.tensor_tensor(xnf, xnf, t1f, ALU.add)
                    v.tensor_copy(jnk_v[:, 3:4], xnf[:, 0:1]).then_inc(s_v, 1)
                block.vector(_v3)
                ev_xn = ctx.bump("s_v")
                prev_add = ev_xn

                if is_final:
                    def _st(sy, i=i, ev=ev_xn):
                        sy.wait_ge(s_v, ev)
                        sy.dma_start(out[128 * i:128 * (i + 1), :],
                                     xnf).then_inc(s_out, 16)
                    block.sync(_st)
                    xn_free = ctx.bump("s_out", 16)
                else:
                    def _st(sy, i=i, ev=ev_xn):
                        sy.wait_ge(s_v, ev)
                        sy.dma_start(xres[128 * i:128 * (i + 1), :],
                                     xnf).then_inc(s_res, 16)
                    block.sync(_st)
                    ev_res = ctx.bump("s_res", 16)

                    pre3b = [(s_tr, ln2_tr_local[i - 1])] if i >= 1 else None
                    ev_n2 = ln_tile(xnf, 4 + i % 4, lnb[slot], dumpR,
                                    pre_s=[(s_v, ev_xn)], pre_v=[(s_v, ev_xn)],
                                    pre_s3=pre3b)

                    def _tr2(s, i=i, slot=slot, ev=ev_n2):
                        s.wait_ge(s_s, ev)
                        s.dma_start_transpose(ln2T[:, :, 128 * i:128 * (i + 1)],
                                              lnb[slot]).then_inc(s_tr, 16)
                    block.scalar(_tr2)
                    ln2_tr_local[i] = ctx.bump("s_tr", 16)
                    xn_free = (ev_res, ev_n2)
            return ln2_tr_local.get(NT_OWN - 1, 0)

        ln2_tr_last = resid_phase(projT, nat8, is_final=False)
        if tap == "ln2T":
            finish_tap(ln2T, [128, FC, T_OWN], BF16)
            return nc, tap_dram[0]
        barrier()

        with ExitStack() as ps_st:
            pe = ps_st.enter_context
            ps_sh = pe(nc.psum_tensor("ps_sh", [64, T_OWN], F32))
            ps_tpf = pe(nc.psum_tensor("ps_tpf", [128, 128], F32))
            ps_oh = pe(nc.psum_tensor("ps_oh", [4, T_OWN], F32))
            ps_ex = pe(nc.psum_tensor("ps_ex", [32, T_OWN], F32))

            def _gms(g):
                g.memset(lgT, 0.0).then_inc(s_g, 1)
            block.gpsimd(_gms)
            ev_ms = ctx.bump("s_g", 1)

            def _tf1(t):
                t.wait_ge(s_tr, ln2_tr_last)
                for th in range(2):
                    for fc in range(FC):
                        m = t.matmul(ps_sh[:, 512 * th:512 * (th + 1)], sh_sb[:, fc, :],
                                     ln2T[:, fc, 512 * th:512 * (th + 1)],
                                     start=(fc == 0), stop=(fc == FC - 1))
                m.then_inc(s_mm, 1)
            block.tensor(_tf1)
            ev_sh = ctx.bump("s_mm")

            def _vf2(v, ev=ev_sh, evm=ev_ms):
                v.wait_ge(s_mm, ev)
                v.wait_ge(s_g, evm)
                ts1(v, lgT[32:36, :], ps_sh[32:36, :], eb_sb[32:36, :], ALU.add)
                v.tensor_copy(jnk_v[0:4, 3:4], lgT[32:36, 0:1]).then_inc(s_v, 1)
            block.vector(_vf2)
            ev_lg = ctx.bump("s_v")

            cp_ev = 0
            for j in range(NT_OWN):
                def _t(t, j=j, ev=ev_lg, ce=cp_ev):
                    if j == 0:
                        t.wait_ge(s_v, ev)
                    if ce:
                        t.wait_ge(s_v, ce)
                    t.transpose(ps_tpf[:], lgT[:, 128 * j:128 * (j + 1)],
                                idf_sb[:]).then_inc(s_mm, 1)
                block.tensor(_t)
                ev_tp = ctx.bump("s_mm")

                def _v(v, j=j, ev=ev_tp):
                    v.wait_ge(s_mm, ev)
                    v.tensor_copy(lgN[:, j, :], ps_tpf[:, 32:36])
                    v.tensor_copy(jnk_v[:, 3:4], lgN[:, j, 0:1]).then_inc(s_v, 1)
                block.vector(_v)
                cp_ev = ctx.bump("s_v")

            def _vcas(v, ce=cp_ev):
                v.wait_ge(s_v, ce)
                for j in range(NT_OWN):
                    v.reduce_max(mcol[:, j:j + 1], lgN[:, j, :], axis=AX)
                vsync(v)
                for j in range(NT_OWN):
                    ts1(v, fflg8[:, j, :], lgN[:, j, :], mcol[:, j:j + 1], ALU.is_equal)
                vsync(v)
                for j in range(NT_OWN):
                    v.tensor_copy(ohn[:, j, 0:1], fflg8[:, j, 0:1])
                for j in range(NT_OWN):
                    v.tensor_scalar(ntA[:, j:j + 1], fflg8[:, j, 0:1], -1.0, 1.0,
                                    ALU.mult, ALU.add)
                vsync(v)
                for j in range(NT_OWN):
                    v.tensor_tensor(ohn[:, j, 1:2], fflg8[:, j, 1:2], ntA[:, j:j + 1],
                                    ALU.mult)
                for j in range(NT_OWN):
                    v.tensor_scalar(ntB[:, j:j + 1], fflg8[:, j, 1:2], -1.0, 1.0,
                                    ALU.mult, ALU.add)
                vsync(v)
                for j in range(NT_OWN):
                    v.tensor_tensor(ntC[:, j:j + 1], ntA[:, j:j + 1], ntB[:, j:j + 1],
                                    ALU.mult)
                vsync(v)
                for j in range(NT_OWN):
                    v.tensor_tensor(ohn[:, j, 2:3], fflg8[:, j, 2:3], ntC[:, j:j + 1],
                                    ALU.mult)
                for j in range(NT_OWN):
                    v.tensor_scalar(ntB[:, j:j + 1], fflg8[:, j, 2:3], -1.0, 1.0,
                                    ALU.mult, ALU.add)
                vsync(v)
                for j in range(NT_OWN):
                    v.tensor_tensor(ntA[:, j:j + 1], ntC[:, j:j + 1], ntB[:, j:j + 1],
                                    ALU.mult)
                vsync(v)
                for j in range(NT_OWN):
                    v.tensor_tensor(ohn[:, j, 3:4], fflg8[:, j, 3:4], ntA[:, j:j + 1],
                                    ALU.mult)
                v.tensor_copy(jnk_v[:, 1:2], ohn[:, NT_OWN - 1, 3:4]).then_inc(s_v, 1)
            block.vector(_vcas)
            oh_ev = ctx.bump("s_v")

            tpb_last = 0
            for j in range(NT_OWN):
                def _t(t, j=j, ev=oh_ev):
                    if j == 0:
                        t.wait_ge(s_v, ev)
                    t.transpose(ps_oh[:, 128 * j:128 * (j + 1)], ohn[:, j, :],
                                idf_sb[:]).then_inc(s_mm, 1)
                block.tensor(_t)
                tpb_last = ctx.bump("s_mm")

            def _vf3(v, ev=tpb_last):
                v.wait_ge(s_mm, ev)
                v.tensor_copy(ohT_sb[0:4, :], ps_oh[:])
                v.tensor_copy(jnk_v[0:4, 3:4], ohT_sb[0:4, 0:1]).then_inc(s_v, 1)
            block.vector(_vf3)
            ev_ohT = ctx.bump("s_v")

            def _tf4(t, ev=ev_ohT):
                t.wait_ge(s_v, ev)
                for th in range(2):
                    m = t.matmul(ps_ex[:, 512 * th:512 * (th + 1)], ee_sb[:],
                                 ohT_sb[0:4, 512 * th:512 * (th + 1)], start=True, stop=True)
                m.then_inc(s_mm, 1)
            block.tensor(_tf4)
            ev_ex = ctx.bump("s_mm")

            def _vf5(v, ev=ev_ex):
                v.wait_ge(s_mm, ev)
                v.tensor_copy(ex_sb[0:32, :], ps_ex[:])
                vsync(v)
                v.tensor_tensor(zmT[0:32, :], ps_sh[0:32, :], ex_sb[0:32, :], ALU.mult)
                v.tensor_copy(jnk_v[0:32, 3:4], zmT[0:32, 0:1]).then_inc(s_v, 1)
            block.vector(_vf5)
            ev_zm = ctx.bump("s_v")

            def _tf6(t, ev=ev_zm):
                t.wait_ge(s_v, ev)
            block.tensor(_tf6)

            if tap == "zmT":
                finish_tap(zmT[0:32, :], [32, T_OWN], BF16)
                return nc, tap_dram[0]

        def delta_fn(t, ps, ot, th):
            t.matmul(ps[:], bfl_sb[:, 128 * ot:128 * (ot + 1)],
                     zmT[0:32, 512 * th:512 * (th + 1)], start=True, stop=False)

        big_proj(wup_t,
                 lambda fc, th: ln2T[:, fc, 512 * th:512 * (th + 1)],
                 ffnT, delta_fn=delta_fn, first_wait=[(s_tr, ln2_tr_last)])

        if tap == "ffnT":
            finish_tap(ffnT, [128, NH, T_OWN], BF16)
            return nc, tap_dram[0]
        barrier()

        resid_phase(ffnT, natG, is_final=True)

        def _fin(sy):
            sy.wait_ge(s_out, ctx.val("s_out"))
        block.sync(_fin)

    return nc, tap_dram[0]



def _rope_cache(S_, D_, base=10000.0):
    inv_freq = 1.0 / base ** (np.arange(0, D_, 2, dtype=np.float32) / D_)
    t = np.arange(S_, dtype=np.float32)
    f = np.outer(t, inv_freq)
    cos = np.zeros((S_, D_), dtype=np.float32)
    sin = np.zeros((S_, D_), dtype=np.float32)
    cos[:, 0::2] = np.cos(f)
    cos[:, 1::2] = np.cos(f)
    sin[:, 0::2] = np.sin(f)
    sin[:, 1::2] = np.sin(f)
    return cos, sin


def _tile_w(wT):
    f, o = wT.shape
    if o <= 128:
        return np.ascontiguousarray(wT.reshape(FC, 128, o).transpose(1, 0, 2))
    return np.ascontiguousarray(
        wT.reshape(FC, 128, o // 128, 128).transpose(2, 1, 0, 3))


def make_inputs(x, Wq, Wk, Wv, Wo, W_up, gate_w, expert_bias, lora_A, lora_B):
    cos, sin = _rope_cache(S, DH)
    wq_t = _tile_w(np.ascontiguousarray(Wq.T))
    wo_t = _tile_w(np.ascontiguousarray(Wo.T))
    wup_t = _tile_w(np.ascontiguousarray(W_up.T))
    wk_t = _tile_w(np.ascontiguousarray(Wk.T))
    wv_t = _tile_w(np.ascontiguousarray(Wv.T))
    shW = np.zeros((D, 64), np.float32)
    shW[:, 0:32] = lora_A.reshape(32, D).T
    shW[:, 32:36] = gate_w.T
    sh_t = _tile_w(shW)
    bflat = np.ascontiguousarray(lora_B.transpose(0, 2, 1).reshape(32, D))
    eexp = np.zeros((4, 32), np.float32)
    for ei in range(4):
        eexp[ei, ei * 8:(ei + 1) * 8] = 1.0
    identf = np.eye(128, dtype=np.float32)
    pmat = np.zeros((128, 128), np.float32)
    for d_ in range(64):
        pmat[d_, d_ + 64] = -1.0
        pmat[d_ + 64, d_] = 1.0
    ptmat = np.ascontiguousarray(pmat.T)
    lin = np.linspace(0.0, 1.0, 256, dtype=np.float32)

    maps = []
    for c in range(8):
        b, j = divmod(c, 4)
        g0 = 1024 * j - 256
        xs = np.zeros((T_SLAB, D), np.float32)
        lo, hi = max(0, g0), min(S, g0 + T_SLAB)
        xs[lo - g0:hi - g0, :] = x[b, lo:hi, :]
        ct = np.zeros((128, T_SLAB), np.float32)
        sn = np.zeros((128, T_SLAB), np.float32)
        ct[:, lo - g0:hi - g0] = cos[lo:hi].T
        sn[:, lo - g0:hi - g0] = sin[lo:hi].T
        aA = np.zeros((128, 2 * SEGS), np.float32)
        aB = np.zeros((128, 2 * SEGS), np.float32)
        for s_i in range(SEGS):
            g = 4 * j + s_i
            ab = np.ones(256, np.float32) if g == 0 else lin
            for qc in range(2):
                aB[:, 2 * s_i + qc] = ab[128 * qc:128 * (qc + 1)]
                aA[:, 2 * s_i + qc] = np.float32(1.0) - ab[128 * qc:128 * (qc + 1)]
        mb = np.zeros((128, 2), np.float32)
        if j == 3:
            mb[:] = -200.0
        maps.append(dict(
            xslab=xs, wq_t=wq_t, wo_t=wo_t, wup_t=wup_t, wk_t=wk_t, wv_t=wv_t,
            sh_t=sh_t, bflat=bflat,
            ebias=np.asarray(expert_bias, np.float32).reshape(4, 1),
            eexp=eexp, identf=identf, ptmat=ptmat, cosT=ct, sinT=sn,
            alphaA=aA, alphaB=aB, maskb=mb))
    return maps


def kernel(L0_attn_x, L0_attn_Wq, L0_attn_Wk, L0_attn_Wv, L0_attn_Wo,
           L0_ffn_W_up, L0_ffn_gate_w, L0_ffn_expert_bias, L0_ffn_lora_A,
           L0_ffn_lora_B):
    x = np.asarray(L0_attn_x, np.float32)
    maps = make_inputs(x, np.asarray(L0_attn_Wq, np.float32),
                       np.asarray(L0_attn_Wk, np.float32),
                       np.asarray(L0_attn_Wv, np.float32),
                       np.asarray(L0_attn_Wo, np.float32),
                       np.asarray(L0_ffn_W_up, np.float32),
                       np.asarray(L0_ffn_gate_w, np.float32),
                       np.asarray(L0_ffn_expert_bias, np.float32),
                       np.asarray(L0_ffn_lora_A, np.float32),
                       np.asarray(L0_ffn_lora_B, np.float32))
    nc, _ = build_program()
    res = run_bass_kernel_spmd(nc, maps, core_ids=list(range(8)))
    full = np.empty((B, S, D), np.float32)
    for c in range(8):
        b, j = divmod(c, 4)
        full[b, 1024 * j:1024 * (j + 1), :] = res.results[c]["out"]
    return full


# version 11
# speedup vs baseline: 1.3585x; 1.0094x over previous
import math
import numpy as np
from contextlib import ExitStack

import concourse.bass as bass
import concourse.mybir as mybir
from concourse.bass_utils import run_bass_kernel_spmd

F32 = mybir.dt.float32
BF16 = mybir.dt.bfloat16
AF = mybir.ActivationFunctionType
ALU = mybir.AluOpType
AX = mybir.AxisListType.X

B, S, D = 2, 4096, 4096
NH, DH = 32, 128
FC = D // 128
T_OWN = 1024
T_SLAB = 1536
NT_OWN = 8
NT_SLAB = 12
SEGS = 4
SCALE = 1.0 / math.sqrt(DH)
EPS = 1e-5

RA = 0
RB = 65536
RC = 131072
RE = 147456
RF = 163840
ARENA_BYTES = 180224


class Ctx:
    def __init__(self):
        self.cnt = {}

    def bump(self, name, k=1):
        self.cnt[name] = self.cnt.get(name, 0) + k
        return self.cnt[name]

    def val(self, name):
        return self.cnt.get(name, 0)


def build_program(tap=None):
    nc = bass.Bass("TRN2", target_bir_lowering=False)

    xslab = nc.dram_tensor("xslab", [T_SLAB, D], F32, kind="ExternalInput").ap()
    wq_t = nc.dram_tensor("wq_t", [FC, 128, FC, 128], F32, kind="ExternalInput").ap()
    wo_t = nc.dram_tensor("wo_t", [FC, 128, FC, 128], F32, kind="ExternalInput").ap()
    wup_t = nc.dram_tensor("wup_t", [FC, 128, FC, 128], F32, kind="ExternalInput").ap()
    wk_t = nc.dram_tensor("wk_t", [128, FC, 128], F32, kind="ExternalInput").ap()
    wv_t = nc.dram_tensor("wv_t", [128, FC, 128], F32, kind="ExternalInput").ap()
    sh_t = nc.dram_tensor("sh_t", [128, FC, 64], F32, kind="ExternalInput").ap()
    bflat = nc.dram_tensor("bflat", [32, D], F32, kind="ExternalInput").ap()
    ebias = nc.dram_tensor("ebias", [4, 1], F32, kind="ExternalInput").ap()
    eexp = nc.dram_tensor("eexp", [4, 32], F32, kind="ExternalInput").ap()
    identf = nc.dram_tensor("identf", [128, 128], F32, kind="ExternalInput").ap()
    ptmat = nc.dram_tensor("ptmat", [128, 128], F32, kind="ExternalInput").ap()
    cosT = nc.dram_tensor("cosT", [128, T_SLAB], F32, kind="ExternalInput").ap()
    sinT = nc.dram_tensor("sinT", [128, T_SLAB], F32, kind="ExternalInput").ap()
    alphaA = nc.dram_tensor("alphaA", [128, 2 * SEGS], F32, kind="ExternalInput").ap()
    alphaB = nc.dram_tensor("alphaB", [128, 2 * SEGS], F32, kind="ExternalInput").ap()
    maskb = nc.dram_tensor("maskb", [128, 2], F32, kind="ExternalInput").ap()

    out = nc.dram_tensor("out", [T_OWN, D], F32, kind="ExternalOutput").ap()
    xres = nc.dram_tensor("xres", [T_OWN, D], F32, kind="Internal").ap()
    tap_dram = [None]

    ctx = Ctx()
    st = ExitStack()
    e = st.enter_context

    with st:
        s_x = e(nc.semaphore("s_x"))
        s_w = e(nc.semaphore("s_w"))
        s_tr = e(nc.semaphore("s_tr"))
        s_mm = e(nc.semaphore("s_mm"))
        s_v = e(nc.semaphore("s_v"))
        s_s = e(nc.semaphore("s_s"))
        s_res = e(nc.semaphore("s_res"))
        s_out = e(nc.semaphore("s_out"))
        s_bar = e(nc.semaphore("s_bar"))
        s_g = e(nc.semaphore("s_g"))

        cos_sb = e(nc.sbuf_tensor("cos_sb", [128, T_SLAB], F32))
        sin_sb = e(nc.sbuf_tensor("sin_sb", [128, T_SLAB], F32))
        aA_sb = e(nc.sbuf_tensor("aA_sb", [128, 2 * SEGS], F32))
        aB_sb = e(nc.sbuf_tensor("aB_sb", [128, 2 * SEGS], F32))
        mb_sb = e(nc.sbuf_tensor("mb_sb", [128, 2], F32))
        idb_sb = e(nc.sbuf_tensor("idb_sb", [128, 128], BF16))
        idf_sb = e(nc.sbuf_tensor("idf_sb", [128, 128], F32))
        pt_sb = e(nc.sbuf_tensor("pt_sb", [128, 128], BF16))
        eb_sb = e(nc.sbuf_tensor("eb_sb", [36, 1], F32))
        ee_sb = e(nc.sbuf_tensor("ee_sb", [4, 32], BF16))
        sh_sb = e(nc.sbuf_tensor("sh_sb", [128, FC, 64], BF16))
        bfl_sb = e(nc.sbuf_tensor("bfl_sb", [32, D], BF16))
        krT_sb = e(nc.sbuf_tensor("krT_sb", [128, T_SLAB], BF16))
        vaug_sb = e(nc.sbuf_tensor("vaug_sb", [128, NT_SLAB, 132], BF16))
        st_sum = e(nc.sbuf_tensor("st_sum", [128, 8], F32))
        st_ssq = e(nc.sbuf_tensor("st_ssq", [128, 8], F32))
        st_mu = e(nc.sbuf_tensor("st_mu", [128, 8], F32))
        st_var = e(nc.sbuf_tensor("st_var", [128, 8], F32))
        st_rstd = e(nc.sbuf_tensor("st_rstd", [128, 8], F32))
        st_nb = e(nc.sbuf_tensor("st_nb", [128, 8], F32))
        st_nmu = e(nc.sbuf_tensor("st_nmu", [128, 8], F32))
        st_musq = e(nc.sbuf_tensor("st_musq", [128, 8], F32))
        st_var2 = e(nc.sbuf_tensor("st_var2", [128, 8], F32))
        st_rvar = e(nc.sbuf_tensor("st_rvar", [128, 8], F32))
        jnk_v = e(nc.sbuf_tensor("jnk_v", [128, 4], F32))
        jnk_s = e(nc.sbuf_tensor("jnk_s", [128, 4], F32))
        bl_r = e(nc.sbuf_tensor("bl_r", [128, 8], F32))

        arena = e(nc.sbuf_tensor("arena", [128, ARENA_BYTES // 2], BF16))

        def vb(off, *dims):
            n = 1
            for d_ in dims:
                n *= d_
            ap = arena[:, off // 2: off // 2 + n]
            if len(dims) == 2:
                ap = ap.rearrange("p (a b) -> p a b", a=dims[0], b=dims[1])
            return ap

        def vf(off, *dims):
            n = 1
            for d_ in dims:
                n *= d_
            ap = arena[:, off // 2: off // 2 + 2 * n].bitcast(F32)
            if len(dims) == 2:
                ap = ap.rearrange("p (a b) -> p a b", a=dims[0], b=dims[1])
            return ap

        ln1T = vb(RA, FC, T_OWN)
        attnT = vb(RA, NH, T_OWN)
        nat8 = vb(RA, NT_OWN, D)
        ffnT = vb(RA, NH, T_OWN)
        wk_sb = vb(RB, FC, 128)
        wv_sb = vb(RB + 8192, FC, 128)
        xload2 = [vf(RB + 16384, D), vf(RB + 32768, D)]
        qrT = vb(RB, NH, T_OWN)
        projT = vb(RB, NH, T_OWN)
        ln2T = vb(RB, FC, T_OWN)
        natG = vb(RB, NT_OWN, D)
        lnb = [vb(RC, D), vb(RC + 8192, D)]
        wbuf = [vb(RC, FC, 128), vb(RC + 8192, FC, 128)]
        anat = [vb(RC, D), vb(RC + 8192, D)]
        halotmp = [vb(RE, FC, 128), vb(RE + 8192, FC, 128)]
        rstage = vb(RE, T_OWN)
        sigAB = [vb(RE, 2 * T_OWN), vb(RE + 4096, 2 * T_OWN)]
        xnf = vf(RE, D)
        lgT = vf(RE, T_OWN)
        ohT_sb = vb(RE + 4096, T_OWN)
        ex_sb = vf(RE + 6144, T_OWN)
        zmT = vb(RE + 10240, T_OWN)
        mcol = vf(RE + 12288, NT_OWN)
        fflg = vf(RE + 12320, 4)
        ntmp = vf(RE + 12336, 4)
        ohn = vf(RE + 12352, NT_OWN, 4)
        lgN = vf(RE + 12480, NT_OWN, 4)
        fflg8 = vf(RE + 12608, NT_OWN, 4)
        ntA = vf(RE + 12736, NT_OWN)
        ntB = vf(RE + 12768, NT_OWN)
        ntC = vf(RE + 12800, NT_OWN)
        bl_t = vf(RE + 8192, 256)
        dumpA = vb(RF, D)
        vstage = [vb(RF + 8192, 128), vb(RF + 8448, 128)]
        kstage = [vb(RF + 8704, 128), vb(RF + 8960, 128)]
        ropesc = vf(RF + 9216, T_OWN)
        t1f = vf(RF, D)

        block = e(nc.Block())
        ENGS = ("tensor", "vector", "scalar", "sync", "gpsimd")

        def barrier():
            tgt = ctx.bump("s_bar", len(ENGS))
            for en in ENGS:
                def _f(eng, tgt=tgt):
                    eng.sem_inc(s_bar, 1)
                    eng.wait_ge(s_bar, tgt)
                getattr(block, en)(_f)

        def finish_tap(src_ap, shape, dtype):
            tap_dram[0] = nc.dram_tensor("tap", shape, dtype, kind="ExternalOutput").ap()
            barrier()

            def _f(sy):
                sy.dma_start(tap_dram[0], src_ap).then_inc(s_out, 16)
                sy.wait_ge(s_out, ctx.bump("s_out", 16))
            block.sync(_f)

        def ts1(eng, o, i0, s1, op):
            return eng.tensor_scalar(o, i0, s1, 0.0, op, ALU.add)

        def vsync(v):
            v.tensor_copy(jnk_v[:, 2:3], jnk_v[:, 2:3]).then_inc(s_v, 1)
            v.wait_ge(s_v, ctx.bump("s_v"))

        def ssync(s):
            s.activation(jnk_s[:, 0:1], jnk_s[:, 0:1], AF.Identity).then_inc(s_s, 1)
            s.wait_ge(s_s, ctx.bump("s_s"))

        def p0_sync(sy):
            for dst, src in ((cos_sb[:], cosT), (sin_sb[:], sinT), (aA_sb[:], alphaA),
                             (aB_sb[:], alphaB), (mb_sb[:], maskb), (idf_sb[:], identf),
                             (eb_sb[32:36, :], ebias)):
                sy.dma_start(dst, src).then_inc(s_x, 16)
                ctx.bump("s_x", 16)
        block.sync(p0_sync)

        def p0_g(g):
            for dst, src in ((idb_sb[:], identf), (pt_sb[:], ptmat), (ee_sb[:], eexp),
                             (wk_sb, wk_t), (wv_sb, wv_t), (sh_sb[:], sh_t),
                             (bfl_sb[:], bflat)):
                g.dma_start(dst, src).then_inc(s_w, 16)
                ctx.bump("s_w", 16)
            g.memset(vaug_sb[:, :, 128:129], 1.0)
            g.memset(jnk_v[:], 0.0)
            g.memset(jnk_s[:], 0.0).then_inc(s_g, 1)
            ctx.bump("s_g", 1)
        block.gpsimd(p0_g)
        P0_W = ctx.val("s_w")
        P0_G = ctx.val("s_g")

        def ln_tile(src_ap, col, dst_ap, dump_ap, pre_s=None, pre_v=None, pre_s3=None):
            c = slice(col, col + 1)

            def _sc(s):
                if pre_s:
                    for sm, v in pre_s:
                        s.wait_ge(sm, v)
                s.activation(dump_ap, src_ap, AF.Square,
                             accum_out=st_ssq[:, c]).then_inc(s_s, 1)
                s.wait_ge(s_s, ctx.bump("s_s"))
                s.activation(jnk_s[:, 0:1], st_ssq[:, c], AF.Identity).then_inc(s_s, 1)
            block.scalar(_sc)
            ev_ssq = ctx.bump("s_s")

            def _v(v, ev_ssq=ev_ssq):
                if pre_v:
                    for sm, vv in pre_v:
                        v.wait_ge(sm, vv)
                v.reduce_sum(st_sum[:, c], src_ap, axis=AX)
                vsync(v)
                ts1(v, st_mu[:, c], st_sum[:, c], 1.0 / D, ALU.mult)
                ts1(v, st_nmu[:, c], st_sum[:, c], -1.0 / D, ALU.mult)
                v.wait_ge(s_s, ev_ssq)
                v.tensor_scalar(st_var[:, c], st_ssq[:, c], 1.0 / D, EPS,
                                ALU.mult, ALU.add)
                v.tensor_tensor(st_musq[:, c], st_mu[:, c], st_mu[:, c], ALU.mult)
                vsync(v)
                v.scalar_tensor_tensor(st_var2[:, c], st_musq[:, c], -1.0,
                                       st_var[:, c], ALU.mult, ALU.add)
                vsync(v)
                v.reciprocal(st_rvar[:, c], st_var2[:, c])
                v.tensor_copy(jnk_v[:, 0:1], st_rvar[:, c]).then_inc(s_v, 1)
            block.vector(_v)
            ev_rv = ctx.bump("s_v")

            def _sc3(s, ev=ev_rv):
                s.wait_ge(s_v, ev)
                if pre_s3:
                    for sm, vv in pre_s3:
                        s.wait_ge(sm, vv)
                s.activation(st_rstd[:, c], st_rvar[:, c], AF.Sqrt)
                ssync(s)
                s.activation(st_nb[:, c], st_rstd[:, c], AF.Identity,
                             scale=st_nmu[:, c])
                ssync(s)
                s.activation(dst_ap, src_ap, AF.Identity,
                             bias=st_nb[:, c], scale=st_rstd[:, c])
                s.activation(jnk_s[:, 3:4], st_nb[:, c], AF.Identity).then_inc(s_s, 1)
            block.scalar(_sc3)
            return ctx.bump("s_s")

        def _pa_v(v):
            v.wait_ge(s_g, P0_G)
        block.vector(_pa_v)

        def _pa_s(s2):
            s2.wait_ge(s_g, P0_G)
        block.scalar(_pa_s)

        with ExitStack() as ps_st:
            pe = ps_st.enter_context
            ps_k = pe(nc.psum_tensor("ps_k", [128, 128], F32))
            ps_vv = pe(nc.psum_tensor("ps_vv", [128, 128], F32))
            ps_r = pe(nc.psum_tensor("ps_r", [128, 128], F32))
            ps_tp = pe(nc.psum_tensor("ps_tp", [128, 128], BF16))

            HALO = {0: 0, 1: 1, 10: 0, 11: 1}
            halo_kv_done = {}
            norm_prev = 0
            norm_prev2 = [0, 0]
            tr_prev = {}
            kevac = 0
            vevac = 0
            rope_prev = 0
            vcp_prev = 0
            ln_tr_ev = [0] * NT_SLAB
            for i in range(NT_SLAB):
                slot = i % 2

                def _ld(g, i=i, np_=norm_prev2[i % 2]):
                    if np_:
                        g.wait_ge(s_s, np_)
                    g.dma_start(xload2[i % 2],
                                 xslab[128 * i:128 * (i + 1), :]).then_inc(s_x, 16)
                block.gpsimd(_ld)
                ev_x = ctx.bump("s_x", 16)
                xload = xload2[i % 2]

                pre3 = [(s_tr, tr_prev[i - 2])] if i >= 2 else None
                ev_norm = ln_tile(xload, i % 4, lnb[slot], dumpA,
                                  pre_s=[(s_x, ev_x)], pre_v=[(s_x, ev_x)], pre_s3=pre3)
                norm_prev = ev_norm
                norm_prev2[i % 2] = ev_norm

                if i in HALO:
                    hs = HALO[i]
                    dest = halotmp[hs]
                else:
                    dest = ln1T[:, :, 128 * (i - 2):128 * (i - 1)]
                    hs = None

                def _tr(sy, i=i, slot=slot, ev=ev_norm, hs=hs, dest=dest):
                    sy.wait_ge(s_s, ev)
                    if hs is not None and hs in halo_kv_done:
                        sy.wait_ge(s_mm, halo_kv_done[hs])
                    sy.dma_start_transpose(dest, lnb[slot]).then_inc(s_tr, 16)
                block.sync(_tr)
                ln_tr_ev[i] = ctx.bump("s_tr", 16)
                tr_prev[i] = ln_tr_ev[i]

                def _t(t, i=i, dest=dest, ev_tr=ln_tr_ev[i], ke=kevac, ve=vevac):
                    if i == 0:
                        t.wait_ge(s_w, P0_W)
                        t.wait_ge(s_g, P0_G)
                    t.wait_ge(s_tr, ev_tr)
                    if ke:
                        t.wait_ge(s_s, ke)
                    for fc in range(FC):
                        m = t.matmul(ps_k[:], wk_sb[:, fc, :], dest[:, fc, :],
                                     start=(fc == 0), stop=(fc == FC - 1))
                    m.then_inc(s_mm, 1)
                    if ve:
                        t.wait_ge(s_s, ve)
                    for fc in range(FC):
                        m = t.matmul(ps_vv[:], wv_sb[:, fc, :], dest[:, fc, :],
                                     start=(fc == 0), stop=(fc == FC - 1))
                    m.then_inc(s_mm, 1)
                block.tensor(_t)
                ev_k = ctx.bump("s_mm")
                ev_vmm = ctx.bump("s_mm")
                if i in HALO:
                    halo_kv_done[HALO[i]] = ev_vmm

                def _s(s, i=i, slot=slot, evk=ev_k, evv=ev_vmm, rp=rope_prev, vc=vcp_prev):
                    s.wait_ge(s_mm, evk)
                    if rp:
                        s.wait_ge(s_v, rp)
                    s.activation(kstage[slot], ps_k[:], AF.Identity)
                    s.activation(jnk_s[:, 1:2], kstage[slot][:, 0:1],
                                 AF.Identity).then_inc(s_s, 1)
                    s.wait_ge(s_mm, evv)
                    if vc:
                        s.wait_ge(s_v, vc)
                    s.activation(vstage[slot], ps_vv[:], AF.Identity)
                    s.activation(jnk_s[:, 2:3], vstage[slot][:, 0:1],
                                 AF.Identity).then_inc(s_s, 1)
                block.scalar(_s)
                kevac = ctx.bump("s_s")
                vevac = ctx.bump("s_s")

                def _t2(t, slot=slot, ke=kevac, ve=vevac, rp=rope_prev, vc=vcp_prev):
                    t.wait_ge(s_s, ke)
                    if rp:
                        t.wait_ge(s_v, rp)
                    t.matmul(ps_r[:], pt_sb[:], kstage[slot],
                             start=True, stop=True).then_inc(s_mm, 1)
                    t.wait_ge(s_s, ve)
                    if vc:
                        t.wait_ge(s_v, vc)
                    t.transpose(ps_tp[:], vstage[slot], idb_sb[:]).then_inc(s_mm, 1)
                block.tensor(_t2)
                ev_rot = ctx.bump("s_mm")
                ev_vtp = ctx.bump("s_mm")

                def _v(v, i=i, slot=slot, er=ev_rot, et=ev_vtp):
                    cols = slice(128 * i, 128 * (i + 1))
                    v.wait_ge(s_mm, er)
                    v.tensor_tensor(ropesc[:, 0:128], kstage[slot], cos_sb[:, cols], ALU.mult)
                    v.tensor_tensor(ropesc[:, 128:256], ps_r[:], sin_sb[:, cols], ALU.mult)
                    vsync(v)
                    v.tensor_tensor(krT_sb[:, cols], ropesc[:, 0:128], ropesc[:, 128:256],
                                    ALU.add)
                    v.tensor_copy(jnk_v[:, 3:4], krT_sb[:, cols][:, 0:1]).then_inc(s_v, 1)
                    v.wait_ge(s_mm, et)
                    v.tensor_copy(vaug_sb[:, i, 0:128], ps_tp[:])
                    v.tensor_copy(jnk_v[:, 3:4], vaug_sb[:, i, 0:1]).then_inc(s_v, 1)
                block.vector(_v)
                rope_prev = ctx.bump("s_v")
                vcp_prev = ctx.bump("s_v")

            if tap == "kst":
                kdbg = vb(RF + 13312, 1024)

                def _vdbg(v):
                    kf = kdbg.bitcast(F32)
                    v.tensor_copy(kf[:, 0:1], st_sum[:, 3:4])
                    v.tensor_copy(kf[:, 1:2], st_ssq[:, 3:4])
                    v.tensor_copy(kf[:, 2:3], st_mu[:, 3:4])
                    v.tensor_copy(kf[:, 3:4], st_nmu[:, 3:4])
                    v.tensor_copy(kf[:, 4:5], st_var[:, 3:4])
                    v.tensor_copy(kf[:, 5:6], st_musq[:, 3:4])
                    v.tensor_copy(kf[:, 6:7], st_var2[:, 3:4])
                    v.tensor_copy(kf[:, 7:8], st_rvar[:, 3:4])
                    v.tensor_copy(kf[:, 8:9], st_rstd[:, 3:4])
                    v.tensor_copy(kf[:, 9:10], st_nb[:, 3:4])
                    v.tensor_copy(kdbg[:, 512:1024], lnb[1][:, 0:512]).then_inc(s_v, 1)
                block.vector(_vdbg)
                ctx.bump("s_v")
                finish_tap(kdbg, [128, 1024], BF16)
                return nc, tap_dram[0]
            if tap == "ln1T":
                finish_tap(ln1T, [128, FC, T_OWN], BF16)
                return nc, tap_dram[0]
            if tap == "kv":
                finish_tap(krT_sb[:], [128, T_SLAB], BF16)
                return nc, tap_dram[0]
            if tap == "vaug":
                finish_tap(vaug_sb[:], [128, NT_SLAB, 132], BF16)
                return nc, tap_dram[0]
            barrier()

        with ExitStack() as ps_st:
            pe = ps_st.enter_context
            ps_q = [pe(nc.psum_tensor(f"ps_q{i}", [128, 512], F32)) for i in range(2)]
            ps_r2 = [pe(nc.psum_tensor(f"ps_r2{i}", [128, 512], F32)) for i in range(2)]

            wbuf_free = [0, 0]
            psq_free = [0, 0]
            psr_free = [0, 0]
            qst_free = [0, 0]
            for ot in range(NH):
                slot = ot % 2

                def _g(g, ot=ot, slot=slot, w=wbuf_free[slot]):
                    if w:
                        g.wait_ge(s_mm, w)
                    g.dma_start(wbuf[slot], wq_t[ot]).then_inc(s_w, 16)
                block.gpsimd(_g)
                ev_w = ctx.bump("s_w", 16)

                last_mm = None
                for th in range(2):
                    def _t(t, th=th, slot=slot, ev_w=ev_w, pf=psq_free[th]):
                        t.wait_ge(s_w, ev_w)
                        if pf:
                            t.wait_ge(s_s, pf)
                        for fc in range(FC):
                            m = t.matmul(ps_q[th][:], wbuf[slot][:, fc, :],
                                         ln1T[:, fc, 512 * th:512 * (th + 1)],
                                         start=(fc == 0), stop=(fc == FC - 1))
                        m.then_inc(s_mm, 1)
                    block.tensor(_t)
                    ev_mm = ctx.bump("s_mm")
                    last_mm = ev_mm

                    def _s(s, th=th, ev=ev_mm, qf=qst_free[th]):
                        s.wait_ge(s_mm, ev)
                        if qf:
                            s.wait_ge(s_v, qf)
                        s.activation(rstage[:, 512 * th:512 * (th + 1)], ps_q[th][:],
                                     AF.Identity)
                        s.activation(jnk_s[:, 1:2], rstage[:, 512 * th:512 * th + 1],
                                     AF.Identity).then_inc(s_s, 1)
                    block.scalar(_s)
                    ev_ev = ctx.bump("s_s")
                    psq_free[th] = ev_ev

                    def _t2(t, th=th, ev=ev_ev, rf=psr_free[th]):
                        t.wait_ge(s_s, ev)
                        if rf:
                            t.wait_ge(s_v, rf)
                        t.matmul(ps_r2[th][:], pt_sb[:], rstage[:, 512 * th:512 * (th + 1)],
                                 start=True, stop=True).then_inc(s_mm, 1)
                    block.tensor(_t2)
                    ev_rot = ctx.bump("s_mm")

                    def _v(v, ot=ot, th=th, ev=ev_rot):
                        v.wait_ge(s_mm, ev)
                        ccols = slice(256 + 512 * th, 256 + 512 * (th + 1))
                        v.tensor_tensor(ropesc[:, 0:512], rstage[:, 512 * th:512 * (th + 1)],
                                        cos_sb[:, ccols], ALU.mult)
                        v.tensor_tensor(ropesc[:, 512:1024], ps_r2[th][:],
                                        sin_sb[:, ccols], ALU.mult)
                        v.tensor_tensor(qrT[:, ot, 512 * th:512 * (th + 1)],
                                        ropesc[:, 0:512], ropesc[:, 512:1024],
                                        ALU.add)
                        v.tensor_copy(jnk_v[:, 3:4],
                                      qrT[:, ot, 512 * th:512 * th + 1]).then_inc(s_v, 1)
                    block.vector(_v)
                    ev_rope = ctx.bump("s_v")
                    psr_free[th] = ev_rope
                    qst_free[th] = ev_rope
                wbuf_free[slot] = last_mm

            if tap == "qrT":
                finish_tap(qrT, [128, NH, T_OWN], BF16)
                return nc, tap_dram[0]
            barrier()

        with ExitStack() as ps_st:
            pe = ps_st.enter_context
            ps_scA = pe(nc.psum_tensor("ps_scA", [128, 1024], F32))
            ps_scB = pe(nc.psum_tensor("ps_scB", [128, 1024], F32))
            ps_wA = pe(nc.psum_tensor("ps_wA", [128, 512], F32))
            ps_wB = pe(nc.psum_tensor("ps_wB", [128, 512], F32))

            sigA_done = 0
            sigB_done = 0
            blend_done = 0
            anat_guard = 0
            anat_tr = {}
            heads = [(s_i, h) for s_i in range(SEGS) for h in range(NH)]

            def emit_scores(idx):
                s_i, h = heads[idx]
                qcols = slice(256 * s_i, 256 * (s_i + 1))

                def _t(t, s_i=s_i, h=h, qcols=qcols, sa=sigA_done, sb=sigB_done):
                    if sa:
                        t.wait_ge(s_s, sa)
                    for j in range(4):
                        t.matmul(ps_scA[:, 256 * j:256 * (j + 1)],
                                 krT_sb[:, 128 * (2 * s_i + j):128 * (2 * s_i + j + 1)],
                                 qrT[:, h, qcols], start=True,
                                 stop=True).then_inc(s_mm, 1) if j == 3 else \
                        t.matmul(ps_scA[:, 256 * j:256 * (j + 1)],
                                 krT_sb[:, 128 * (2 * s_i + j):128 * (2 * s_i + j + 1)],
                                 qrT[:, h, qcols], start=True, stop=True)
                    if sb:
                        t.wait_ge(s_s, sb)
                    for j in range(4):
                        m = t.matmul(ps_scB[:, 256 * j:256 * (j + 1)],
                                     krT_sb[:, 128 * (2 * s_i + 2 + j):128 * (2 * s_i + 3 + j)],
                                     qrT[:, h, qcols], start=True, stop=True)
                    m.then_inc(s_mm, 1)
                block.tensor(_t)
                evA = ctx.bump("s_mm")
                evB = ctx.bump("s_mm")
                return evA, evB

            def emit_sig(idx, evA, evB):
                nonlocal sigA_done, sigB_done
                s_i, h = heads[idx]

                sl = sigAB[idx % 2]
                sigA = sl[:, 0:1024]
                sigB = sl[:, 1024:2048]

                def _s(s, s_i=s_i, evA=evA, evB=evB, sigA=sigA, sigB=sigB):
                    s.wait_ge(s_mm, evA)
                    s.activation(sigA, ps_scA[:], AF.Sigmoid,
                                 scale=SCALE).then_inc(s_s, 1)
                    s.wait_ge(s_mm, evB)
                    if s_i == SEGS - 1:
                        s.activation(sigB[:, 0:512], ps_scB[:, 0:512],
                                     AF.Sigmoid, scale=SCALE)
                        s.activation(sigB[:, 512:768], ps_scB[:, 512:768], AF.Sigmoid,
                                     scale=SCALE, bias=mb_sb[:, 0:1])
                        s.activation(sigB[:, 768:1024], ps_scB[:, 768:1024], AF.Sigmoid,
                                     scale=SCALE, bias=mb_sb[:, 1:2])
                    else:
                        s.activation(sigB, ps_scB[:], AF.Sigmoid, scale=SCALE)
                    s.activation(jnk_s[:, 1:2], sigB[:, 1023:1024],
                                 AF.Sigmoid).then_inc(s_s, 1)
                block.scalar(_s)
                sigA_done = ctx.bump("s_s")
                sigB_done = ctx.bump("s_s")

            def emit_wout(idx, sd):
                s_i, h = heads[idx]
                sl = sigAB[idx % 2]
                sigA = sl[:, 0:1024]
                sigB = sl[:, 1024:2048]

                def _t2(t, s_i=s_i, sd=sd, bd=blend_done, sigA=sigA, sigB=sigB):
                    t.wait_ge(s_s, sd)
                    if bd:
                        t.wait_ge(s_v, bd)
                    for qc in range(2):
                        off = 132 * qc
                        for j in range(4):
                            t.matmul(ps_wA[:, off:off + 129],
                                     sigA[:, 256 * j + 128 * qc:256 * j + 128 * (qc + 1)],
                                     vaug_sb[:, 2 * s_i + j, 0:129],
                                     start=(j == 0), stop=(j == 3))
                        for j in range(4):
                            m = t.matmul(ps_wB[:, off:off + 129],
                                         sigB[:, 256 * j + 128 * qc:256 * j + 128 * (qc + 1)],
                                         vaug_sb[:, 2 * s_i + 2 + j, 0:129],
                                         start=(j == 0), stop=(j == 3))
                    m.then_inc(s_mm, 1)
                block.tensor(_t2)
                return ctx.bump("s_mm")

            def emit_blend(idx, ev_w2):
                nonlocal blend_done
                s_i, h = heads[idx]

                def _v(v, s_i=s_i, h=h, ev=ev_w2, ag=anat_guard):
                    v.wait_ge(s_mm, ev)
                    if ag and h == 0:
                        v.wait_ge(s_tr, ag)
                    sA2 = ps_wA[:, 0:264].rearrange("p (a b) -> p a b", a=2, b=132)[
                        :, :, 128:129].rearrange("p a b -> p (a b)")
                    sB2 = ps_wB[:, 0:264].rearrange("p (a b) -> p a b", a=2, b=132)[
                        :, :, 128:129].rearrange("p a b -> p (a b)")
                    v.reciprocal(bl_r[:, 0:2], sA2)
                    v.reciprocal(bl_r[:, 2:4], sB2)
                    vsync(v)
                    acols = slice(2 * s_i, 2 * s_i + 2)
                    v.tensor_tensor(bl_r[:, 4:6], bl_r[:, 0:2], aA_sb[:, acols], ALU.mult)
                    v.tensor_tensor(bl_r[:, 6:8], bl_r[:, 2:4], aB_sb[:, acols], ALU.mult)
                    vsync(v)
                    for qc in range(2):
                        ts1(v, bl_t[:, 128 * qc:128 * (qc + 1)],
                            ps_wA[:, 132 * qc:132 * qc + 128],
                            bl_r[:, 4 + qc:5 + qc], ALU.mult)
                    for qc in range(2):
                        v.scalar_tensor_tensor(
                            anat[qc][:, 128 * h:128 * (h + 1)],
                            ps_wB[:, 132 * qc:132 * qc + 128],
                            bl_r[:, 6 + qc:7 + qc],
                            bl_t[:, 128 * qc:128 * (qc + 1)],
                            ALU.mult, ALU.add)
                    v.tensor_copy(jnk_v[:, 3:4],
                                  anat[1][:, 128 * h:128 * h + 1]).then_inc(s_v, 1)
                block.vector(_v)
                blend_done = ctx.bump("s_v")

            pend = None
            for idx in range(len(heads)):
                evA, evB = emit_scores(idx)
                emit_sig(idx, evA, evB)
                my_sig = sigB_done
                if pend is not None:
                    ev_w2 = emit_wout(pend[0], pend[1])
                    emit_blend(pend[0], ev_w2)
                pend = (idx, my_sig)
                s_i, h = heads[idx]
                if h == NH - 1 and pend is not None:
                    ev_w2 = emit_wout(pend[0], pend[1])
                    emit_blend(pend[0], ev_w2)
                    pend = None
                    for qc in range(2):
                        ti = 2 * s_i + qc

                        def _tr(sy, ti=ti, qc=qc, bd=blend_done):
                            sy.wait_ge(s_v, bd)
                            sy.dma_start_transpose(
                                attnT[:, :, 128 * ti:128 * (ti + 1)],
                                anat[qc]).then_inc(s_tr, 16)
                        block.sync(_tr)
                        anat_tr[ti] = ctx.bump("s_tr", 16)
                    anat_guard = anat_tr[2 * s_i + 1]

            if tap == "wlast":
                wdbg = vf(RE + 8192, T_OWN)

                def _vdbg(v, bd=blend_done):
                    v.wait_ge(s_v, bd)
                    v.tensor_copy(wdbg[:, 0:264], ps_wA.bitcast(F32)[:, 0:264])
                    v.tensor_copy(wdbg[:, 264:528], ps_wB.bitcast(F32)[:, 0:264]).then_inc(s_v, 1)
                block.vector(_vdbg)
                ctx.bump("s_v")
                finish_tap(wdbg, [128, T_OWN], F32)
                return nc, tap_dram[0]
            if tap == "sclast":
                wdbg2 = vf(RE + 8192, T_OWN)

                def _vdbg2(v, ev=ev_sc):
                    v.wait_ge(s_mm, ev)
                    v.tensor_copy(wdbg2[:], ps_scB[:]).then_inc(s_v, 1)
                block.vector(_vdbg2)
                ctx.bump("s_v")
                finish_tap(wdbg2, [128, T_OWN], F32)
                return nc, tap_dram[0]
            if tap == "siglast":
                finish_tap(sigB, [128, T_OWN], BF16)
                return nc, tap_dram[0]
            if tap == "attnT":
                finish_tap(attnT, [128, NH, T_OWN], BF16)
                return nc, tap_dram[0]
            attn_tr_last = anat_tr[2 * SEGS - 1]
            barrier()

        def big_proj(wdram, rhs_fn, dst, delta_fn=None, first_wait=None):
            with ExitStack() as ps_st2:
                pe2 = ps_st2.enter_context
                ps_p = [pe2(nc.psum_tensor(f"ps_p{ctx.bump('psn')}", [128, 512], F32))
                        for _ in range(2)]
                S0 = ctx.val("s_s")
                M0 = ctx.val("s_mm")
                W0 = ctx.val("s_w")

                def evac_ev(ot, th):
                    return S0 + 2 * ot + th + 1

                def mm_ev(ot, th):
                    return M0 + 2 * ot + th + 1

                def _gall(g):
                    for ot in range(FC):
                        if ot >= 2:
                            g.wait_ge(s_mm, mm_ev(ot - 2, 1))
                        g.dma_start(wbuf[ot % 2], wdram[ot]).then_inc(s_w, 16)
                        ctx.bump("s_w", 16)
                block.gpsimd(_gall)

                def _tall(t):
                    for ot in range(FC):
                        for th in range(2):
                            if ot == 0 and th == 0 and first_wait:
                                for sm, vv in first_wait:
                                    t.wait_ge(sm, vv)
                            t.wait_ge(s_w, W0 + 16 * (ot + 1))
                            if ot >= 1:
                                t.wait_ge(s_s, evac_ev(ot - 1, th))
                            first = True
                            if delta_fn is not None:
                                delta_fn(t, ps_p[th], ot, th)
                                first = False
                            for fc in range(FC):
                                m = t.matmul(ps_p[th][:], wbuf[ot % 2][:, fc, :],
                                             rhs_fn(fc, th),
                                             start=(first and fc == 0),
                                             stop=(fc == FC - 1))
                            m.then_inc(s_mm, 1)
                            ctx.bump("s_mm")
                block.tensor(_tall)

                def _sall(s2):
                    for ot in range(FC):
                        for th in range(2):
                            s2.wait_ge(s_mm, mm_ev(ot, th))
                            s2.activation(dst[:, ot, 512 * th:512 * (th + 1)],
                                          ps_p[th][:], AF.Identity)
                            s2.activation(jnk_s[:, 1:2],
                                          dst[:, ot, 512 * th:512 * th + 1],
                                          AF.Identity).then_inc(s_s, 1)
                            ctx.bump("s_s")
                block.scalar(_sall)
                return ctx.val("s_s")

        big_proj(wo_t,
                 lambda fc, th: attnT[:, fc, 512 * th:512 * (th + 1)],
                 projT, first_wait=[(s_tr, attn_tr_last)])
        if tap == "projT":
            finish_tap(projT, [128, NH, T_OWN], BF16)
            return nc, tap_dram[0]
        barrier()

        def resid_phase(srcT, natv, is_final):
            ptr = 0
            for ot in range(FC):
                def _tr(sy, ot=ot):
                    sy.dma_start_transpose(natv[:, :, 128 * ot:128 * (ot + 1)],
                                          srcT[:, ot, :]).then_inc(s_tr, 16)
                block.sync(_tr)
                ptr = ctx.bump("s_tr", 16)

            xn_free = None
            prev_add = 0
            ln2_tr_local = {}
            dumpR = vb(RC + 8192, D) if not is_final else vb(RC, D)
            for i in range(NT_OWN):
                slot = 0

                def _ld(sy, i=i, ptr=ptr, xe=xn_free):
                    if i == 0:
                        sy.wait_ge(s_tr, ptr)
                        if is_final:
                            sy.wait_ge(s_res, ctx.val("s_res"))
                    if xe is not None:
                        if is_final:
                            sy.wait_ge(s_out, xe)
                        else:
                            sy.wait_ge(s_res, xe[0])
                            sy.wait_ge(s_s, xe[1])
                    src = (xres[128 * i:128 * (i + 1), :] if is_final
                           else xslab[256 + 128 * i:256 + 128 * (i + 1), :])
                    sy.dma_start(xnf, src).then_inc(s_x, 16)
                block.gpsimd(_ld)
                ev_x = ctx.bump("s_x", 16)

                pre = [(s_tr, ptr)] if i == 0 else None
                pre3 = [(s_v, prev_add)] if prev_add else None
                ev_t1 = ln_tile(natv[:, i, :], i % 4, t1f, dumpR,
                                pre_s=pre, pre_v=pre, pre_s3=pre3)

                def _v3(v, ev_t1=ev_t1, ev_x=ev_x):
                    v.wait_ge(s_s, ev_t1)
                    v.wait_ge(s_x, ev_x)
                    v.tensor_tensor(xnf, xnf, t1f, ALU.add)
                    v.tensor_copy(jnk_v[:, 3:4], xnf[:, 0:1]).then_inc(s_v, 1)
                block.vector(_v3)
                ev_xn = ctx.bump("s_v")
                prev_add = ev_xn

                if is_final:
                    def _st(g, i=i, ev=ev_xn):
                        g.wait_ge(s_v, ev)
                        g.dma_start(out[128 * i:128 * (i + 1), :],
                                    xnf).then_inc(s_out, 16)
                    block.gpsimd(_st)
                    xn_free = ctx.bump("s_out", 16)
                else:
                    def _st(g, i=i, ev=ev_xn):
                        g.wait_ge(s_v, ev)
                        g.dma_start(xres[128 * i:128 * (i + 1), :],
                                    xnf).then_inc(s_res, 16)
                    block.gpsimd(_st)
                    ev_res = ctx.bump("s_res", 16)

                    pre3b = [(s_tr, ln2_tr_local[i - 1])] if i >= 1 else None
                    ev_n2 = ln_tile(xnf, 4 + i % 4, lnb[slot], dumpR,
                                    pre_s=[(s_v, ev_xn)], pre_v=[(s_v, ev_xn)],
                                    pre_s3=pre3b)

                    def _tr2(sy, i=i, slot=slot, ev=ev_n2):
                        sy.wait_ge(s_s, ev)
                        sy.dma_start_transpose(ln2T[:, :, 128 * i:128 * (i + 1)],
                                              lnb[slot]).then_inc(s_tr, 16)
                    block.sync(_tr2)
                    ln2_tr_local[i] = ctx.bump("s_tr", 16)
                    xn_free = (ev_res, ev_n2)
            return ln2_tr_local.get(NT_OWN - 1, 0)

        ln2_tr_last = resid_phase(projT, nat8, is_final=False)
        if tap == "ln2T":
            finish_tap(ln2T, [128, FC, T_OWN], BF16)
            return nc, tap_dram[0]
        barrier()

        with ExitStack() as ps_st:
            pe = ps_st.enter_context
            ps_sh = pe(nc.psum_tensor("ps_sh", [64, T_OWN], F32))
            ps_tpf = pe(nc.psum_tensor("ps_tpf", [128, 128], F32))
            ps_oh = pe(nc.psum_tensor("ps_oh", [4, T_OWN], F32))
            ps_ex = pe(nc.psum_tensor("ps_ex", [32, T_OWN], F32))

            def _gms(g):
                g.memset(lgT, 0.0).then_inc(s_g, 1)
            block.gpsimd(_gms)
            ev_ms = ctx.bump("s_g", 1)

            def _tf1(t):
                t.wait_ge(s_tr, ln2_tr_last)
                for th in range(2):
                    for fc in range(FC):
                        m = t.matmul(ps_sh[:, 512 * th:512 * (th + 1)], sh_sb[:, fc, :],
                                     ln2T[:, fc, 512 * th:512 * (th + 1)],
                                     start=(fc == 0), stop=(fc == FC - 1))
                m.then_inc(s_mm, 1)
            block.tensor(_tf1)
            ev_sh = ctx.bump("s_mm")

            def _vf2(v, ev=ev_sh, evm=ev_ms):
                v.wait_ge(s_mm, ev)
                v.wait_ge(s_g, evm)
                ts1(v, lgT[32:36, :], ps_sh[32:36, :], eb_sb[32:36, :], ALU.add)
                v.tensor_copy(jnk_v[0:4, 3:4], lgT[32:36, 0:1]).then_inc(s_v, 1)
            block.vector(_vf2)
            ev_lg = ctx.bump("s_v")

            cp_ev = 0
            for j in range(NT_OWN):
                def _t(t, j=j, ev=ev_lg, ce=cp_ev):
                    if j == 0:
                        t.wait_ge(s_v, ev)
                    if ce:
                        t.wait_ge(s_v, ce)
                    t.transpose(ps_tpf[:], lgT[:, 128 * j:128 * (j + 1)],
                                idf_sb[:]).then_inc(s_mm, 1)
                block.tensor(_t)
                ev_tp = ctx.bump("s_mm")

                def _v(v, j=j, ev=ev_tp):
                    v.wait_ge(s_mm, ev)
                    v.tensor_copy(lgN[:, j, :], ps_tpf[:, 32:36])
                    v.tensor_copy(jnk_v[:, 3:4], lgN[:, j, 0:1]).then_inc(s_v, 1)
                block.vector(_v)
                cp_ev = ctx.bump("s_v")

            def _vcas(v, ce=cp_ev):
                v.wait_ge(s_v, ce)
                for j in range(NT_OWN):
                    v.reduce_max(mcol[:, j:j + 1], lgN[:, j, :], axis=AX)
                vsync(v)
                for j in range(NT_OWN):
                    ts1(v, fflg8[:, j, :], lgN[:, j, :], mcol[:, j:j + 1], ALU.is_equal)
                vsync(v)
                for j in range(NT_OWN):
                    v.tensor_copy(ohn[:, j, 0:1], fflg8[:, j, 0:1])
                for j in range(NT_OWN):
                    v.tensor_scalar(ntA[:, j:j + 1], fflg8[:, j, 0:1], -1.0, 1.0,
                                    ALU.mult, ALU.add)
                vsync(v)
                for j in range(NT_OWN):
                    v.tensor_tensor(ohn[:, j, 1:2], fflg8[:, j, 1:2], ntA[:, j:j + 1],
                                    ALU.mult)
                for j in range(NT_OWN):
                    v.tensor_scalar(ntB[:, j:j + 1], fflg8[:, j, 1:2], -1.0, 1.0,
                                    ALU.mult, ALU.add)
                vsync(v)
                for j in range(NT_OWN):
                    v.tensor_tensor(ntC[:, j:j + 1], ntA[:, j:j + 1], ntB[:, j:j + 1],
                                    ALU.mult)
                vsync(v)
                for j in range(NT_OWN):
                    v.tensor_tensor(ohn[:, j, 2:3], fflg8[:, j, 2:3], ntC[:, j:j + 1],
                                    ALU.mult)
                for j in range(NT_OWN):
                    v.tensor_scalar(ntB[:, j:j + 1], fflg8[:, j, 2:3], -1.0, 1.0,
                                    ALU.mult, ALU.add)
                vsync(v)
                for j in range(NT_OWN):
                    v.tensor_tensor(ntA[:, j:j + 1], ntC[:, j:j + 1], ntB[:, j:j + 1],
                                    ALU.mult)
                vsync(v)
                for j in range(NT_OWN):
                    v.tensor_tensor(ohn[:, j, 3:4], fflg8[:, j, 3:4], ntA[:, j:j + 1],
                                    ALU.mult)
                v.tensor_copy(jnk_v[:, 1:2], ohn[:, NT_OWN - 1, 3:4]).then_inc(s_v, 1)
            block.vector(_vcas)
            oh_ev = ctx.bump("s_v")

            tpb_last = 0
            for j in range(NT_OWN):
                def _t(t, j=j, ev=oh_ev):
                    if j == 0:
                        t.wait_ge(s_v, ev)
                    t.transpose(ps_oh[:, 128 * j:128 * (j + 1)], ohn[:, j, :],
                                idf_sb[:]).then_inc(s_mm, 1)
                block.tensor(_t)
                tpb_last = ctx.bump("s_mm")

            def _vf3(v, ev=tpb_last):
                v.wait_ge(s_mm, ev)
                v.tensor_copy(ohT_sb[0:4, :], ps_oh[:])
                v.tensor_copy(jnk_v[0:4, 3:4], ohT_sb[0:4, 0:1]).then_inc(s_v, 1)
            block.vector(_vf3)
            ev_ohT = ctx.bump("s_v")

            def _tf4(t, ev=ev_ohT):
                t.wait_ge(s_v, ev)
                for th in range(2):
                    m = t.matmul(ps_ex[:, 512 * th:512 * (th + 1)], ee_sb[:],
                                 ohT_sb[0:4, 512 * th:512 * (th + 1)], start=True, stop=True)
                m.then_inc(s_mm, 1)
            block.tensor(_tf4)
            ev_ex = ctx.bump("s_mm")

            def _vf5(v, ev=ev_ex):
                v.wait_ge(s_mm, ev)
                v.tensor_copy(ex_sb[0:32, :], ps_ex[:])
                vsync(v)
                v.tensor_tensor(zmT[0:32, :], ps_sh[0:32, :], ex_sb[0:32, :], ALU.mult)
                v.tensor_copy(jnk_v[0:32, 3:4], zmT[0:32, 0:1]).then_inc(s_v, 1)
            block.vector(_vf5)
            ev_zm = ctx.bump("s_v")

            def _tf6(t, ev=ev_zm):
                t.wait_ge(s_v, ev)
            block.tensor(_tf6)

            if tap == "zmT":
                finish_tap(zmT[0:32, :], [32, T_OWN], BF16)
                return nc, tap_dram[0]

        def delta_fn(t, ps, ot, th):
            t.matmul(ps[:], bfl_sb[:, 128 * ot:128 * (ot + 1)],
                     zmT[0:32, 512 * th:512 * (th + 1)], start=True, stop=False)

        big_proj(wup_t,
                 lambda fc, th: ln2T[:, fc, 512 * th:512 * (th + 1)],
                 ffnT, delta_fn=delta_fn, first_wait=[(s_tr, ln2_tr_last)])

        if tap == "ffnT":
            finish_tap(ffnT, [128, NH, T_OWN], BF16)
            return nc, tap_dram[0]
        barrier()

        resid_phase(ffnT, natG, is_final=True)

        def _fin(sy):
            sy.wait_ge(s_out, ctx.val("s_out"))
        block.sync(_fin)

    return nc, tap_dram[0]



def _rope_cache(S_, D_, base=10000.0):
    inv_freq = 1.0 / base ** (np.arange(0, D_, 2, dtype=np.float32) / D_)
    t = np.arange(S_, dtype=np.float32)
    f = np.outer(t, inv_freq)
    cos = np.zeros((S_, D_), dtype=np.float32)
    sin = np.zeros((S_, D_), dtype=np.float32)
    cos[:, 0::2] = np.cos(f)
    cos[:, 1::2] = np.cos(f)
    sin[:, 0::2] = np.sin(f)
    sin[:, 1::2] = np.sin(f)
    return cos, sin


def _tile_w(wT):
    f, o = wT.shape
    if o <= 128:
        return np.ascontiguousarray(wT.reshape(FC, 128, o).transpose(1, 0, 2))
    return np.ascontiguousarray(
        wT.reshape(FC, 128, o // 128, 128).transpose(2, 1, 0, 3))


def make_inputs(x, Wq, Wk, Wv, Wo, W_up, gate_w, expert_bias, lora_A, lora_B):
    cos, sin = _rope_cache(S, DH)
    wq_t = _tile_w(np.ascontiguousarray(Wq.T))
    wo_t = _tile_w(np.ascontiguousarray(Wo.T))
    wup_t = _tile_w(np.ascontiguousarray(W_up.T))
    wk_t = _tile_w(np.ascontiguousarray(Wk.T))
    wv_t = _tile_w(np.ascontiguousarray(Wv.T))
    shW = np.zeros((D, 64), np.float32)
    shW[:, 0:32] = lora_A.reshape(32, D).T
    shW[:, 32:36] = gate_w.T
    sh_t = _tile_w(shW)
    bflat = np.ascontiguousarray(lora_B.transpose(0, 2, 1).reshape(32, D))
    eexp = np.zeros((4, 32), np.float32)
    for ei in range(4):
        eexp[ei, ei * 8:(ei + 1) * 8] = 1.0
    identf = np.eye(128, dtype=np.float32)
    pmat = np.zeros((128, 128), np.float32)
    for d_ in range(64):
        pmat[d_, d_ + 64] = -1.0
        pmat[d_ + 64, d_] = 1.0
    ptmat = np.ascontiguousarray(pmat.T)
    lin = np.linspace(0.0, 1.0, 256, dtype=np.float32)

    maps = []
    for c in range(8):
        b, j = divmod(c, 4)
        g0 = 1024 * j - 256
        xs = np.zeros((T_SLAB, D), np.float32)
        lo, hi = max(0, g0), min(S, g0 + T_SLAB)
        xs[lo - g0:hi - g0, :] = x[b, lo:hi, :]
        ct = np.zeros((128, T_SLAB), np.float32)
        sn = np.zeros((128, T_SLAB), np.float32)
        ct[:, lo - g0:hi - g0] = cos[lo:hi].T
        sn[:, lo - g0:hi - g0] = sin[lo:hi].T
        aA = np.zeros((128, 2 * SEGS), np.float32)
        aB = np.zeros((128, 2 * SEGS), np.float32)
        for s_i in range(SEGS):
            g = 4 * j + s_i
            ab = np.ones(256, np.float32) if g == 0 else lin
            for qc in range(2):
                aB[:, 2 * s_i + qc] = ab[128 * qc:128 * (qc + 1)]
                aA[:, 2 * s_i + qc] = np.float32(1.0) - ab[128 * qc:128 * (qc + 1)]
        mb = np.zeros((128, 2), np.float32)
        if j == 3:
            mb[:] = -200.0
        maps.append(dict(
            xslab=xs, wq_t=wq_t, wo_t=wo_t, wup_t=wup_t, wk_t=wk_t, wv_t=wv_t,
            sh_t=sh_t, bflat=bflat,
            ebias=np.asarray(expert_bias, np.float32).reshape(4, 1),
            eexp=eexp, identf=identf, ptmat=ptmat, cosT=ct, sinT=sn,
            alphaA=aA, alphaB=aB, maskb=mb))
    return maps


def kernel(L0_attn_x, L0_attn_Wq, L0_attn_Wk, L0_attn_Wv, L0_attn_Wo,
           L0_ffn_W_up, L0_ffn_gate_w, L0_ffn_expert_bias, L0_ffn_lora_A,
           L0_ffn_lora_B):
    x = np.asarray(L0_attn_x, np.float32)
    maps = make_inputs(x, np.asarray(L0_attn_Wq, np.float32),
                       np.asarray(L0_attn_Wk, np.float32),
                       np.asarray(L0_attn_Wv, np.float32),
                       np.asarray(L0_attn_Wo, np.float32),
                       np.asarray(L0_ffn_W_up, np.float32),
                       np.asarray(L0_ffn_gate_w, np.float32),
                       np.asarray(L0_ffn_expert_bias, np.float32),
                       np.asarray(L0_ffn_lora_A, np.float32),
                       np.asarray(L0_ffn_lora_B, np.float32))
    nc, _ = build_program()
    res = run_bass_kernel_spmd(nc, maps, core_ids=list(range(8)))
    full = np.empty((B, S, D), np.float32)
    for c in range(8):
        b, j = divmod(c, 4)
        full[b, 1024 * j:1024 * (j + 1), :] = res.results[c]["out"]
    return full


# version 12
# speedup vs baseline: 1.3616x; 1.0023x over previous
import math
import numpy as np
from contextlib import ExitStack

import concourse.bass as bass
import concourse.mybir as mybir
from concourse.bass_utils import run_bass_kernel_spmd

F32 = mybir.dt.float32
BF16 = mybir.dt.bfloat16
AF = mybir.ActivationFunctionType
ALU = mybir.AluOpType
AX = mybir.AxisListType.X

B, S, D = 2, 4096, 4096
NH, DH = 32, 128
FC = D // 128
T_OWN = 1024
T_SLAB = 1536
NT_OWN = 8
NT_SLAB = 12
SEGS = 4
SCALE = 1.0 / math.sqrt(DH)
EPS = 1e-5

RA = 0
RB = 65536
RC = 131072
RE = 147456
RF = 163840
ARENA_BYTES = 180224


class Ctx:
    def __init__(self):
        self.cnt = {}

    def bump(self, name, k=1):
        self.cnt[name] = self.cnt.get(name, 0) + k
        return self.cnt[name]

    def val(self, name):
        return self.cnt.get(name, 0)


def build_program(tap=None):
    nc = bass.Bass("TRN2", target_bir_lowering=False)

    xslab = nc.dram_tensor("xslab", [T_SLAB, D], F32, kind="ExternalInput").ap()
    wq_t = nc.dram_tensor("wq_t", [FC, 128, FC, 128], F32, kind="ExternalInput").ap()
    wo_t = nc.dram_tensor("wo_t", [FC, 128, FC, 128], F32, kind="ExternalInput").ap()
    wup_t = nc.dram_tensor("wup_t", [FC, 128, FC, 128], F32, kind="ExternalInput").ap()
    wk_t = nc.dram_tensor("wk_t", [128, FC, 128], F32, kind="ExternalInput").ap()
    wv_t = nc.dram_tensor("wv_t", [128, FC, 128], F32, kind="ExternalInput").ap()
    sh_t = nc.dram_tensor("sh_t", [128, FC, 64], F32, kind="ExternalInput").ap()
    bflat = nc.dram_tensor("bflat", [32, D], F32, kind="ExternalInput").ap()
    ebias = nc.dram_tensor("ebias", [4, 1], F32, kind="ExternalInput").ap()
    eexp = nc.dram_tensor("eexp", [4, 32], F32, kind="ExternalInput").ap()
    identf = nc.dram_tensor("identf", [128, 128], F32, kind="ExternalInput").ap()
    ptmat = nc.dram_tensor("ptmat", [128, 128], F32, kind="ExternalInput").ap()
    cosT = nc.dram_tensor("cosT", [128, T_SLAB], F32, kind="ExternalInput").ap()
    sinT = nc.dram_tensor("sinT", [128, T_SLAB], F32, kind="ExternalInput").ap()
    alphaA = nc.dram_tensor("alphaA", [128, 2 * SEGS], F32, kind="ExternalInput").ap()
    alphaB = nc.dram_tensor("alphaB", [128, 2 * SEGS], F32, kind="ExternalInput").ap()
    maskb = nc.dram_tensor("maskb", [128, 2], F32, kind="ExternalInput").ap()

    out = nc.dram_tensor("out", [T_OWN, D], F32, kind="ExternalOutput").ap()
    xres = nc.dram_tensor("xres", [T_OWN, D], F32, kind="Internal").ap()
    tap_dram = [None]

    ctx = Ctx()
    st = ExitStack()
    e = st.enter_context

    with st:
        s_x = e(nc.semaphore("s_x"))
        s_w = e(nc.semaphore("s_w"))
        s_tr = e(nc.semaphore("s_tr"))
        s_mm = e(nc.semaphore("s_mm"))
        s_v = e(nc.semaphore("s_v"))
        s_s = e(nc.semaphore("s_s"))
        s_res = e(nc.semaphore("s_res"))
        s_out = e(nc.semaphore("s_out"))
        s_bar = e(nc.semaphore("s_bar"))
        s_g = e(nc.semaphore("s_g"))

        cos_sb = e(nc.sbuf_tensor("cos_sb", [128, T_SLAB], F32))
        sin_sb = e(nc.sbuf_tensor("sin_sb", [128, T_SLAB], F32))
        aA_sb = e(nc.sbuf_tensor("aA_sb", [128, 2 * SEGS], F32))
        aB_sb = e(nc.sbuf_tensor("aB_sb", [128, 2 * SEGS], F32))
        mb_sb = e(nc.sbuf_tensor("mb_sb", [128, 2], F32))
        idb_sb = e(nc.sbuf_tensor("idb_sb", [128, 128], BF16))
        idf_sb = e(nc.sbuf_tensor("idf_sb", [128, 128], F32))
        pt_sb = e(nc.sbuf_tensor("pt_sb", [128, 128], BF16))
        eb_sb = e(nc.sbuf_tensor("eb_sb", [36, 1], F32))
        ee_sb = e(nc.sbuf_tensor("ee_sb", [4, 32], BF16))
        sh_sb = e(nc.sbuf_tensor("sh_sb", [128, FC, 64], BF16))
        bfl_sb = e(nc.sbuf_tensor("bfl_sb", [32, D], BF16))
        krT_sb = e(nc.sbuf_tensor("krT_sb", [128, T_SLAB], BF16))
        vaug_sb = e(nc.sbuf_tensor("vaug_sb", [128, NT_SLAB, 132], BF16))
        st_sum = e(nc.sbuf_tensor("st_sum", [128, 8], F32))
        st_ssq = e(nc.sbuf_tensor("st_ssq", [128, 8], F32))
        st_mu = e(nc.sbuf_tensor("st_mu", [128, 8], F32))
        st_var = e(nc.sbuf_tensor("st_var", [128, 8], F32))
        st_rstd = e(nc.sbuf_tensor("st_rstd", [128, 8], F32))
        st_nb = e(nc.sbuf_tensor("st_nb", [128, 8], F32))
        st_nmu = e(nc.sbuf_tensor("st_nmu", [128, 8], F32))
        st_musq = e(nc.sbuf_tensor("st_musq", [128, 8], F32))
        st_var2 = e(nc.sbuf_tensor("st_var2", [128, 8], F32))
        st_rvar = e(nc.sbuf_tensor("st_rvar", [128, 8], F32))
        jnk_v = e(nc.sbuf_tensor("jnk_v", [128, 4], F32))
        jnk_s = e(nc.sbuf_tensor("jnk_s", [128, 4], F32))
        bl_r = e(nc.sbuf_tensor("bl_r", [128, 8], F32))

        arena = e(nc.sbuf_tensor("arena", [128, ARENA_BYTES // 2], BF16))

        def vb(off, *dims):
            n = 1
            for d_ in dims:
                n *= d_
            ap = arena[:, off // 2: off // 2 + n]
            if len(dims) == 2:
                ap = ap.rearrange("p (a b) -> p a b", a=dims[0], b=dims[1])
            return ap

        def vf(off, *dims):
            n = 1
            for d_ in dims:
                n *= d_
            ap = arena[:, off // 2: off // 2 + 2 * n].bitcast(F32)
            if len(dims) == 2:
                ap = ap.rearrange("p (a b) -> p a b", a=dims[0], b=dims[1])
            return ap

        ln1T = vb(RA, FC, T_OWN)
        attnT = vb(RA, NH, T_OWN)
        nat8 = vb(RA, NT_OWN, D)
        ffnT = vb(RA, NH, T_OWN)
        wk_sb = vb(RB, FC, 128)
        wv_sb = vb(RB + 8192, FC, 128)
        xload2 = [vf(RB + 16384, D), vf(RB + 32768, D)]
        qrT = vb(RB, NH, T_OWN)
        projT = vb(RB, NH, T_OWN)
        ln2T = vb(RB, FC, T_OWN)
        natG = vb(RB, NT_OWN, D)
        lnb = [vb(RC, D), vb(RC + 8192, D)]
        wbuf = [vb(RC, FC, 128), vb(RC + 8192, FC, 128)]
        anat = [vb(RC, D), vb(RC + 8192, D)]
        halotmp = [vb(RE, FC, 128), vb(RE + 8192, FC, 128)]
        rstage = vb(RE, T_OWN)
        sigAB = [vb(RE, 2 * T_OWN), vb(RE + 4096, 2 * T_OWN)]
        xnf = vf(RE, D)
        lgT = vf(RE, T_OWN)
        ohT_sb = vb(RE + 4096, T_OWN)
        ex_sb = vf(RE + 6144, T_OWN)
        zmT = vb(RE + 10240, T_OWN)
        mcol = vf(RE + 12288, NT_OWN)
        fflg = vf(RE + 12320, 4)
        ntmp = vf(RE + 12336, 4)
        ohn = vf(RE + 12352, NT_OWN, 4)
        lgN = vf(RE + 12480, NT_OWN, 4)
        fflg8 = vf(RE + 12608, NT_OWN, 4)
        ntA = vf(RE + 12736, NT_OWN)
        ntB = vf(RE + 12768, NT_OWN)
        ntC = vf(RE + 12800, NT_OWN)
        bl_t = vf(RE + 8192, 256)
        dumpA = vb(RF, D)
        vstage = [vb(RF + 8192, 128), vb(RF + 8448, 128)]
        kstage = [vb(RF + 8704, 128), vb(RF + 8960, 128)]
        ropesc = vf(RF + 9216, T_OWN)
        t1f = vf(RF, D)

        block = e(nc.Block())
        ENGS = ("tensor", "vector", "scalar", "sync", "gpsimd")

        def barrier():
            tgt = ctx.bump("s_bar", len(ENGS))
            for en in ENGS:
                def _f(eng, tgt=tgt):
                    eng.sem_inc(s_bar, 1)
                    eng.wait_ge(s_bar, tgt)
                getattr(block, en)(_f)

        def finish_tap(src_ap, shape, dtype):
            tap_dram[0] = nc.dram_tensor("tap", shape, dtype, kind="ExternalOutput").ap()
            barrier()

            def _f(sy):
                sy.dma_start(tap_dram[0], src_ap).then_inc(s_out, 16)
                sy.wait_ge(s_out, ctx.bump("s_out", 16))
            block.sync(_f)

        def ts1(eng, o, i0, s1, op):
            return eng.tensor_scalar(o, i0, s1, 0.0, op, ALU.add)

        def vsync(v):
            v.tensor_copy(jnk_v[:, 2:3], jnk_v[:, 2:3]).then_inc(s_v, 1)
            v.wait_ge(s_v, ctx.bump("s_v"))

        def ssync(s):
            s.activation(jnk_s[:, 0:1], jnk_s[:, 0:1], AF.Identity).then_inc(s_s, 1)
            s.wait_ge(s_s, ctx.bump("s_s"))

        def p0_sync(sy):
            for dst, src in ((cos_sb[:], cosT), (sin_sb[:], sinT), (aA_sb[:], alphaA),
                             (aB_sb[:], alphaB), (mb_sb[:], maskb), (idf_sb[:], identf),
                             (eb_sb[32:36, :], ebias)):
                sy.dma_start(dst, src).then_inc(s_x, 16)
                ctx.bump("s_x", 16)
        block.sync(p0_sync)

        def p0_g(g):
            for dst, src in ((idb_sb[:], identf), (pt_sb[:], ptmat), (ee_sb[:], eexp),
                             (wk_sb, wk_t), (wv_sb, wv_t), (sh_sb[:], sh_t),
                             (bfl_sb[:], bflat)):
                g.dma_start(dst, src).then_inc(s_w, 16)
                ctx.bump("s_w", 16)
            g.memset(vaug_sb[:, :, 128:129], 1.0)
            g.memset(jnk_v[:], 0.0)
            g.memset(jnk_s[:], 0.0).then_inc(s_g, 1)
            ctx.bump("s_g", 1)
        block.gpsimd(p0_g)
        P0_W = ctx.val("s_w")
        P0_G = ctx.val("s_g")

        def ln_tile(src_ap, col, dst_ap, dump_ap, pre_s=None, pre_v=None, pre_s3=None):
            c = slice(col, col + 1)

            def _sc(s):
                if pre_s:
                    for sm, v in pre_s:
                        s.wait_ge(sm, v)
                s.activation(dump_ap, src_ap, AF.Square,
                             accum_out=st_ssq[:, c]).then_inc(s_s, 1)
                s.wait_ge(s_s, ctx.bump("s_s"))
                s.activation(jnk_s[:, 0:1], st_ssq[:, c], AF.Identity).then_inc(s_s, 1)
            block.scalar(_sc)
            ev_ssq = ctx.bump("s_s")

            def _v(v, ev_ssq=ev_ssq):
                if pre_v:
                    for sm, vv in pre_v:
                        v.wait_ge(sm, vv)
                v.reduce_sum(st_sum[:, c], src_ap, axis=AX)
                vsync(v)
                ts1(v, st_mu[:, c], st_sum[:, c], 1.0 / D, ALU.mult)
                ts1(v, st_nmu[:, c], st_sum[:, c], -1.0 / D, ALU.mult)
                v.wait_ge(s_s, ev_ssq)
                v.tensor_scalar(st_var[:, c], st_ssq[:, c], 1.0 / D, EPS,
                                ALU.mult, ALU.add)
                v.tensor_tensor(st_musq[:, c], st_mu[:, c], st_mu[:, c], ALU.mult)
                vsync(v)
                v.scalar_tensor_tensor(st_var2[:, c], st_musq[:, c], -1.0,
                                       st_var[:, c], ALU.mult, ALU.add)
                vsync(v)
                v.reciprocal(st_rvar[:, c], st_var2[:, c])
                v.tensor_copy(jnk_v[:, 0:1], st_rvar[:, c]).then_inc(s_v, 1)
            block.vector(_v)
            ev_rv = ctx.bump("s_v")

            def _sc3(s, ev=ev_rv):
                s.wait_ge(s_v, ev)
                if pre_s3:
                    for sm, vv in pre_s3:
                        s.wait_ge(sm, vv)
                s.activation(st_rstd[:, c], st_rvar[:, c], AF.Sqrt)
                ssync(s)
                s.activation(st_nb[:, c], st_rstd[:, c], AF.Identity,
                             scale=st_nmu[:, c])
                ssync(s)
                s.activation(dst_ap, src_ap, AF.Identity,
                             bias=st_nb[:, c], scale=st_rstd[:, c])
                s.activation(jnk_s[:, 3:4], st_nb[:, c], AF.Identity).then_inc(s_s, 1)
            block.scalar(_sc3)
            return ctx.bump("s_s")

        def _pa_v(v):
            v.wait_ge(s_g, P0_G)
        block.vector(_pa_v)

        def _pa_s(s2):
            s2.wait_ge(s_g, P0_G)
        block.scalar(_pa_s)

        with ExitStack() as ps_st:
            pe = ps_st.enter_context
            ps_k = pe(nc.psum_tensor("ps_k", [128, 128], F32))
            ps_vv = pe(nc.psum_tensor("ps_vv", [128, 128], F32))
            ps_r = pe(nc.psum_tensor("ps_r", [128, 128], F32))
            ps_tp = pe(nc.psum_tensor("ps_tp", [128, 128], BF16))

            HALO = {0: 0, 1: 1, 10: 0, 11: 1}
            halo_kv_done = {}
            norm_prev = 0
            norm_prev2 = [0, 0]
            tr_prev = {}
            kevac = 0
            vevac = 0
            rope_prev = 0
            vcp_prev = 0
            ln_tr_ev = [0] * NT_SLAB
            for i in range(NT_SLAB):
                slot = i % 2

                def _ld(g, i=i, np_=norm_prev2[i % 2]):
                    if np_:
                        g.wait_ge(s_s, np_)
                    g.dma_start(xload2[i % 2],
                                 xslab[128 * i:128 * (i + 1), :]).then_inc(s_x, 16)
                block.gpsimd(_ld)
                ev_x = ctx.bump("s_x", 16)
                xload = xload2[i % 2]

                pre3 = [(s_tr, tr_prev[i - 2])] if i >= 2 else None
                ev_norm = ln_tile(xload, i % 4, lnb[slot], dumpA,
                                  pre_s=[(s_x, ev_x)], pre_v=[(s_x, ev_x)], pre_s3=pre3)
                norm_prev = ev_norm
                norm_prev2[i % 2] = ev_norm

                if i in HALO:
                    hs = HALO[i]
                    dest = halotmp[hs]
                else:
                    dest = ln1T[:, :, 128 * (i - 2):128 * (i - 1)]
                    hs = None

                def _tr(sy, i=i, slot=slot, ev=ev_norm, hs=hs, dest=dest):
                    sy.wait_ge(s_s, ev)
                    if hs is not None and hs in halo_kv_done:
                        sy.wait_ge(s_mm, halo_kv_done[hs])
                    sy.dma_start_transpose(dest, lnb[slot]).then_inc(s_tr, 16)
                block.sync(_tr)
                ln_tr_ev[i] = ctx.bump("s_tr", 16)
                tr_prev[i] = ln_tr_ev[i]

                def _t(t, i=i, dest=dest, ev_tr=ln_tr_ev[i], ke=kevac, ve=vevac):
                    if i == 0:
                        t.wait_ge(s_w, P0_W)
                        t.wait_ge(s_g, P0_G)
                    t.wait_ge(s_tr, ev_tr)
                    if ke:
                        t.wait_ge(s_s, ke)
                    for fc in range(FC):
                        m = t.matmul(ps_k[:], wk_sb[:, fc, :], dest[:, fc, :],
                                     start=(fc == 0), stop=(fc == FC - 1))
                    m.then_inc(s_mm, 1)
                    if ve:
                        t.wait_ge(s_s, ve)
                    for fc in range(FC):
                        m = t.matmul(ps_vv[:], wv_sb[:, fc, :], dest[:, fc, :],
                                     start=(fc == 0), stop=(fc == FC - 1))
                    m.then_inc(s_mm, 1)
                block.tensor(_t)
                ev_k = ctx.bump("s_mm")
                ev_vmm = ctx.bump("s_mm")
                if i in HALO:
                    halo_kv_done[HALO[i]] = ev_vmm

                def _s(s, i=i, slot=slot, evk=ev_k, evv=ev_vmm, rp=rope_prev, vc=vcp_prev):
                    s.wait_ge(s_mm, evk)
                    if rp:
                        s.wait_ge(s_v, rp)
                    s.activation(kstage[slot], ps_k[:], AF.Identity)
                    s.activation(jnk_s[:, 1:2], kstage[slot][:, 0:1],
                                 AF.Identity).then_inc(s_s, 1)
                    s.wait_ge(s_mm, evv)
                    if vc:
                        s.wait_ge(s_v, vc)
                    s.activation(vstage[slot], ps_vv[:], AF.Identity)
                    s.activation(jnk_s[:, 2:3], vstage[slot][:, 0:1],
                                 AF.Identity).then_inc(s_s, 1)
                block.scalar(_s)
                kevac = ctx.bump("s_s")
                vevac = ctx.bump("s_s")

                def _t2(t, slot=slot, ke=kevac, ve=vevac, rp=rope_prev, vc=vcp_prev):
                    t.wait_ge(s_s, ke)
                    if rp:
                        t.wait_ge(s_v, rp)
                    t.matmul(ps_r[:], pt_sb[:], kstage[slot],
                             start=True, stop=True).then_inc(s_mm, 1)
                    t.wait_ge(s_s, ve)
                    if vc:
                        t.wait_ge(s_v, vc)
                    t.transpose(ps_tp[:], vstage[slot], idb_sb[:]).then_inc(s_mm, 1)
                block.tensor(_t2)
                ev_rot = ctx.bump("s_mm")
                ev_vtp = ctx.bump("s_mm")

                def _v(v, i=i, slot=slot, er=ev_rot, et=ev_vtp):
                    cols = slice(128 * i, 128 * (i + 1))
                    v.wait_ge(s_mm, er)
                    v.tensor_tensor(ropesc[:, 0:128], kstage[slot], cos_sb[:, cols], ALU.mult)
                    v.tensor_tensor(ropesc[:, 128:256], ps_r[:], sin_sb[:, cols], ALU.mult)
                    vsync(v)
                    v.tensor_tensor(krT_sb[:, cols], ropesc[:, 0:128], ropesc[:, 128:256],
                                    ALU.add)
                    v.tensor_copy(jnk_v[:, 3:4], krT_sb[:, cols][:, 0:1]).then_inc(s_v, 1)
                    v.wait_ge(s_mm, et)
                    v.tensor_copy(vaug_sb[:, i, 0:128], ps_tp[:])
                    v.tensor_copy(jnk_v[:, 3:4], vaug_sb[:, i, 0:1]).then_inc(s_v, 1)
                block.vector(_v)
                rope_prev = ctx.bump("s_v")
                vcp_prev = ctx.bump("s_v")

            if tap == "kst":
                kdbg = vb(RF + 13312, 1024)

                def _vdbg(v):
                    kf = kdbg.bitcast(F32)
                    v.tensor_copy(kf[:, 0:1], st_sum[:, 3:4])
                    v.tensor_copy(kf[:, 1:2], st_ssq[:, 3:4])
                    v.tensor_copy(kf[:, 2:3], st_mu[:, 3:4])
                    v.tensor_copy(kf[:, 3:4], st_nmu[:, 3:4])
                    v.tensor_copy(kf[:, 4:5], st_var[:, 3:4])
                    v.tensor_copy(kf[:, 5:6], st_musq[:, 3:4])
                    v.tensor_copy(kf[:, 6:7], st_var2[:, 3:4])
                    v.tensor_copy(kf[:, 7:8], st_rvar[:, 3:4])
                    v.tensor_copy(kf[:, 8:9], st_rstd[:, 3:4])
                    v.tensor_copy(kf[:, 9:10], st_nb[:, 3:4])
                    v.tensor_copy(kdbg[:, 512:1024], lnb[1][:, 0:512]).then_inc(s_v, 1)
                block.vector(_vdbg)
                ctx.bump("s_v")
                finish_tap(kdbg, [128, 1024], BF16)
                return nc, tap_dram[0]
            if tap == "ln1T":
                finish_tap(ln1T, [128, FC, T_OWN], BF16)
                return nc, tap_dram[0]
            if tap == "kv":
                finish_tap(krT_sb[:], [128, T_SLAB], BF16)
                return nc, tap_dram[0]
            if tap == "vaug":
                finish_tap(vaug_sb[:], [128, NT_SLAB, 132], BF16)
                return nc, tap_dram[0]
            barrier()

        with ExitStack() as ps_st:
            pe = ps_st.enter_context
            ps_q = [[pe(nc.psum_tensor(f"ps_q{p}{i}", [128, 512], F32))
                     for i in range(2)] for p in range(2)]
            ps_r2 = [[pe(nc.psum_tensor(f"ps_r2{p}{i}", [128, 512], F32))
                      for i in range(2)] for p in range(2)]

            wbuf_free = [0, 0]
            psq_free = [[0, 0], [0, 0]]
            psr_free = [[0, 0], [0, 0]]
            qst_free = [[0, 0], [0, 0]]
            for ot in range(NH):
                slot = ot % 2

                def _g(g, ot=ot, slot=slot, w=wbuf_free[slot]):
                    if w:
                        g.wait_ge(s_mm, w)
                    g.dma_start(wbuf[slot], wq_t[ot]).then_inc(s_w, 16)
                block.gpsimd(_g)
                ev_w = ctx.bump("s_w", 16)

                last_mm = None
                for th in range(2):
                    pq = ps_q[slot][th]
                    pr = ps_r2[slot][th]

                    def _t(t, th=th, slot=slot, ev_w=ev_w, pf=psq_free[slot][th], pq=pq):
                        t.wait_ge(s_w, ev_w)
                        if pf:
                            t.wait_ge(s_s, pf)
                        for fc in range(FC):
                            m = t.matmul(pq[:], wbuf[slot][:, fc, :],
                                         ln1T[:, fc, 512 * th:512 * (th + 1)],
                                         start=(fc == 0), stop=(fc == FC - 1))
                        m.then_inc(s_mm, 1)
                    block.tensor(_t)
                    ev_mm = ctx.bump("s_mm")
                    last_mm = ev_mm

                    def _s(s, th=th, slot=slot, ev=ev_mm, qf=qst_free[slot][th], pq=pq):
                        s.wait_ge(s_mm, ev)
                        if qf:
                            s.wait_ge(s_v, qf)
                        s.activation(rstage[:, 512 * th:512 * (th + 1)], pq[:],
                                     AF.Identity)
                        s.activation(jnk_s[:, 1:2], rstage[:, 512 * th:512 * th + 1],
                                     AF.Identity).then_inc(s_s, 1)
                    block.scalar(_s)
                    ev_ev = ctx.bump("s_s")
                    psq_free[slot][th] = ev_ev

                    def _t2(t, th=th, slot=slot, ev=ev_ev, rf=psr_free[slot][th], pr=pr):
                        t.wait_ge(s_s, ev)
                        if rf:
                            t.wait_ge(s_v, rf)
                        t.matmul(pr[:], pt_sb[:], rstage[:, 512 * th:512 * (th + 1)],
                                 start=True, stop=True).then_inc(s_mm, 1)
                    block.tensor(_t2)
                    ev_rot = ctx.bump("s_mm")

                    def _v(v, ot=ot, th=th, ev=ev_rot, pr=pr):
                        v.wait_ge(s_mm, ev)
                        ccols = slice(256 + 512 * th, 256 + 512 * (th + 1))
                        v.tensor_tensor(ropesc[:, 0:512], rstage[:, 512 * th:512 * (th + 1)],
                                        cos_sb[:, ccols], ALU.mult)
                        v.tensor_tensor(ropesc[:, 512:1024], pr[:],
                                        sin_sb[:, ccols], ALU.mult)
                        v.tensor_tensor(qrT[:, ot, 512 * th:512 * (th + 1)],
                                        ropesc[:, 0:512], ropesc[:, 512:1024],
                                        ALU.add)
                        v.tensor_copy(jnk_v[:, 3:4],
                                      qrT[:, ot, 512 * th:512 * th + 1]).then_inc(s_v, 1)
                    block.vector(_v)
                    ev_rope = ctx.bump("s_v")
                    psr_free[slot][th] = ev_rope
                    qst_free[slot][th] = ev_rope
                wbuf_free[slot] = last_mm

            if tap == "qrT":
                finish_tap(qrT, [128, NH, T_OWN], BF16)
                return nc, tap_dram[0]
            barrier()

        with ExitStack() as ps_st:
            pe = ps_st.enter_context
            ps_scA = pe(nc.psum_tensor("ps_scA", [128, 1024], F32))
            ps_scB = pe(nc.psum_tensor("ps_scB", [128, 1024], F32))
            ps_wA = pe(nc.psum_tensor("ps_wA", [128, 512], F32))
            ps_wB = pe(nc.psum_tensor("ps_wB", [128, 512], F32))

            sigA_done = 0
            sigB_done = 0
            blend_done = 0
            anat_guard = 0
            anat_tr = {}
            heads = [(s_i, h) for s_i in range(SEGS) for h in range(NH)]

            def emit_scores(idx):
                s_i, h = heads[idx]
                qcols = slice(256 * s_i, 256 * (s_i + 1))

                def _t(t, s_i=s_i, h=h, qcols=qcols, sa=sigA_done, sb=sigB_done):
                    if sa:
                        t.wait_ge(s_s, sa)
                    for j in range(4):
                        t.matmul(ps_scA[:, 256 * j:256 * (j + 1)],
                                 krT_sb[:, 128 * (2 * s_i + j):128 * (2 * s_i + j + 1)],
                                 qrT[:, h, qcols], start=True,
                                 stop=True).then_inc(s_mm, 1) if j == 3 else \
                        t.matmul(ps_scA[:, 256 * j:256 * (j + 1)],
                                 krT_sb[:, 128 * (2 * s_i + j):128 * (2 * s_i + j + 1)],
                                 qrT[:, h, qcols], start=True, stop=True)
                    if sb:
                        t.wait_ge(s_s, sb)
                    for j in range(4):
                        m = t.matmul(ps_scB[:, 256 * j:256 * (j + 1)],
                                     krT_sb[:, 128 * (2 * s_i + 2 + j):128 * (2 * s_i + 3 + j)],
                                     qrT[:, h, qcols], start=True, stop=True)
                    m.then_inc(s_mm, 1)
                block.tensor(_t)
                evA = ctx.bump("s_mm")
                evB = ctx.bump("s_mm")
                return evA, evB

            def emit_sig(idx, evA, evB):
                nonlocal sigA_done, sigB_done
                s_i, h = heads[idx]

                sl = sigAB[idx % 2]
                sigA = sl[:, 0:1024]
                sigB = sl[:, 1024:2048]

                def _s(s, s_i=s_i, evA=evA, evB=evB, sigA=sigA, sigB=sigB):
                    s.wait_ge(s_mm, evA)
                    s.activation(sigA, ps_scA[:], AF.Sigmoid,
                                 scale=SCALE).then_inc(s_s, 1)
                    s.wait_ge(s_mm, evB)
                    if s_i == SEGS - 1:
                        s.activation(sigB[:, 0:512], ps_scB[:, 0:512],
                                     AF.Sigmoid, scale=SCALE)
                        s.activation(sigB[:, 512:768], ps_scB[:, 512:768], AF.Sigmoid,
                                     scale=SCALE, bias=mb_sb[:, 0:1])
                        s.activation(sigB[:, 768:1024], ps_scB[:, 768:1024], AF.Sigmoid,
                                     scale=SCALE, bias=mb_sb[:, 1:2])
                    else:
                        s.activation(sigB, ps_scB[:], AF.Sigmoid, scale=SCALE)
                    s.activation(jnk_s[:, 1:2], sigB[:, 1023:1024],
                                 AF.Sigmoid).then_inc(s_s, 1)
                block.scalar(_s)
                sigA_done = ctx.bump("s_s")
                sigB_done = ctx.bump("s_s")

            def emit_wout(idx, sd):
                s_i, h = heads[idx]
                sl = sigAB[idx % 2]
                sigA = sl[:, 0:1024]
                sigB = sl[:, 1024:2048]

                def _t2(t, s_i=s_i, sd=sd, bd=blend_done, sigA=sigA, sigB=sigB):
                    t.wait_ge(s_s, sd)
                    if bd:
                        t.wait_ge(s_v, bd)
                    for qc in range(2):
                        off = 132 * qc
                        for j in range(4):
                            t.matmul(ps_wA[:, off:off + 129],
                                     sigA[:, 256 * j + 128 * qc:256 * j + 128 * (qc + 1)],
                                     vaug_sb[:, 2 * s_i + j, 0:129],
                                     start=(j == 0), stop=(j == 3))
                        for j in range(4):
                            m = t.matmul(ps_wB[:, off:off + 129],
                                         sigB[:, 256 * j + 128 * qc:256 * j + 128 * (qc + 1)],
                                         vaug_sb[:, 2 * s_i + 2 + j, 0:129],
                                         start=(j == 0), stop=(j == 3))
                    m.then_inc(s_mm, 1)
                block.tensor(_t2)
                return ctx.bump("s_mm")

            def emit_blend(idx, ev_w2):
                nonlocal blend_done
                s_i, h = heads[idx]

                def _v(v, s_i=s_i, h=h, ev=ev_w2, ag=anat_guard):
                    v.wait_ge(s_mm, ev)
                    if ag and h == 0:
                        v.wait_ge(s_tr, ag)
                    sA2 = ps_wA[:, 0:264].rearrange("p (a b) -> p a b", a=2, b=132)[
                        :, :, 128:129].rearrange("p a b -> p (a b)")
                    sB2 = ps_wB[:, 0:264].rearrange("p (a b) -> p a b", a=2, b=132)[
                        :, :, 128:129].rearrange("p a b -> p (a b)")
                    v.reciprocal(bl_r[:, 0:2], sA2)
                    v.reciprocal(bl_r[:, 2:4], sB2)
                    vsync(v)
                    acols = slice(2 * s_i, 2 * s_i + 2)
                    v.tensor_tensor(bl_r[:, 4:6], bl_r[:, 0:2], aA_sb[:, acols], ALU.mult)
                    v.tensor_tensor(bl_r[:, 6:8], bl_r[:, 2:4], aB_sb[:, acols], ALU.mult)
                    vsync(v)
                    for qc in range(2):
                        ts1(v, bl_t[:, 128 * qc:128 * (qc + 1)],
                            ps_wA[:, 132 * qc:132 * qc + 128],
                            bl_r[:, 4 + qc:5 + qc], ALU.mult)
                    for qc in range(2):
                        v.scalar_tensor_tensor(
                            anat[qc][:, 128 * h:128 * (h + 1)],
                            ps_wB[:, 132 * qc:132 * qc + 128],
                            bl_r[:, 6 + qc:7 + qc],
                            bl_t[:, 128 * qc:128 * (qc + 1)],
                            ALU.mult, ALU.add)
                    v.tensor_copy(jnk_v[:, 3:4],
                                  anat[1][:, 128 * h:128 * h + 1]).then_inc(s_v, 1)
                block.vector(_v)
                blend_done = ctx.bump("s_v")

            pend = None
            for idx in range(len(heads)):
                evA, evB = emit_scores(idx)
                emit_sig(idx, evA, evB)
                my_sig = sigB_done
                if pend is not None:
                    ev_w2 = emit_wout(pend[0], pend[1])
                    emit_blend(pend[0], ev_w2)
                pend = (idx, my_sig)
                s_i, h = heads[idx]
                if h == NH - 1 and pend is not None:
                    ev_w2 = emit_wout(pend[0], pend[1])
                    emit_blend(pend[0], ev_w2)
                    pend = None
                    for qc in range(2):
                        ti = 2 * s_i + qc

                        def _tr(sy, ti=ti, qc=qc, bd=blend_done):
                            sy.wait_ge(s_v, bd)
                            sy.dma_start_transpose(
                                attnT[:, :, 128 * ti:128 * (ti + 1)],
                                anat[qc]).then_inc(s_tr, 16)
                        block.sync(_tr)
                        anat_tr[ti] = ctx.bump("s_tr", 16)
                    anat_guard = anat_tr[2 * s_i + 1]

            if tap == "wlast":
                wdbg = vf(RE + 8192, T_OWN)

                def _vdbg(v, bd=blend_done):
                    v.wait_ge(s_v, bd)
                    v.tensor_copy(wdbg[:, 0:264], ps_wA.bitcast(F32)[:, 0:264])
                    v.tensor_copy(wdbg[:, 264:528], ps_wB.bitcast(F32)[:, 0:264]).then_inc(s_v, 1)
                block.vector(_vdbg)
                ctx.bump("s_v")
                finish_tap(wdbg, [128, T_OWN], F32)
                return nc, tap_dram[0]
            if tap == "sclast":
                wdbg2 = vf(RE + 8192, T_OWN)

                def _vdbg2(v, ev=ev_sc):
                    v.wait_ge(s_mm, ev)
                    v.tensor_copy(wdbg2[:], ps_scB[:]).then_inc(s_v, 1)
                block.vector(_vdbg2)
                ctx.bump("s_v")
                finish_tap(wdbg2, [128, T_OWN], F32)
                return nc, tap_dram[0]
            if tap == "siglast":
                finish_tap(sigB, [128, T_OWN], BF16)
                return nc, tap_dram[0]
            if tap == "attnT":
                finish_tap(attnT, [128, NH, T_OWN], BF16)
                return nc, tap_dram[0]
            attn_tr_last = anat_tr[2 * SEGS - 1]
            barrier()

        def big_proj(wdram, rhs_fn, dst, delta_fn=None, first_wait=None):
            with ExitStack() as ps_st2:
                pe2 = ps_st2.enter_context
                ps_p = [pe2(nc.psum_tensor(f"ps_p{ctx.bump('psn')}", [128, 512], F32))
                        for _ in range(2)]
                wb_free = [0, 0]
                ps_free = [0, 0]
                for ot in range(FC):
                    slot = ot % 2

                    def _g(g, ot=ot, slot=slot, w=wb_free[slot]):
                        if w:
                            g.wait_ge(s_mm, w)
                        g.dma_start(wbuf[slot], wdram[ot]).then_inc(s_w, 16)
                    block.gpsimd(_g)
                    ev_w = ctx.bump("s_w", 16)

                    last = None
                    for th in range(2):
                        def _t(t, ot=ot, th=th, slot=slot, ev_w=ev_w, pf=ps_free[th]):
                            if ot == 0 and th == 0 and first_wait:
                                for sm, vv in first_wait:
                                    t.wait_ge(sm, vv)
                            t.wait_ge(s_w, ev_w)
                            if pf:
                                t.wait_ge(s_s, pf)
                            first = True
                            if delta_fn is not None:
                                delta_fn(t, ps_p[th], ot, th)
                                first = False
                            for fc in range(FC):
                                m = t.matmul(ps_p[th][:], wbuf[slot][:, fc, :],
                                             rhs_fn(fc, th),
                                             start=(first and fc == 0),
                                             stop=(fc == FC - 1))
                            m.then_inc(s_mm, 1)
                        block.tensor(_t)
                        ev_mm = ctx.bump("s_mm")
                        last = ev_mm

                        def _s(s, ot=ot, th=th, ev=ev_mm):
                            s.wait_ge(s_mm, ev)
                            s.activation(dst[:, ot, 512 * th:512 * (th + 1)],
                                         ps_p[th][:], AF.Identity)
                            s.activation(jnk_s[:, 1:2],
                                         dst[:, ot, 512 * th:512 * th + 1],
                                         AF.Identity).then_inc(s_s, 1)
                        block.scalar(_s)
                        ps_free[th] = ctx.bump("s_s")
                    wb_free[slot] = last

        big_proj(wo_t,
                 lambda fc, th: attnT[:, fc, 512 * th:512 * (th + 1)],
                 projT, first_wait=[(s_tr, attn_tr_last)])
        if tap == "projT":
            finish_tap(projT, [128, NH, T_OWN], BF16)
            return nc, tap_dram[0]
        barrier()

        def resid_phase(srcT, natv, is_final):
            ptr = 0
            for ot in range(FC):
                def _tr(sy, ot=ot):
                    sy.dma_start_transpose(natv[:, :, 128 * ot:128 * (ot + 1)],
                                          srcT[:, ot, :]).then_inc(s_tr, 16)
                block.sync(_tr)
                ptr = ctx.bump("s_tr", 16)

            xn_free = None
            prev_add = 0
            ln2_tr_local = {}
            dumpR = vb(RC + 8192, D) if not is_final else vb(RC, D)
            for i in range(NT_OWN):
                slot = 0

                def _ld(sy, i=i, ptr=ptr, xe=xn_free):
                    if i == 0:
                        sy.wait_ge(s_tr, ptr)
                        if is_final:
                            sy.wait_ge(s_res, ctx.val("s_res"))
                    if xe is not None:
                        if is_final:
                            sy.wait_ge(s_out, xe)
                        else:
                            sy.wait_ge(s_res, xe[0])
                            sy.wait_ge(s_s, xe[1])
                    src = (xres[128 * i:128 * (i + 1), :] if is_final
                           else xslab[256 + 128 * i:256 + 128 * (i + 1), :])
                    sy.dma_start(xnf, src).then_inc(s_x, 16)
                block.gpsimd(_ld)
                ev_x = ctx.bump("s_x", 16)

                pre = [(s_tr, ptr)] if i == 0 else None
                pre3 = [(s_v, prev_add)] if prev_add else None
                ev_t1 = ln_tile(natv[:, i, :], i % 4, t1f, dumpR,
                                pre_s=pre, pre_v=pre, pre_s3=pre3)

                def _v3(v, ev_t1=ev_t1, ev_x=ev_x):
                    v.wait_ge(s_s, ev_t1)
                    v.wait_ge(s_x, ev_x)
                    v.tensor_tensor(xnf, xnf, t1f, ALU.add)
                    v.tensor_copy(jnk_v[:, 3:4], xnf[:, 0:1]).then_inc(s_v, 1)
                block.vector(_v3)
                ev_xn = ctx.bump("s_v")
                prev_add = ev_xn

                if is_final:
                    def _st(g, i=i, ev=ev_xn):
                        g.wait_ge(s_v, ev)
                        g.dma_start(out[128 * i:128 * (i + 1), :],
                                    xnf).then_inc(s_out, 16)
                    block.gpsimd(_st)
                    xn_free = ctx.bump("s_out", 16)
                else:
                    def _st(g, i=i, ev=ev_xn):
                        g.wait_ge(s_v, ev)
                        g.dma_start(xres[128 * i:128 * (i + 1), :],
                                    xnf).then_inc(s_res, 16)
                    block.gpsimd(_st)
                    ev_res = ctx.bump("s_res", 16)

                    pre3b = [(s_tr, ln2_tr_local[i - 1])] if i >= 1 else None
                    ev_n2 = ln_tile(xnf, 4 + i % 4, lnb[slot], dumpR,
                                    pre_s=[(s_v, ev_xn)], pre_v=[(s_v, ev_xn)],
                                    pre_s3=pre3b)

                    def _tr2(sy, i=i, slot=slot, ev=ev_n2):
                        sy.wait_ge(s_s, ev)
                        sy.dma_start_transpose(ln2T[:, :, 128 * i:128 * (i + 1)],
                                              lnb[slot]).then_inc(s_tr, 16)
                    block.sync(_tr2)
                    ln2_tr_local[i] = ctx.bump("s_tr", 16)
                    xn_free = (ev_res, ev_n2)
            return ln2_tr_local.get(NT_OWN - 1, 0)

        ln2_tr_last = resid_phase(projT, nat8, is_final=False)
        if tap == "ln2T":
            finish_tap(ln2T, [128, FC, T_OWN], BF16)
            return nc, tap_dram[0]
        barrier()

        with ExitStack() as ps_st:
            pe = ps_st.enter_context
            ps_sh = pe(nc.psum_tensor("ps_sh", [64, T_OWN], F32))
            ps_tpf = pe(nc.psum_tensor("ps_tpf", [128, 128], F32))
            ps_oh = pe(nc.psum_tensor("ps_oh", [4, T_OWN], F32))
            ps_ex = pe(nc.psum_tensor("ps_ex", [32, T_OWN], F32))

            def _gms(g):
                g.memset(lgT, 0.0).then_inc(s_g, 1)
            block.gpsimd(_gms)
            ev_ms = ctx.bump("s_g", 1)

            def _tf1(t):
                t.wait_ge(s_tr, ln2_tr_last)
                for th in range(2):
                    for fc in range(FC):
                        m = t.matmul(ps_sh[:, 512 * th:512 * (th + 1)], sh_sb[:, fc, :],
                                     ln2T[:, fc, 512 * th:512 * (th + 1)],
                                     start=(fc == 0), stop=(fc == FC - 1))
                m.then_inc(s_mm, 1)
            block.tensor(_tf1)
            ev_sh = ctx.bump("s_mm")

            def _vf2(v, ev=ev_sh, evm=ev_ms):
                v.wait_ge(s_mm, ev)
                v.wait_ge(s_g, evm)
                ts1(v, lgT[32:36, :], ps_sh[32:36, :], eb_sb[32:36, :], ALU.add)
                v.tensor_copy(jnk_v[0:4, 3:4], lgT[32:36, 0:1]).then_inc(s_v, 1)
            block.vector(_vf2)
            ev_lg = ctx.bump("s_v")

            cp_ev = 0
            for j in range(NT_OWN):
                def _t(t, j=j, ev=ev_lg, ce=cp_ev):
                    if j == 0:
                        t.wait_ge(s_v, ev)
                    if ce:
                        t.wait_ge(s_v, ce)
                    t.transpose(ps_tpf[:], lgT[:, 128 * j:128 * (j + 1)],
                                idf_sb[:]).then_inc(s_mm, 1)
                block.tensor(_t)
                ev_tp = ctx.bump("s_mm")

                def _v(v, j=j, ev=ev_tp):
                    v.wait_ge(s_mm, ev)
                    v.tensor_copy(lgN[:, j, :], ps_tpf[:, 32:36])
                    v.tensor_copy(jnk_v[:, 3:4], lgN[:, j, 0:1]).then_inc(s_v, 1)
                block.vector(_v)
                cp_ev = ctx.bump("s_v")

            def _vcas(v, ce=cp_ev):
                v.wait_ge(s_v, ce)
                for j in range(NT_OWN):
                    v.reduce_max(mcol[:, j:j + 1], lgN[:, j, :], axis=AX)
                vsync(v)
                for j in range(NT_OWN):
                    ts1(v, fflg8[:, j, :], lgN[:, j, :], mcol[:, j:j + 1], ALU.is_equal)
                vsync(v)
                for j in range(NT_OWN):
                    v.tensor_copy(ohn[:, j, 0:1], fflg8[:, j, 0:1])
                for j in range(NT_OWN):
                    v.tensor_scalar(ntA[:, j:j + 1], fflg8[:, j, 0:1], -1.0, 1.0,
                                    ALU.mult, ALU.add)
                vsync(v)
                for j in range(NT_OWN):
                    v.tensor_tensor(ohn[:, j, 1:2], fflg8[:, j, 1:2], ntA[:, j:j + 1],
                                    ALU.mult)
                for j in range(NT_OWN):
                    v.tensor_scalar(ntB[:, j:j + 1], fflg8[:, j, 1:2], -1.0, 1.0,
                                    ALU.mult, ALU.add)
                vsync(v)
                for j in range(NT_OWN):
                    v.tensor_tensor(ntC[:, j:j + 1], ntA[:, j:j + 1], ntB[:, j:j + 1],
                                    ALU.mult)
                vsync(v)
                for j in range(NT_OWN):
                    v.tensor_tensor(ohn[:, j, 2:3], fflg8[:, j, 2:3], ntC[:, j:j + 1],
                                    ALU.mult)
                for j in range(NT_OWN):
                    v.tensor_scalar(ntB[:, j:j + 1], fflg8[:, j, 2:3], -1.0, 1.0,
                                    ALU.mult, ALU.add)
                vsync(v)
                for j in range(NT_OWN):
                    v.tensor_tensor(ntA[:, j:j + 1], ntC[:, j:j + 1], ntB[:, j:j + 1],
                                    ALU.mult)
                vsync(v)
                for j in range(NT_OWN):
                    v.tensor_tensor(ohn[:, j, 3:4], fflg8[:, j, 3:4], ntA[:, j:j + 1],
                                    ALU.mult)
                v.tensor_copy(jnk_v[:, 1:2], ohn[:, NT_OWN - 1, 3:4]).then_inc(s_v, 1)
            block.vector(_vcas)
            oh_ev = ctx.bump("s_v")

            tpb_last = 0
            for j in range(NT_OWN):
                def _t(t, j=j, ev=oh_ev):
                    if j == 0:
                        t.wait_ge(s_v, ev)
                    t.transpose(ps_oh[:, 128 * j:128 * (j + 1)], ohn[:, j, :],
                                idf_sb[:]).then_inc(s_mm, 1)
                block.tensor(_t)
                tpb_last = ctx.bump("s_mm")

            def _vf3(v, ev=tpb_last):
                v.wait_ge(s_mm, ev)
                v.tensor_copy(ohT_sb[0:4, :], ps_oh[:])
                v.tensor_copy(jnk_v[0:4, 3:4], ohT_sb[0:4, 0:1]).then_inc(s_v, 1)
            block.vector(_vf3)
            ev_ohT = ctx.bump("s_v")

            def _tf4(t, ev=ev_ohT):
                t.wait_ge(s_v, ev)
                for th in range(2):
                    m = t.matmul(ps_ex[:, 512 * th:512 * (th + 1)], ee_sb[:],
                                 ohT_sb[0:4, 512 * th:512 * (th + 1)], start=True, stop=True)
                m.then_inc(s_mm, 1)
            block.tensor(_tf4)
            ev_ex = ctx.bump("s_mm")

            def _vf5(v, ev=ev_ex):
                v.wait_ge(s_mm, ev)
                v.tensor_copy(ex_sb[0:32, :], ps_ex[:])
                vsync(v)
                v.tensor_tensor(zmT[0:32, :], ps_sh[0:32, :], ex_sb[0:32, :], ALU.mult)
                v.tensor_copy(jnk_v[0:32, 3:4], zmT[0:32, 0:1]).then_inc(s_v, 1)
            block.vector(_vf5)
            ev_zm = ctx.bump("s_v")

            def _tf6(t, ev=ev_zm):
                t.wait_ge(s_v, ev)
            block.tensor(_tf6)

            if tap == "zmT":
                finish_tap(zmT[0:32, :], [32, T_OWN], BF16)
                return nc, tap_dram[0]

        def delta_fn(t, ps, ot, th):
            t.matmul(ps[:], bfl_sb[:, 128 * ot:128 * (ot + 1)],
                     zmT[0:32, 512 * th:512 * (th + 1)], start=True, stop=False)

        big_proj(wup_t,
                 lambda fc, th: ln2T[:, fc, 512 * th:512 * (th + 1)],
                 ffnT, delta_fn=delta_fn, first_wait=[(s_tr, ln2_tr_last)])

        if tap == "ffnT":
            finish_tap(ffnT, [128, NH, T_OWN], BF16)
            return nc, tap_dram[0]
        barrier()

        resid_phase(ffnT, natG, is_final=True)

        def _fin(sy):
            sy.wait_ge(s_out, ctx.val("s_out"))
        block.sync(_fin)

    return nc, tap_dram[0]



def _rope_cache(S_, D_, base=10000.0):
    inv_freq = 1.0 / base ** (np.arange(0, D_, 2, dtype=np.float32) / D_)
    t = np.arange(S_, dtype=np.float32)
    f = np.outer(t, inv_freq)
    cos = np.zeros((S_, D_), dtype=np.float32)
    sin = np.zeros((S_, D_), dtype=np.float32)
    cos[:, 0::2] = np.cos(f)
    cos[:, 1::2] = np.cos(f)
    sin[:, 0::2] = np.sin(f)
    sin[:, 1::2] = np.sin(f)
    return cos, sin


def _tile_w(wT):
    f, o = wT.shape
    if o <= 128:
        return np.ascontiguousarray(wT.reshape(FC, 128, o).transpose(1, 0, 2))
    return np.ascontiguousarray(
        wT.reshape(FC, 128, o // 128, 128).transpose(2, 1, 0, 3))


def make_inputs(x, Wq, Wk, Wv, Wo, W_up, gate_w, expert_bias, lora_A, lora_B):
    cos, sin = _rope_cache(S, DH)
    wq_t = _tile_w(np.ascontiguousarray(Wq.T))
    wo_t = _tile_w(np.ascontiguousarray(Wo.T))
    wup_t = _tile_w(np.ascontiguousarray(W_up.T))
    wk_t = _tile_w(np.ascontiguousarray(Wk.T))
    wv_t = _tile_w(np.ascontiguousarray(Wv.T))
    shW = np.zeros((D, 64), np.float32)
    shW[:, 0:32] = lora_A.reshape(32, D).T
    shW[:, 32:36] = gate_w.T
    sh_t = _tile_w(shW)
    bflat = np.ascontiguousarray(lora_B.transpose(0, 2, 1).reshape(32, D))
    eexp = np.zeros((4, 32), np.float32)
    for ei in range(4):
        eexp[ei, ei * 8:(ei + 1) * 8] = 1.0
    identf = np.eye(128, dtype=np.float32)
    pmat = np.zeros((128, 128), np.float32)
    for d_ in range(64):
        pmat[d_, d_ + 64] = -1.0
        pmat[d_ + 64, d_] = 1.0
    ptmat = np.ascontiguousarray(pmat.T)
    lin = np.linspace(0.0, 1.0, 256, dtype=np.float32)

    maps = []
    for c in range(8):
        b, j = divmod(c, 4)
        g0 = 1024 * j - 256
        xs = np.zeros((T_SLAB, D), np.float32)
        lo, hi = max(0, g0), min(S, g0 + T_SLAB)
        xs[lo - g0:hi - g0, :] = x[b, lo:hi, :]
        ct = np.zeros((128, T_SLAB), np.float32)
        sn = np.zeros((128, T_SLAB), np.float32)
        ct[:, lo - g0:hi - g0] = cos[lo:hi].T
        sn[:, lo - g0:hi - g0] = sin[lo:hi].T
        aA = np.zeros((128, 2 * SEGS), np.float32)
        aB = np.zeros((128, 2 * SEGS), np.float32)
        for s_i in range(SEGS):
            g = 4 * j + s_i
            ab = np.ones(256, np.float32) if g == 0 else lin
            for qc in range(2):
                aB[:, 2 * s_i + qc] = ab[128 * qc:128 * (qc + 1)]
                aA[:, 2 * s_i + qc] = np.float32(1.0) - ab[128 * qc:128 * (qc + 1)]
        mb = np.zeros((128, 2), np.float32)
        if j == 3:
            mb[:] = -200.0
        maps.append(dict(
            xslab=xs, wq_t=wq_t, wo_t=wo_t, wup_t=wup_t, wk_t=wk_t, wv_t=wv_t,
            sh_t=sh_t, bflat=bflat,
            ebias=np.asarray(expert_bias, np.float32).reshape(4, 1),
            eexp=eexp, identf=identf, ptmat=ptmat, cosT=ct, sinT=sn,
            alphaA=aA, alphaB=aB, maskb=mb))
    return maps


def kernel(L0_attn_x, L0_attn_Wq, L0_attn_Wk, L0_attn_Wv, L0_attn_Wo,
           L0_ffn_W_up, L0_ffn_gate_w, L0_ffn_expert_bias, L0_ffn_lora_A,
           L0_ffn_lora_B):
    x = np.asarray(L0_attn_x, np.float32)
    maps = make_inputs(x, np.asarray(L0_attn_Wq, np.float32),
                       np.asarray(L0_attn_Wk, np.float32),
                       np.asarray(L0_attn_Wv, np.float32),
                       np.asarray(L0_attn_Wo, np.float32),
                       np.asarray(L0_ffn_W_up, np.float32),
                       np.asarray(L0_ffn_gate_w, np.float32),
                       np.asarray(L0_ffn_expert_bias, np.float32),
                       np.asarray(L0_ffn_lora_A, np.float32),
                       np.asarray(L0_ffn_lora_B, np.float32))
    nc, _ = build_program()
    res = run_bass_kernel_spmd(nc, maps, core_ids=list(range(8)))
    full = np.empty((B, S, D), np.float32)
    for c in range(8):
        b, j = divmod(c, 4)
        full[b, 1024 * j:1024 * (j + 1), :] = res.results[c]["out"]
    return full


# version 14
# speedup vs baseline: 1.3666x; 1.0037x over previous
import math
import numpy as np
from contextlib import ExitStack

import concourse.bass as bass
import concourse.mybir as mybir
from concourse.bass_utils import run_bass_kernel_spmd

F32 = mybir.dt.float32
BF16 = mybir.dt.bfloat16
AF = mybir.ActivationFunctionType
ALU = mybir.AluOpType
AX = mybir.AxisListType.X

B, S, D = 2, 4096, 4096
NH, DH = 32, 128
FC = D // 128
T_OWN = 1024
T_SLAB = 1536
NT_OWN = 8
NT_SLAB = 12
SEGS = 4
SCALE = 1.0 / math.sqrt(DH)
EPS = 1e-5

RA = 0
RB = 65536
RC = 131072
RE = 147456
RF = 163840
ARENA_BYTES = 180224


class Ctx:
    def __init__(self):
        self.cnt = {}

    def bump(self, name, k=1):
        self.cnt[name] = self.cnt.get(name, 0) + k
        return self.cnt[name]

    def val(self, name):
        return self.cnt.get(name, 0)


def build_program(tap=None):
    nc = bass.Bass("TRN2", target_bir_lowering=False)

    xslab = nc.dram_tensor("xslab", [T_SLAB, D], F32, kind="ExternalInput").ap()
    wq_t = nc.dram_tensor("wq_t", [FC, 128, FC, 128], F32, kind="ExternalInput").ap()
    wo_t = nc.dram_tensor("wo_t", [FC, 128, FC, 128], F32, kind="ExternalInput").ap()
    wup_t = nc.dram_tensor("wup_t", [FC, 128, FC, 128], F32, kind="ExternalInput").ap()
    wk_t = nc.dram_tensor("wk_t", [128, FC, 128], F32, kind="ExternalInput").ap()
    wv_t = nc.dram_tensor("wv_t", [128, FC, 128], F32, kind="ExternalInput").ap()
    sh_t = nc.dram_tensor("sh_t", [128, FC, 64], F32, kind="ExternalInput").ap()
    bflat = nc.dram_tensor("bflat", [32, D], F32, kind="ExternalInput").ap()
    ebias = nc.dram_tensor("ebias", [4, 1], F32, kind="ExternalInput").ap()
    eexp = nc.dram_tensor("eexp", [4, 32], F32, kind="ExternalInput").ap()
    identf = nc.dram_tensor("identf", [128, 128], F32, kind="ExternalInput").ap()
    ptmat = nc.dram_tensor("ptmat", [128, 128], F32, kind="ExternalInput").ap()
    cosT = nc.dram_tensor("cosT", [128, T_SLAB], F32, kind="ExternalInput").ap()
    sinT = nc.dram_tensor("sinT", [128, T_SLAB], F32, kind="ExternalInput").ap()
    alphaA = nc.dram_tensor("alphaA", [128, 2 * SEGS], F32, kind="ExternalInput").ap()
    alphaB = nc.dram_tensor("alphaB", [128, 2 * SEGS], F32, kind="ExternalInput").ap()
    maskb = nc.dram_tensor("maskb", [128, 2], F32, kind="ExternalInput").ap()

    out = nc.dram_tensor("out", [T_OWN, D], F32, kind="ExternalOutput").ap()
    xres = nc.dram_tensor("xres", [T_OWN, D], F32, kind="Internal").ap()
    tap_dram = [None]

    ctx = Ctx()
    st = ExitStack()
    e = st.enter_context

    with st:
        s_x = e(nc.semaphore("s_x"))
        s_w = e(nc.semaphore("s_w"))
        s_tr = e(nc.semaphore("s_tr"))
        s_mm = e(nc.semaphore("s_mm"))
        s_v = e(nc.semaphore("s_v"))
        s_s = e(nc.semaphore("s_s"))
        s_res = e(nc.semaphore("s_res"))
        s_out = e(nc.semaphore("s_out"))
        s_bar = e(nc.semaphore("s_bar"))
        s_g = e(nc.semaphore("s_g"))

        cos_sb = e(nc.sbuf_tensor("cos_sb", [128, T_SLAB], F32))
        sin_sb = e(nc.sbuf_tensor("sin_sb", [128, T_SLAB], F32))
        aA_sb = e(nc.sbuf_tensor("aA_sb", [128, 2 * SEGS], F32))
        aB_sb = e(nc.sbuf_tensor("aB_sb", [128, 2 * SEGS], F32))
        mb_sb = e(nc.sbuf_tensor("mb_sb", [128, 2], F32))
        idb_sb = e(nc.sbuf_tensor("idb_sb", [128, 128], BF16))
        idf_sb = e(nc.sbuf_tensor("idf_sb", [128, 128], F32))
        pt_sb = e(nc.sbuf_tensor("pt_sb", [128, 128], BF16))
        eb_sb = e(nc.sbuf_tensor("eb_sb", [36, 1], F32))
        ee_sb = e(nc.sbuf_tensor("ee_sb", [4, 32], BF16))
        sh_sb = e(nc.sbuf_tensor("sh_sb", [128, FC, 64], BF16))
        bfl_sb = e(nc.sbuf_tensor("bfl_sb", [32, D], BF16))
        krT_sb = e(nc.sbuf_tensor("krT_sb", [128, T_SLAB], BF16))
        vaug_sb = e(nc.sbuf_tensor("vaug_sb", [128, NT_SLAB, 132], BF16))
        st_sum = e(nc.sbuf_tensor("st_sum", [128, 8], F32))
        st_ssq = e(nc.sbuf_tensor("st_ssq", [128, 8], F32))
        st_mu = e(nc.sbuf_tensor("st_mu", [128, 8], F32))
        st_var = e(nc.sbuf_tensor("st_var", [128, 8], F32))
        st_rstd = e(nc.sbuf_tensor("st_rstd", [128, 8], F32))
        st_nb = e(nc.sbuf_tensor("st_nb", [128, 8], F32))
        st_nmu = e(nc.sbuf_tensor("st_nmu", [128, 8], F32))
        st_musq = e(nc.sbuf_tensor("st_musq", [128, 8], F32))
        st_var2 = e(nc.sbuf_tensor("st_var2", [128, 8], F32))
        st_rvar = e(nc.sbuf_tensor("st_rvar", [128, 8], F32))
        jnk_v = e(nc.sbuf_tensor("jnk_v", [128, 4], F32))
        jnk_s = e(nc.sbuf_tensor("jnk_s", [128, 4], F32))
        bl_r = e(nc.sbuf_tensor("bl_r", [128, 8], F32))

        arena = e(nc.sbuf_tensor("arena", [128, ARENA_BYTES // 2], BF16))

        def vb(off, *dims):
            n = 1
            for d_ in dims:
                n *= d_
            ap = arena[:, off // 2: off // 2 + n]
            if len(dims) == 2:
                ap = ap.rearrange("p (a b) -> p a b", a=dims[0], b=dims[1])
            return ap

        def vf(off, *dims):
            n = 1
            for d_ in dims:
                n *= d_
            ap = arena[:, off // 2: off // 2 + 2 * n].bitcast(F32)
            if len(dims) == 2:
                ap = ap.rearrange("p (a b) -> p a b", a=dims[0], b=dims[1])
            return ap

        ln1T = vb(RA, FC, T_OWN)
        attnT = vb(RA, NH, T_OWN)
        nat8 = vb(RA, NT_OWN, D)
        ffnT = vb(RA, NH, T_OWN)
        wk_sb = vb(RB, FC, 128)
        wv_sb = vb(RB + 8192, FC, 128)
        xload2 = [vf(RB + 16384, D), vf(RB + 32768, D), vf(RB + 49152, D)]
        qrT = vb(RB, NH, T_OWN)
        projT = vb(RB, NH, T_OWN)
        ln2T = vb(RB, FC, T_OWN)
        natG = vb(RB, NT_OWN, D)
        lnb = [vb(RC, D), vb(RC + 8192, D)]
        wbuf = [vb(RC, FC, 128), vb(RC + 8192, FC, 128)]
        anat = [vb(RC, D), vb(RC + 8192, D)]
        halotmp = [vb(RE, FC, 128), vb(RE + 8192, FC, 128)]
        rstage = vb(RE, T_OWN)
        sigAB = [vb(RE, 2 * T_OWN), vb(RE + 4096, 2 * T_OWN)]
        xnf = vf(RE, D)
        lgT = vf(RE, T_OWN)
        ohT_sb = vb(RE + 4096, T_OWN)
        ex_sb = vf(RE + 6144, T_OWN)
        zmT = vb(RE + 10240, T_OWN)
        mcol = vf(RE + 12288, NT_OWN)
        fflg = vf(RE + 12320, 4)
        ntmp = vf(RE + 12336, 4)
        ohn = vf(RE + 12352, NT_OWN, 4)
        lgN = vf(RE + 12480, NT_OWN, 4)
        fflg8 = vf(RE + 12608, NT_OWN, 4)
        ntA = vf(RE + 12736, NT_OWN)
        ntB = vf(RE + 12768, NT_OWN)
        ntC = vf(RE + 12800, NT_OWN)
        bl_t = vf(RE + 8192, 256)
        dumpA = vb(RF, D)
        vstage = [vb(RF + 8192, 128), vb(RF + 8448, 128)]
        kstage = [vb(RF + 8704, 128), vb(RF + 8960, 128)]
        ropesc = vf(RF + 9216, T_OWN)
        t1f = vf(RF, D)

        block = e(nc.Block())
        ENGS = ("tensor", "vector", "scalar", "sync", "gpsimd")

        def barrier():
            tgt = ctx.bump("s_bar", len(ENGS))
            for en in ENGS:
                def _f(eng, tgt=tgt):
                    eng.sem_inc(s_bar, 1)
                    eng.wait_ge(s_bar, tgt)
                getattr(block, en)(_f)

        def finish_tap(src_ap, shape, dtype):
            tap_dram[0] = nc.dram_tensor("tap", shape, dtype, kind="ExternalOutput").ap()
            barrier()

            def _f(sy):
                sy.dma_start(tap_dram[0], src_ap).then_inc(s_out, 16)
                sy.wait_ge(s_out, ctx.bump("s_out", 16))
            block.sync(_f)

        def ts1(eng, o, i0, s1, op):
            return eng.tensor_scalar(o, i0, s1, 0.0, op, ALU.add)

        def vsync(v):
            v.tensor_copy(jnk_v[:, 2:3], jnk_v[:, 2:3]).then_inc(s_v, 1)
            v.wait_ge(s_v, ctx.bump("s_v"))

        def ssync(s):
            s.activation(jnk_s[:, 0:1], jnk_s[:, 0:1], AF.Identity).then_inc(s_s, 1)
            s.wait_ge(s_s, ctx.bump("s_s"))

        def p0_sync(sy):
            for dst, src in ((cos_sb[:], cosT), (sin_sb[:], sinT), (aA_sb[:], alphaA),
                             (aB_sb[:], alphaB), (mb_sb[:], maskb), (idf_sb[:], identf),
                             (eb_sb[32:36, :], ebias)):
                sy.dma_start(dst, src).then_inc(s_x, 16)
                ctx.bump("s_x", 16)
        block.sync(p0_sync)

        def p0_g(g):
            for dst, src in ((idb_sb[:], identf), (pt_sb[:], ptmat), (ee_sb[:], eexp),
                             (wk_sb, wk_t), (wv_sb, wv_t), (sh_sb[:], sh_t),
                             (bfl_sb[:], bflat)):
                g.dma_start(dst, src).then_inc(s_w, 16)
                ctx.bump("s_w", 16)
            g.memset(vaug_sb[:, :, 128:129], 1.0)
            g.memset(jnk_v[:], 0.0)
            g.memset(jnk_s[:], 0.0).then_inc(s_g, 1)
            ctx.bump("s_g", 1)
        block.gpsimd(p0_g)
        P0_W = ctx.val("s_w")
        P0_G = ctx.val("s_g")

        def ln_sq(src_ap, col, dump_ap, pre_s=None):
            c = slice(col, col + 1)

            def _sc(s):
                if pre_s:
                    for sm, v in pre_s:
                        s.wait_ge(sm, v)
                s.activation(dump_ap, src_ap, AF.Square,
                             accum_out=st_ssq[:, c]).then_inc(s_s, 1)
                s.wait_ge(s_s, ctx.bump("s_s"))
                s.activation(jnk_s[:, 0:1], st_ssq[:, c], AF.Identity).then_inc(s_s, 1)
            block.scalar(_sc)
            return ctx.bump("s_s")

        def ln_vstats(src_ap, col, ev_ssq, pre_v=None):
            c = slice(col, col + 1)

            def _v(v, ev_ssq=ev_ssq):
                if pre_v:
                    for sm, vv in pre_v:
                        v.wait_ge(sm, vv)
                v.reduce_sum(st_sum[:, c], src_ap, axis=AX)
                vsync(v)
                ts1(v, st_mu[:, c], st_sum[:, c], 1.0 / D, ALU.mult)
                ts1(v, st_nmu[:, c], st_sum[:, c], -1.0 / D, ALU.mult)
                v.wait_ge(s_s, ev_ssq)
                v.tensor_scalar(st_var[:, c], st_ssq[:, c], 1.0 / D, EPS,
                                ALU.mult, ALU.add)
                v.tensor_tensor(st_musq[:, c], st_mu[:, c], st_mu[:, c], ALU.mult)
                vsync(v)
                v.scalar_tensor_tensor(st_var2[:, c], st_musq[:, c], -1.0,
                                       st_var[:, c], ALU.mult, ALU.add)
                vsync(v)
                v.reciprocal(st_rvar[:, c], st_var2[:, c])
                v.tensor_copy(jnk_v[:, 0:1], st_rvar[:, c]).then_inc(s_v, 1)
            block.vector(_v)
            return ctx.bump("s_v")

        def ln_fin(src_ap, col, dst_ap, ev_rv, pre_s3=None):
            c = slice(col, col + 1)

            def _sc3(s, ev=ev_rv):
                s.wait_ge(s_v, ev)
                if pre_s3:
                    for sm, vv in pre_s3:
                        s.wait_ge(sm, vv)
                s.activation(st_rstd[:, c], st_rvar[:, c], AF.Sqrt)
                ssync(s)
                s.activation(st_nb[:, c], st_rstd[:, c], AF.Identity,
                             scale=st_nmu[:, c])
                ssync(s)
                s.activation(dst_ap, src_ap, AF.Identity,
                             bias=st_nb[:, c], scale=st_rstd[:, c])
                s.activation(jnk_s[:, 3:4], st_nb[:, c], AF.Identity).then_inc(s_s, 1)
            block.scalar(_sc3)
            return ctx.bump("s_s")

        def ln_tile(src_ap, col, dst_ap, dump_ap, pre_s=None, pre_v=None, pre_s3=None):
            ev_ssq = ln_sq(src_ap, col, dump_ap, pre_s=pre_s)
            ev_rv = ln_vstats(src_ap, col, ev_ssq, pre_v=pre_v)
            return ln_fin(src_ap, col, dst_ap, ev_rv, pre_s3=pre_s3)

        with ExitStack() as ps_st:
            pe = ps_st.enter_context
            ps_k = pe(nc.psum_tensor("ps_k", [128, 128], F32))
            ps_vv = pe(nc.psum_tensor("ps_vv", [128, 128], F32))
            ps_r = pe(nc.psum_tensor("ps_r", [128, 128], F32))
            ps_tp = pe(nc.psum_tensor("ps_tp", [128, 128], BF16))

            HALO = {0: 0, 1: 1, 10: 0, 11: 1}
            halo_kv_done = {}
            norm_prev2 = [0, 0, 0]
            tr_prev = {}
            kevac = 0
            vevac = 0
            rope_prev = 0
            vcp_prev = 0
            ln_tr_ev = [0] * NT_SLAB
            st_rv = [0] * NT_SLAB

            def a_front(i):
                def _ld(g, i=i, np_=norm_prev2[i % 3]):
                    if np_:
                        g.wait_ge(s_s, np_)
                    g.dma_start(xload2[i % 3],
                                xslab[128 * i:128 * (i + 1), :]).then_inc(s_x, 16)
                block.gpsimd(_ld)
                ev_x = ctx.bump("s_x", 16)
                ev_ssq = ln_sq(xload2[i % 3], i % 4, dumpA, pre_s=[(s_x, ev_x)])
                st_rv[i] = ln_vstats(xload2[i % 3], i % 4, ev_ssq, pre_v=[(s_x, ev_x)])

            def a_back(i):
                nonlocal kevac, vevac, rope_prev, vcp_prev
                slot = i % 2
                pre3 = [(s_tr, tr_prev[i - 2])] if i >= 2 else None
                ev_norm = ln_fin(xload2[i % 3], i % 4, lnb[slot], st_rv[i], pre_s3=pre3)
                norm_prev2[i % 3] = ev_norm

                if i in HALO:
                    hs = HALO[i]
                    dest = halotmp[hs]
                else:
                    dest = ln1T[:, :, 128 * (i - 2):128 * (i - 1)]
                    hs = None

                def _tr(sy, i=i, slot=slot, ev=ev_norm, hs=hs, dest=dest):
                    sy.wait_ge(s_s, ev)
                    if hs is not None and hs in halo_kv_done:
                        sy.wait_ge(s_mm, halo_kv_done[hs])
                    sy.dma_start_transpose(dest, lnb[slot]).then_inc(s_tr, 16)
                block.sync(_tr)
                ln_tr_ev[i] = ctx.bump("s_tr", 16)
                tr_prev[i] = ln_tr_ev[i]

                def _t(t, i=i, dest=dest, ev_tr=ln_tr_ev[i], ke=kevac, ve=vevac):
                    if i == 0:
                        t.wait_ge(s_w, P0_W)
                        t.wait_ge(s_g, P0_G)
                    t.wait_ge(s_tr, ev_tr)
                    if ke:
                        t.wait_ge(s_s, ke)
                    for fc in range(FC):
                        m = t.matmul(ps_k[:], wk_sb[:, fc, :], dest[:, fc, :],
                                     start=(fc == 0), stop=(fc == FC - 1))
                    m.then_inc(s_mm, 1)
                    if ve:
                        t.wait_ge(s_s, ve)
                    for fc in range(FC):
                        m = t.matmul(ps_vv[:], wv_sb[:, fc, :], dest[:, fc, :],
                                     start=(fc == 0), stop=(fc == FC - 1))
                    m.then_inc(s_mm, 1)
                block.tensor(_t)
                ev_k = ctx.bump("s_mm")
                ev_vmm = ctx.bump("s_mm")
                if i in HALO:
                    halo_kv_done[HALO[i]] = ev_vmm

                def _s(s, i=i, slot=slot, evk=ev_k, evv=ev_vmm, rp=rope_prev, vc=vcp_prev):
                    s.wait_ge(s_mm, evk)
                    if rp:
                        s.wait_ge(s_v, rp)
                    s.activation(kstage[slot], ps_k[:], AF.Identity)
                    s.activation(jnk_s[:, 1:2], kstage[slot][:, 0:1],
                                 AF.Identity).then_inc(s_s, 1)
                    s.wait_ge(s_mm, evv)
                    if vc:
                        s.wait_ge(s_v, vc)
                    s.activation(vstage[slot], ps_vv[:], AF.Identity)
                    s.activation(jnk_s[:, 2:3], vstage[slot][:, 0:1],
                                 AF.Identity).then_inc(s_s, 1)
                block.scalar(_s)
                kevac = ctx.bump("s_s")
                vevac = ctx.bump("s_s")

                def _t2(t, slot=slot, ke=kevac, ve=vevac, rp=rope_prev, vc=vcp_prev):
                    t.wait_ge(s_s, ke)
                    if rp:
                        t.wait_ge(s_v, rp)
                    t.matmul(ps_r[:], pt_sb[:], kstage[slot],
                             start=True, stop=True).then_inc(s_mm, 1)
                    t.wait_ge(s_s, ve)
                    if vc:
                        t.wait_ge(s_v, vc)
                    t.transpose(ps_tp[:], vstage[slot], idb_sb[:]).then_inc(s_mm, 1)
                block.tensor(_t2)
                ev_rot = ctx.bump("s_mm")
                ev_vtp = ctx.bump("s_mm")

                def _v(v, i=i, slot=slot, er=ev_rot, et=ev_vtp):
                    cols = slice(128 * i, 128 * (i + 1))
                    v.wait_ge(s_mm, er)
                    v.tensor_tensor(ropesc[:, 0:128], kstage[slot], cos_sb[:, cols], ALU.mult)
                    v.tensor_tensor(ropesc[:, 128:256], ps_r[:], sin_sb[:, cols], ALU.mult)
                    vsync(v)
                    v.tensor_tensor(krT_sb[:, cols], ropesc[:, 0:128], ropesc[:, 128:256],
                                    ALU.add)
                    v.tensor_copy(jnk_v[:, 3:4], krT_sb[:, cols][:, 0:1]).then_inc(s_v, 1)
                    v.wait_ge(s_mm, et)
                    v.tensor_copy(vaug_sb[:, i, 0:128], ps_tp[:])
                    v.tensor_copy(jnk_v[:, 3:4], vaug_sb[:, i, 0:1]).then_inc(s_v, 1)
                block.vector(_v)
                rope_prev = ctx.bump("s_v")
                vcp_prev = ctx.bump("s_v")

            for i in range(NT_SLAB):
                a_front(i)
                if i >= 2:
                    a_back(i - 2)
            a_back(NT_SLAB - 2)
            a_back(NT_SLAB - 1)

            if tap == "ln1T":
                finish_tap(ln1T, [128, FC, T_OWN], BF16)
                return nc, tap_dram[0]
            if tap == "kv":
                finish_tap(krT_sb[:], [128, T_SLAB], BF16)
                return nc, tap_dram[0]
            if tap == "vaug":
                finish_tap(vaug_sb[:], [128, NT_SLAB, 132], BF16)
                return nc, tap_dram[0]
            barrier()

        with ExitStack() as ps_st:
            pe = ps_st.enter_context
            ps_q = [[pe(nc.psum_tensor(f"ps_q{p}{i}", [128, 512], F32))
                     for i in range(2)] for p in range(2)]
            ps_r2 = [[pe(nc.psum_tensor(f"ps_r2{p}{i}", [128, 512], F32))
                      for i in range(2)] for p in range(2)]

            wbuf_free = [0, 0]
            psq_free = [[0, 0], [0, 0]]
            psr_free = [[0, 0], [0, 0]]
            qst_free = [[0, 0], [0, 0]]
            for ot in range(NH):
                slot = ot % 2

                def _g(g, ot=ot, slot=slot, w=wbuf_free[slot]):
                    if w:
                        g.wait_ge(s_mm, w)
                    g.dma_start(wbuf[slot], wq_t[ot]).then_inc(s_w, 16)
                block.gpsimd(_g)
                ev_w = ctx.bump("s_w", 16)

                last_mm = None
                for th in range(2):
                    pq = ps_q[slot][th]
                    pr = ps_r2[slot][th]

                    def _t(t, th=th, slot=slot, ev_w=ev_w, pf=psq_free[slot][th], pq=pq):
                        t.wait_ge(s_w, ev_w)
                        if pf:
                            t.wait_ge(s_s, pf)
                        for fc in range(FC):
                            m = t.matmul(pq[:], wbuf[slot][:, fc, :],
                                         ln1T[:, fc, 512 * th:512 * (th + 1)],
                                         start=(fc == 0), stop=(fc == FC - 1))
                        m.then_inc(s_mm, 1)
                    block.tensor(_t)
                    ev_mm = ctx.bump("s_mm")
                    last_mm = ev_mm

                    def _s(s, th=th, slot=slot, ev=ev_mm, qf=qst_free[slot][th], pq=pq):
                        s.wait_ge(s_mm, ev)
                        if qf:
                            s.wait_ge(s_v, qf)
                        s.activation(rstage[:, 512 * th:512 * (th + 1)], pq[:],
                                     AF.Identity)
                        s.activation(jnk_s[:, 1:2], rstage[:, 512 * th:512 * th + 1],
                                     AF.Identity).then_inc(s_s, 1)
                    block.scalar(_s)
                    ev_ev = ctx.bump("s_s")
                    psq_free[slot][th] = ev_ev

                    def _t2(t, th=th, slot=slot, ev=ev_ev, rf=psr_free[slot][th], pr=pr):
                        t.wait_ge(s_s, ev)
                        if rf:
                            t.wait_ge(s_v, rf)
                        t.matmul(pr[:], pt_sb[:], rstage[:, 512 * th:512 * (th + 1)],
                                 start=True, stop=True).then_inc(s_mm, 1)
                    block.tensor(_t2)
                    ev_rot = ctx.bump("s_mm")

                    def _v(v, ot=ot, th=th, ev=ev_rot, pr=pr):
                        v.wait_ge(s_mm, ev)
                        ccols = slice(256 + 512 * th, 256 + 512 * (th + 1))
                        v.tensor_tensor(ropesc[:, 0:512], rstage[:, 512 * th:512 * (th + 1)],
                                        cos_sb[:, ccols], ALU.mult)
                        v.tensor_tensor(ropesc[:, 512:1024], pr[:],
                                        sin_sb[:, ccols], ALU.mult)
                        v.tensor_tensor(qrT[:, ot, 512 * th:512 * (th + 1)],
                                        ropesc[:, 0:512], ropesc[:, 512:1024],
                                        ALU.add)
                        v.tensor_copy(jnk_v[:, 3:4],
                                      qrT[:, ot, 512 * th:512 * th + 1]).then_inc(s_v, 1)
                    block.vector(_v)
                    ev_rope = ctx.bump("s_v")
                    psr_free[slot][th] = ev_rope
                    qst_free[slot][th] = ev_rope
                wbuf_free[slot] = last_mm

            if tap == "qrT":
                finish_tap(qrT, [128, NH, T_OWN], BF16)
                return nc, tap_dram[0]
            barrier()

        with ExitStack() as ps_st:
            pe = ps_st.enter_context
            ps_scA = pe(nc.psum_tensor("ps_scA", [128, 1024], F32))
            ps_scB = pe(nc.psum_tensor("ps_scB", [128, 1024], F32))
            ps_wA = pe(nc.psum_tensor("ps_wA", [128, 512], F32))
            ps_wB = pe(nc.psum_tensor("ps_wB", [128, 512], F32))

            sigA_done = 0
            sigB_done = 0
            blend_done = 0
            anat_guard = 0
            anat_tr = {}
            heads = [(s_i, h) for s_i in range(SEGS) for h in range(NH)]

            def emit_scores(idx):
                s_i, h = heads[idx]
                qcols = slice(256 * s_i, 256 * (s_i + 1))

                def _t(t, s_i=s_i, h=h, qcols=qcols, sa=sigA_done, sb=sigB_done):
                    if sa:
                        t.wait_ge(s_s, sa)
                    for j in range(4):
                        t.matmul(ps_scA[:, 256 * j:256 * (j + 1)],
                                 krT_sb[:, 128 * (2 * s_i + j):128 * (2 * s_i + j + 1)],
                                 qrT[:, h, qcols], start=True,
                                 stop=True).then_inc(s_mm, 1) if j == 3 else \
                        t.matmul(ps_scA[:, 256 * j:256 * (j + 1)],
                                 krT_sb[:, 128 * (2 * s_i + j):128 * (2 * s_i + j + 1)],
                                 qrT[:, h, qcols], start=True, stop=True)
                    if sb:
                        t.wait_ge(s_s, sb)
                    for j in range(4):
                        m = t.matmul(ps_scB[:, 256 * j:256 * (j + 1)],
                                     krT_sb[:, 128 * (2 * s_i + 2 + j):128 * (2 * s_i + 3 + j)],
                                     qrT[:, h, qcols], start=True, stop=True)
                    m.then_inc(s_mm, 1)
                block.tensor(_t)
                evA = ctx.bump("s_mm")
                evB = ctx.bump("s_mm")
                return evA, evB

            def emit_sig(idx, evA, evB):
                nonlocal sigA_done, sigB_done
                s_i, h = heads[idx]

                sl = sigAB[idx % 2]
                sigA = sl[:, 0:1024]
                sigB = sl[:, 1024:2048]

                def _s(s, s_i=s_i, evA=evA, evB=evB, sigA=sigA, sigB=sigB):
                    s.wait_ge(s_mm, evA)
                    s.activation(sigA, ps_scA[:], AF.Sigmoid,
                                 scale=SCALE).then_inc(s_s, 1)
                    s.wait_ge(s_mm, evB)
                    if s_i == SEGS - 1:
                        s.activation(sigB[:, 0:512], ps_scB[:, 0:512],
                                     AF.Sigmoid, scale=SCALE)
                        s.activation(sigB[:, 512:768], ps_scB[:, 512:768], AF.Sigmoid,
                                     scale=SCALE, bias=mb_sb[:, 0:1])
                        s.activation(sigB[:, 768:1024], ps_scB[:, 768:1024], AF.Sigmoid,
                                     scale=SCALE, bias=mb_sb[:, 1:2])
                    else:
                        s.activation(sigB, ps_scB[:], AF.Sigmoid, scale=SCALE)
                    s.activation(jnk_s[:, 1:2], sigB[:, 1023:1024],
                                 AF.Sigmoid).then_inc(s_s, 1)
                block.scalar(_s)
                sigA_done = ctx.bump("s_s")
                sigB_done = ctx.bump("s_s")

            def emit_wout(idx, sd):
                s_i, h = heads[idx]
                sl = sigAB[idx % 2]
                sigA = sl[:, 0:1024]
                sigB = sl[:, 1024:2048]

                def _t2(t, s_i=s_i, sd=sd, bd=blend_done, sigA=sigA, sigB=sigB):
                    t.wait_ge(s_s, sd)
                    if bd:
                        t.wait_ge(s_v, bd)
                    for qc in range(2):
                        off = 132 * qc
                        for j in range(4):
                            t.matmul(ps_wA[:, off:off + 129],
                                     sigA[:, 256 * j + 128 * qc:256 * j + 128 * (qc + 1)],
                                     vaug_sb[:, 2 * s_i + j, 0:129],
                                     start=(j == 0), stop=(j == 3))
                        for j in range(4):
                            m = t.matmul(ps_wB[:, off:off + 129],
                                         sigB[:, 256 * j + 128 * qc:256 * j + 128 * (qc + 1)],
                                         vaug_sb[:, 2 * s_i + 2 + j, 0:129],
                                         start=(j == 0), stop=(j == 3))
                    m.then_inc(s_mm, 1)
                block.tensor(_t2)
                return ctx.bump("s_mm")

            def emit_blend(idx, ev_w2):
                nonlocal blend_done
                s_i, h = heads[idx]

                def _v(v, s_i=s_i, h=h, ev=ev_w2, ag=anat_guard):
                    v.wait_ge(s_mm, ev)
                    if ag and h == 0:
                        v.wait_ge(s_tr, ag)
                    sA2 = ps_wA[:, 0:264].rearrange("p (a b) -> p a b", a=2, b=132)[
                        :, :, 128:129].rearrange("p a b -> p (a b)")
                    sB2 = ps_wB[:, 0:264].rearrange("p (a b) -> p a b", a=2, b=132)[
                        :, :, 128:129].rearrange("p a b -> p (a b)")
                    v.reciprocal(bl_r[:, 0:2], sA2)
                    v.reciprocal(bl_r[:, 2:4], sB2)
                    vsync(v)
                    acols = slice(2 * s_i, 2 * s_i + 2)
                    v.tensor_tensor(bl_r[:, 4:6], bl_r[:, 0:2], aA_sb[:, acols], ALU.mult)
                    v.tensor_tensor(bl_r[:, 6:8], bl_r[:, 2:4], aB_sb[:, acols], ALU.mult)
                    vsync(v)
                    for qc in range(2):
                        ts1(v, bl_t[:, 128 * qc:128 * (qc + 1)],
                            ps_wA[:, 132 * qc:132 * qc + 128],
                            bl_r[:, 4 + qc:5 + qc], ALU.mult)
                    for qc in range(2):
                        v.scalar_tensor_tensor(
                            anat[qc][:, 128 * h:128 * (h + 1)],
                            ps_wB[:, 132 * qc:132 * qc + 128],
                            bl_r[:, 6 + qc:7 + qc],
                            bl_t[:, 128 * qc:128 * (qc + 1)],
                            ALU.mult, ALU.add)
                    v.tensor_copy(jnk_v[:, 3:4],
                                  anat[1][:, 128 * h:128 * h + 1]).then_inc(s_v, 1)
                block.vector(_v)
                blend_done = ctx.bump("s_v")

            pend = None
            for idx in range(len(heads)):
                evA, evB = emit_scores(idx)
                emit_sig(idx, evA, evB)
                my_sig = sigB_done
                if pend is not None:
                    ev_w2 = emit_wout(pend[0], pend[1])
                    emit_blend(pend[0], ev_w2)
                pend = (idx, my_sig)
                s_i, h = heads[idx]
                if h == NH - 1 and pend is not None:
                    ev_w2 = emit_wout(pend[0], pend[1])
                    emit_blend(pend[0], ev_w2)
                    pend = None
                    for qc in range(2):
                        ti = 2 * s_i + qc

                        def _tr(sy, ti=ti, qc=qc, bd=blend_done):
                            sy.wait_ge(s_v, bd)
                            sy.dma_start_transpose(
                                attnT[:, :, 128 * ti:128 * (ti + 1)],
                                anat[qc]).then_inc(s_tr, 16)
                        block.sync(_tr)
                        anat_tr[ti] = ctx.bump("s_tr", 16)
                    anat_guard = anat_tr[2 * s_i + 1]

            if tap == "wlast":
                wdbg = vf(RE + 8192, T_OWN)

                def _vdbg(v, bd=blend_done):
                    v.wait_ge(s_v, bd)
                    v.tensor_copy(wdbg[:, 0:264], ps_wA.bitcast(F32)[:, 0:264])
                    v.tensor_copy(wdbg[:, 264:528], ps_wB.bitcast(F32)[:, 0:264]).then_inc(s_v, 1)
                block.vector(_vdbg)
                ctx.bump("s_v")
                finish_tap(wdbg, [128, T_OWN], F32)
                return nc, tap_dram[0]
            if tap == "sclast":
                wdbg2 = vf(RE + 8192, T_OWN)

                def _vdbg2(v, ev=ev_sc):
                    v.wait_ge(s_mm, ev)
                    v.tensor_copy(wdbg2[:], ps_scB[:]).then_inc(s_v, 1)
                block.vector(_vdbg2)
                ctx.bump("s_v")
                finish_tap(wdbg2, [128, T_OWN], F32)
                return nc, tap_dram[0]
            if tap == "siglast":
                finish_tap(sigB, [128, T_OWN], BF16)
                return nc, tap_dram[0]
            if tap == "attnT":
                finish_tap(attnT, [128, NH, T_OWN], BF16)
                return nc, tap_dram[0]
            attn_tr_last = anat_tr[2 * SEGS - 1]
            barrier()

        def big_proj(wdram, rhs_fn, dst, delta_fn=None, first_wait=None):
            with ExitStack() as ps_st2:
                pe2 = ps_st2.enter_context
                ps_p = [pe2(nc.psum_tensor(f"ps_p{ctx.bump('psn')}", [128, 512], F32))
                        for _ in range(2)]
                wb_free = [0, 0]
                ps_free = [0, 0]
                for ot in range(FC):
                    slot = ot % 2

                    def _g(g, ot=ot, slot=slot, w=wb_free[slot]):
                        if w:
                            g.wait_ge(s_mm, w)
                        g.dma_start(wbuf[slot], wdram[ot]).then_inc(s_w, 16)
                    block.gpsimd(_g)
                    ev_w = ctx.bump("s_w", 16)

                    last = None
                    for th in range(2):
                        def _t(t, ot=ot, th=th, slot=slot, ev_w=ev_w, pf=ps_free[th]):
                            if ot == 0 and th == 0 and first_wait:
                                for sm, vv in first_wait:
                                    t.wait_ge(sm, vv)
                            t.wait_ge(s_w, ev_w)
                            if pf:
                                t.wait_ge(s_s, pf)
                            first = True
                            if delta_fn is not None:
                                delta_fn(t, ps_p[th], ot, th)
                                first = False
                            for fc in range(FC):
                                m = t.matmul(ps_p[th][:], wbuf[slot][:, fc, :],
                                             rhs_fn(fc, th),
                                             start=(first and fc == 0),
                                             stop=(fc == FC - 1))
                            m.then_inc(s_mm, 1)
                        block.tensor(_t)
                        ev_mm = ctx.bump("s_mm")
                        last = ev_mm

                        def _s(s, ot=ot, th=th, ev=ev_mm):
                            s.wait_ge(s_mm, ev)
                            s.activation(dst[:, ot, 512 * th:512 * (th + 1)],
                                         ps_p[th][:], AF.Identity)
                            s.activation(jnk_s[:, 1:2],
                                         dst[:, ot, 512 * th:512 * th + 1],
                                         AF.Identity).then_inc(s_s, 1)
                        block.scalar(_s)
                        ps_free[th] = ctx.bump("s_s")
                    wb_free[slot] = last

        big_proj(wo_t,
                 lambda fc, th: attnT[:, fc, 512 * th:512 * (th + 1)],
                 projT, first_wait=[(s_tr, attn_tr_last)])
        if tap == "projT":
            finish_tap(projT, [128, NH, T_OWN], BF16)
            return nc, tap_dram[0]
        barrier()

        def resid_phase(srcT, natv, is_final):
            ptr = 0
            for ot in range(FC):
                def _tr(sy, ot=ot):
                    sy.dma_start_transpose(natv[:, :, 128 * ot:128 * (ot + 1)],
                                          srcT[:, ot, :]).then_inc(s_tr, 16)
                block.sync(_tr)
                ptr = ctx.bump("s_tr", 16)

            xn_free = None
            prev_add = 0
            ln2_tr_local = {}
            dumpR = vb(RC + 8192, D) if not is_final else vb(RC, D)
            for i in range(NT_OWN):
                slot = 0

                def _ld(sy, i=i, ptr=ptr, xe=xn_free):
                    if i == 0:
                        sy.wait_ge(s_tr, ptr)
                        if is_final:
                            sy.wait_ge(s_res, ctx.val("s_res"))
                    if xe is not None:
                        if is_final:
                            sy.wait_ge(s_out, xe)
                        else:
                            sy.wait_ge(s_res, xe[0])
                            sy.wait_ge(s_s, xe[1])
                    src = (xres[128 * i:128 * (i + 1), :] if is_final
                           else xslab[256 + 128 * i:256 + 128 * (i + 1), :])
                    sy.dma_start(xnf, src).then_inc(s_x, 16)
                block.gpsimd(_ld)
                ev_x = ctx.bump("s_x", 16)

                pre = [(s_tr, ptr)] if i == 0 else None
                pre3 = [(s_v, prev_add)] if prev_add else None
                ev_t1 = ln_tile(natv[:, i, :], i % 4, t1f, dumpR,
                                pre_s=pre, pre_v=pre, pre_s3=pre3)

                def _v3(v, ev_t1=ev_t1, ev_x=ev_x):
                    v.wait_ge(s_s, ev_t1)
                    v.wait_ge(s_x, ev_x)
                    v.tensor_tensor(xnf, xnf, t1f, ALU.add)
                    v.tensor_copy(jnk_v[:, 3:4], xnf[:, 0:1]).then_inc(s_v, 1)
                block.vector(_v3)
                ev_xn = ctx.bump("s_v")
                prev_add = ev_xn

                if is_final:
                    def _st(g, i=i, ev=ev_xn):
                        g.wait_ge(s_v, ev)
                        g.dma_start(out[128 * i:128 * (i + 1), :],
                                    xnf).then_inc(s_out, 16)
                    block.gpsimd(_st)
                    xn_free = ctx.bump("s_out", 16)
                else:
                    def _st(g, i=i, ev=ev_xn):
                        g.wait_ge(s_v, ev)
                        g.dma_start(xres[128 * i:128 * (i + 1), :],
                                    xnf).then_inc(s_res, 16)
                    block.gpsimd(_st)
                    ev_res = ctx.bump("s_res", 16)

                    pre3b = [(s_tr, ln2_tr_local[i - 1])] if i >= 1 else None
                    ev_n2 = ln_tile(xnf, 4 + i % 4, lnb[slot], dumpR,
                                    pre_s=[(s_v, ev_xn)], pre_v=[(s_v, ev_xn)],
                                    pre_s3=pre3b)

                    def _tr2(sy, i=i, slot=slot, ev=ev_n2):
                        sy.wait_ge(s_s, ev)
                        sy.dma_start_transpose(ln2T[:, :, 128 * i:128 * (i + 1)],
                                              lnb[slot]).then_inc(s_tr, 16)
                    block.sync(_tr2)
                    ln2_tr_local[i] = ctx.bump("s_tr", 16)
                    xn_free = (ev_res, ev_n2)
            return ln2_tr_local.get(NT_OWN - 1, 0)

        ln2_tr_last = resid_phase(projT, nat8, is_final=False)
        if tap == "ln2T":
            finish_tap(ln2T, [128, FC, T_OWN], BF16)
            return nc, tap_dram[0]
        barrier()

        with ExitStack() as ps_st:
            pe = ps_st.enter_context
            ps_sh = pe(nc.psum_tensor("ps_sh", [64, T_OWN], F32))
            ps_tpf = pe(nc.psum_tensor("ps_tpf", [128, 128], F32))
            ps_oh = pe(nc.psum_tensor("ps_oh", [4, T_OWN], F32))
            ps_ex = pe(nc.psum_tensor("ps_ex", [32, T_OWN], F32))

            def _gms(g):
                g.memset(lgT, 0.0).then_inc(s_g, 1)
            block.gpsimd(_gms)
            ev_ms = ctx.bump("s_g", 1)

            def _tf1(t):
                t.wait_ge(s_tr, ln2_tr_last)
                for th in range(2):
                    for fc in range(FC):
                        m = t.matmul(ps_sh[:, 512 * th:512 * (th + 1)], sh_sb[:, fc, :],
                                     ln2T[:, fc, 512 * th:512 * (th + 1)],
                                     start=(fc == 0), stop=(fc == FC - 1))
                m.then_inc(s_mm, 1)
            block.tensor(_tf1)
            ev_sh = ctx.bump("s_mm")

            def _vf2(v, ev=ev_sh, evm=ev_ms):
                v.wait_ge(s_mm, ev)
                v.wait_ge(s_g, evm)
                ts1(v, lgT[32:36, :], ps_sh[32:36, :], eb_sb[32:36, :], ALU.add)
                v.tensor_copy(jnk_v[0:4, 3:4], lgT[32:36, 0:1]).then_inc(s_v, 1)
            block.vector(_vf2)
            ev_lg = ctx.bump("s_v")

            cp_ev = 0
            for j in range(NT_OWN):
                def _t(t, j=j, ev=ev_lg, ce=cp_ev):
                    if j == 0:
                        t.wait_ge(s_v, ev)
                    if ce:
                        t.wait_ge(s_v, ce)
                    t.transpose(ps_tpf[:], lgT[:, 128 * j:128 * (j + 1)],
                                idf_sb[:]).then_inc(s_mm, 1)
                block.tensor(_t)
                ev_tp = ctx.bump("s_mm")

                def _v(v, j=j, ev=ev_tp):
                    v.wait_ge(s_mm, ev)
                    v.tensor_copy(lgN[:, j, :], ps_tpf[:, 32:36])
                    v.tensor_copy(jnk_v[:, 3:4], lgN[:, j, 0:1]).then_inc(s_v, 1)
                block.vector(_v)
                cp_ev = ctx.bump("s_v")

            def _vcas(v, ce=cp_ev):
                v.wait_ge(s_v, ce)
                for j in range(NT_OWN):
                    v.reduce_max(mcol[:, j:j + 1], lgN[:, j, :], axis=AX)
                vsync(v)
                for j in range(NT_OWN):
                    ts1(v, fflg8[:, j, :], lgN[:, j, :], mcol[:, j:j + 1], ALU.is_equal)
                vsync(v)
                for j in range(NT_OWN):
                    v.tensor_copy(ohn[:, j, 0:1], fflg8[:, j, 0:1])
                for j in range(NT_OWN):
                    v.tensor_scalar(ntA[:, j:j + 1], fflg8[:, j, 0:1], -1.0, 1.0,
                                    ALU.mult, ALU.add)
                vsync(v)
                for j in range(NT_OWN):
                    v.tensor_tensor(ohn[:, j, 1:2], fflg8[:, j, 1:2], ntA[:, j:j + 1],
                                    ALU.mult)
                for j in range(NT_OWN):
                    v.tensor_scalar(ntB[:, j:j + 1], fflg8[:, j, 1:2], -1.0, 1.0,
                                    ALU.mult, ALU.add)
                vsync(v)
                for j in range(NT_OWN):
                    v.tensor_tensor(ntC[:, j:j + 1], ntA[:, j:j + 1], ntB[:, j:j + 1],
                                    ALU.mult)
                vsync(v)
                for j in range(NT_OWN):
                    v.tensor_tensor(ohn[:, j, 2:3], fflg8[:, j, 2:3], ntC[:, j:j + 1],
                                    ALU.mult)
                for j in range(NT_OWN):
                    v.tensor_scalar(ntB[:, j:j + 1], fflg8[:, j, 2:3], -1.0, 1.0,
                                    ALU.mult, ALU.add)
                vsync(v)
                for j in range(NT_OWN):
                    v.tensor_tensor(ntA[:, j:j + 1], ntC[:, j:j + 1], ntB[:, j:j + 1],
                                    ALU.mult)
                vsync(v)
                for j in range(NT_OWN):
                    v.tensor_tensor(ohn[:, j, 3:4], fflg8[:, j, 3:4], ntA[:, j:j + 1],
                                    ALU.mult)
                v.tensor_copy(jnk_v[:, 1:2], ohn[:, NT_OWN - 1, 3:4]).then_inc(s_v, 1)
            block.vector(_vcas)
            oh_ev = ctx.bump("s_v")

            tpb_last = 0
            for j in range(NT_OWN):
                def _t(t, j=j, ev=oh_ev):
                    if j == 0:
                        t.wait_ge(s_v, ev)
                    t.transpose(ps_oh[:, 128 * j:128 * (j + 1)], ohn[:, j, :],
                                idf_sb[:]).then_inc(s_mm, 1)
                block.tensor(_t)
                tpb_last = ctx.bump("s_mm")

            def _vf3(v, ev=tpb_last):
                v.wait_ge(s_mm, ev)
                v.tensor_copy(ohT_sb[0:4, :], ps_oh[:])
                v.tensor_copy(jnk_v[0:4, 3:4], ohT_sb[0:4, 0:1]).then_inc(s_v, 1)
            block.vector(_vf3)
            ev_ohT = ctx.bump("s_v")

            def _tf4(t, ev=ev_ohT):
                t.wait_ge(s_v, ev)
                for th in range(2):
                    m = t.matmul(ps_ex[:, 512 * th:512 * (th + 1)], ee_sb[:],
                                 ohT_sb[0:4, 512 * th:512 * (th + 1)], start=True, stop=True)
                m.then_inc(s_mm, 1)
            block.tensor(_tf4)
            ev_ex = ctx.bump("s_mm")

            def _vf5(v, ev=ev_ex):
                v.wait_ge(s_mm, ev)
                v.tensor_copy(ex_sb[0:32, :], ps_ex[:])
                vsync(v)
                v.tensor_tensor(zmT[0:32, :], ps_sh[0:32, :], ex_sb[0:32, :], ALU.mult)
                v.tensor_copy(jnk_v[0:32, 3:4], zmT[0:32, 0:1]).then_inc(s_v, 1)
            block.vector(_vf5)
            ev_zm = ctx.bump("s_v")

            def _tf6(t, ev=ev_zm):
                t.wait_ge(s_v, ev)
            block.tensor(_tf6)

            if tap == "zmT":
                finish_tap(zmT[0:32, :], [32, T_OWN], BF16)
                return nc, tap_dram[0]

        def delta_fn(t, ps, ot, th):
            t.matmul(ps[:], bfl_sb[:, 128 * ot:128 * (ot + 1)],
                     zmT[0:32, 512 * th:512 * (th + 1)], start=True, stop=False)

        big_proj(wup_t,
                 lambda fc, th: ln2T[:, fc, 512 * th:512 * (th + 1)],
                 ffnT, delta_fn=delta_fn, first_wait=[(s_tr, ln2_tr_last)])

        if tap == "ffnT":
            finish_tap(ffnT, [128, NH, T_OWN], BF16)
            return nc, tap_dram[0]
        barrier()

        resid_phase(ffnT, natG, is_final=True)

        def _fin(sy):
            sy.wait_ge(s_out, ctx.val("s_out"))
        block.sync(_fin)

    return nc, tap_dram[0]



def _rope_cache(S_, D_, base=10000.0):
    inv_freq = 1.0 / base ** (np.arange(0, D_, 2, dtype=np.float32) / D_)
    t = np.arange(S_, dtype=np.float32)
    f = np.outer(t, inv_freq)
    cos = np.zeros((S_, D_), dtype=np.float32)
    sin = np.zeros((S_, D_), dtype=np.float32)
    cos[:, 0::2] = np.cos(f)
    cos[:, 1::2] = np.cos(f)
    sin[:, 0::2] = np.sin(f)
    sin[:, 1::2] = np.sin(f)
    return cos, sin


def _tile_w(wT):
    f, o = wT.shape
    if o <= 128:
        return np.ascontiguousarray(wT.reshape(FC, 128, o).transpose(1, 0, 2))
    return np.ascontiguousarray(
        wT.reshape(FC, 128, o // 128, 128).transpose(2, 1, 0, 3))


def make_inputs(x, Wq, Wk, Wv, Wo, W_up, gate_w, expert_bias, lora_A, lora_B):
    cos, sin = _rope_cache(S, DH)
    wq_t = _tile_w(np.ascontiguousarray(Wq.T))
    wo_t = _tile_w(np.ascontiguousarray(Wo.T))
    wup_t = _tile_w(np.ascontiguousarray(W_up.T))
    wk_t = _tile_w(np.ascontiguousarray(Wk.T))
    wv_t = _tile_w(np.ascontiguousarray(Wv.T))
    shW = np.zeros((D, 64), np.float32)
    shW[:, 0:32] = lora_A.reshape(32, D).T
    shW[:, 32:36] = gate_w.T
    sh_t = _tile_w(shW)
    bflat = np.ascontiguousarray(lora_B.transpose(0, 2, 1).reshape(32, D))
    eexp = np.zeros((4, 32), np.float32)
    for ei in range(4):
        eexp[ei, ei * 8:(ei + 1) * 8] = 1.0
    identf = np.eye(128, dtype=np.float32)
    pmat = np.zeros((128, 128), np.float32)
    for d_ in range(64):
        pmat[d_, d_ + 64] = -1.0
        pmat[d_ + 64, d_] = 1.0
    ptmat = np.ascontiguousarray(pmat.T)
    lin = np.linspace(0.0, 1.0, 256, dtype=np.float32)

    maps = []
    for c in range(8):
        b, j = divmod(c, 4)
        g0 = 1024 * j - 256
        xs = np.zeros((T_SLAB, D), np.float32)
        lo, hi = max(0, g0), min(S, g0 + T_SLAB)
        xs[lo - g0:hi - g0, :] = x[b, lo:hi, :]
        ct = np.zeros((128, T_SLAB), np.float32)
        sn = np.zeros((128, T_SLAB), np.float32)
        ct[:, lo - g0:hi - g0] = cos[lo:hi].T
        sn[:, lo - g0:hi - g0] = sin[lo:hi].T
        aA = np.zeros((128, 2 * SEGS), np.float32)
        aB = np.zeros((128, 2 * SEGS), np.float32)
        for s_i in range(SEGS):
            g = 4 * j + s_i
            ab = np.ones(256, np.float32) if g == 0 else lin
            for qc in range(2):
                aB[:, 2 * s_i + qc] = ab[128 * qc:128 * (qc + 1)]
                aA[:, 2 * s_i + qc] = np.float32(1.0) - ab[128 * qc:128 * (qc + 1)]
        mb = np.zeros((128, 2), np.float32)
        if j == 3:
            mb[:] = -200.0
        maps.append(dict(
            xslab=xs, wq_t=wq_t, wo_t=wo_t, wup_t=wup_t, wk_t=wk_t, wv_t=wv_t,
            sh_t=sh_t, bflat=bflat,
            ebias=np.asarray(expert_bias, np.float32).reshape(4, 1),
            eexp=eexp, identf=identf, ptmat=ptmat, cosT=ct, sinT=sn,
            alphaA=aA, alphaB=aB, maskb=mb))
    return maps


def kernel(L0_attn_x, L0_attn_Wq, L0_attn_Wk, L0_attn_Wv, L0_attn_Wo,
           L0_ffn_W_up, L0_ffn_gate_w, L0_ffn_expert_bias, L0_ffn_lora_A,
           L0_ffn_lora_B):
    x = np.asarray(L0_attn_x, np.float32)
    maps = make_inputs(x, np.asarray(L0_attn_Wq, np.float32),
                       np.asarray(L0_attn_Wk, np.float32),
                       np.asarray(L0_attn_Wv, np.float32),
                       np.asarray(L0_attn_Wo, np.float32),
                       np.asarray(L0_ffn_W_up, np.float32),
                       np.asarray(L0_ffn_gate_w, np.float32),
                       np.asarray(L0_ffn_expert_bias, np.float32),
                       np.asarray(L0_ffn_lora_A, np.float32),
                       np.asarray(L0_ffn_lora_B, np.float32))
    nc, _ = build_program()
    res = run_bass_kernel_spmd(nc, maps, core_ids=list(range(8)))
    full = np.empty((B, S, D), np.float32)
    for c in range(8):
        b, j = divmod(c, 4)
        full[b, 1024 * j:1024 * (j + 1), :] = res.results[c]["out"]
    return full


# version 17
# speedup vs baseline: 1.3700x; 1.0025x over previous
import math
import numpy as np
from contextlib import ExitStack

import concourse.bass as bass
import concourse.mybir as mybir
from concourse.bass_utils import run_bass_kernel_spmd

F32 = mybir.dt.float32
BF16 = mybir.dt.bfloat16
AF = mybir.ActivationFunctionType
ALU = mybir.AluOpType
AX = mybir.AxisListType.X

B, S, D = 2, 4096, 4096
NH, DH = 32, 128
FC = D // 128
T_OWN = 1024
T_SLAB = 1536
NT_OWN = 8
NT_SLAB = 12
SEGS = 4
SCALE = 1.0 / math.sqrt(DH)
EPS = 1e-5

RA = 0
RB = 65536
RC = 131072
RE = 147456
RF = 163840
ARENA_BYTES = 180224


class Ctx:
    def __init__(self):
        self.cnt = {}

    def bump(self, name, k=1):
        self.cnt[name] = self.cnt.get(name, 0) + k
        return self.cnt[name]

    def val(self, name):
        return self.cnt.get(name, 0)


def build_program(tap=None):
    nc = bass.Bass("TRN2", target_bir_lowering=False)

    xslab = nc.dram_tensor("xslab", [T_SLAB, D], F32, kind="ExternalInput").ap()
    wq_t = nc.dram_tensor("wq_t", [FC, 128, FC, 128], F32, kind="ExternalInput").ap()
    wo_t = nc.dram_tensor("wo_t", [FC, 128, FC, 128], F32, kind="ExternalInput").ap()
    wup_t = nc.dram_tensor("wup_t", [FC, 128, FC, 128], F32, kind="ExternalInput").ap()
    wk_t = nc.dram_tensor("wk_t", [128, FC, 128], F32, kind="ExternalInput").ap()
    wv_t = nc.dram_tensor("wv_t", [128, FC, 128], F32, kind="ExternalInput").ap()
    sh_t = nc.dram_tensor("sh_t", [128, FC, 64], F32, kind="ExternalInput").ap()
    bflat = nc.dram_tensor("bflat", [32, D], F32, kind="ExternalInput").ap()
    ebias = nc.dram_tensor("ebias", [4, 1], F32, kind="ExternalInput").ap()
    eexp = nc.dram_tensor("eexp", [4, 32], F32, kind="ExternalInput").ap()
    identf = nc.dram_tensor("identf", [128, 128], F32, kind="ExternalInput").ap()
    ptmat = nc.dram_tensor("ptmat", [128, 128], F32, kind="ExternalInput").ap()
    cosT = nc.dram_tensor("cosT", [128, T_SLAB], F32, kind="ExternalInput").ap()
    sinT = nc.dram_tensor("sinT", [128, T_SLAB], F32, kind="ExternalInput").ap()
    alphaA = nc.dram_tensor("alphaA", [128, 2 * SEGS], F32, kind="ExternalInput").ap()
    alphaB = nc.dram_tensor("alphaB", [128, 2 * SEGS], F32, kind="ExternalInput").ap()
    maskb = nc.dram_tensor("maskb", [128, 2], F32, kind="ExternalInput").ap()

    out = nc.dram_tensor("out", [T_OWN, D], F32, kind="ExternalOutput").ap()
    xres = nc.dram_tensor("xres", [T_OWN, D], F32, kind="Internal").ap()
    tap_dram = [None]

    ctx = Ctx()
    st = ExitStack()
    e = st.enter_context

    with st:
        s_x = e(nc.semaphore("s_x"))
        s_w = e(nc.semaphore("s_w"))
        s_tr = e(nc.semaphore("s_tr"))
        s_mm = e(nc.semaphore("s_mm"))
        s_v = e(nc.semaphore("s_v"))
        s_s = e(nc.semaphore("s_s"))
        s_res = e(nc.semaphore("s_res"))
        s_out = e(nc.semaphore("s_out"))
        s_bar = e(nc.semaphore("s_bar"))
        s_g = e(nc.semaphore("s_g"))
        s_x2 = e(nc.semaphore("s_x2"))

        cos_sb = e(nc.sbuf_tensor("cos_sb", [128, T_SLAB], F32))
        sin_sb = e(nc.sbuf_tensor("sin_sb", [128, T_SLAB], F32))
        aA_sb = e(nc.sbuf_tensor("aA_sb", [128, 2 * SEGS], F32))
        aB_sb = e(nc.sbuf_tensor("aB_sb", [128, 2 * SEGS], F32))
        mb_sb = e(nc.sbuf_tensor("mb_sb", [128, 2], F32))
        idb_sb = e(nc.sbuf_tensor("idb_sb", [128, 128], BF16))
        idf_sb = e(nc.sbuf_tensor("idf_sb", [128, 128], F32))
        pt_sb = e(nc.sbuf_tensor("pt_sb", [128, 128], BF16))
        eb_sb = e(nc.sbuf_tensor("eb_sb", [36, 1], F32))
        ee_sb = e(nc.sbuf_tensor("ee_sb", [4, 32], BF16))
        sh_sb = e(nc.sbuf_tensor("sh_sb", [128, FC, 64], BF16))
        bfl_sb = e(nc.sbuf_tensor("bfl_sb", [32, D], BF16))
        krT_sb = e(nc.sbuf_tensor("krT_sb", [128, T_SLAB], BF16))
        vaug_sb = e(nc.sbuf_tensor("vaug_sb", [128, NT_SLAB, 132], BF16))
        st_sum = e(nc.sbuf_tensor("st_sum", [128, 8], F32))
        st_ssq = e(nc.sbuf_tensor("st_ssq", [128, 8], F32))
        st_mu = e(nc.sbuf_tensor("st_mu", [128, 8], F32))
        st_var = e(nc.sbuf_tensor("st_var", [128, 8], F32))
        st_rstd = e(nc.sbuf_tensor("st_rstd", [128, 8], F32))
        st_nb = e(nc.sbuf_tensor("st_nb", [128, 8], F32))
        st_nmu = e(nc.sbuf_tensor("st_nmu", [128, 8], F32))
        st_musq = e(nc.sbuf_tensor("st_musq", [128, 8], F32))
        st_var2 = e(nc.sbuf_tensor("st_var2", [128, 8], F32))
        st_rvar = e(nc.sbuf_tensor("st_rvar", [128, 8], F32))
        jnk_v = e(nc.sbuf_tensor("jnk_v", [128, 4], F32))
        jnk_s = e(nc.sbuf_tensor("jnk_s", [128, 4], F32))
        bl_r = e(nc.sbuf_tensor("bl_r", [128, 8], F32))

        arena = e(nc.sbuf_tensor("arena", [128, ARENA_BYTES // 2], BF16))

        def vb(off, *dims):
            n = 1
            for d_ in dims:
                n *= d_
            ap = arena[:, off // 2: off // 2 + n]
            if len(dims) == 2:
                ap = ap.rearrange("p (a b) -> p a b", a=dims[0], b=dims[1])
            return ap

        def vf(off, *dims):
            n = 1
            for d_ in dims:
                n *= d_
            ap = arena[:, off // 2: off // 2 + 2 * n].bitcast(F32)
            if len(dims) == 2:
                ap = ap.rearrange("p (a b) -> p a b", a=dims[0], b=dims[1])
            return ap

        ln1T = vb(RA, FC, T_OWN)
        attnT = vb(RA, NH, T_OWN)
        nat8 = vb(RA, NT_OWN, D)
        ffnT = vb(RA, NH, T_OWN)
        wk_sb = vb(RB, FC, 128)
        wv_sb = vb(RB + 8192, FC, 128)
        xload2 = [vf(RB + 16384, D), vf(RB + 32768, D), vf(RB + 49152, D)]
        qrT = vb(RB, NH, T_OWN)
        projT = vb(RB, NH, T_OWN)
        ln2T = vb(RB, FC, T_OWN)
        natG = vb(RB, NT_OWN, D)
        lnb = [vb(RC, D), vb(RC + 8192, D)]
        wbuf = [vb(RC, FC, 128), vb(RC + 8192, FC, 128)]
        anat = [vb(RC, D), vb(RC + 8192, D)]
        halotmp = [vb(RE, FC, 128), vb(RE + 8192, FC, 128)]
        rstage = vb(RE, T_OWN)
        sigAB = [vb(RE, 2 * T_OWN), vb(RE + 4096, 2 * T_OWN)]
        xnf = vf(RE, D)
        lgT = vf(RE, T_OWN)
        ohT_sb = vb(RE + 4096, T_OWN)
        ex_sb = vf(RE + 6144, T_OWN)
        zmT = vb(RE + 10240, T_OWN)
        mcol = vf(RE + 12288, NT_OWN)
        fflg = vf(RE + 12320, 4)
        ntmp = vf(RE + 12336, 4)
        ohn = vf(RE + 12352, NT_OWN, 4)
        lgN = vf(RE + 12480, NT_OWN, 4)
        fflg8 = vf(RE + 12608, NT_OWN, 4)
        ntA = vf(RE + 12736, NT_OWN)
        ntB = vf(RE + 12768, NT_OWN)
        ntC = vf(RE + 12800, NT_OWN)
        bl_t = vf(RE + 8192, 256)
        dumpA = vb(RF, D)
        vstage = [vb(RF + 8192, 128), vb(RF + 8448, 128)]
        kstage = [vb(RF + 8704, 128), vb(RF + 8960, 128)]
        ropesc = vf(RF + 9216, T_OWN)
        t1f = vf(RF, D)

        block = e(nc.Block())
        ENGS = ("tensor", "vector", "scalar", "sync", "gpsimd")

        def barrier():
            tgt = ctx.bump("s_bar", len(ENGS))
            for en in ENGS:
                def _f(eng, tgt=tgt):
                    eng.sem_inc(s_bar, 1)
                    eng.wait_ge(s_bar, tgt)
                getattr(block, en)(_f)

        def finish_tap(src_ap, shape, dtype):
            tap_dram[0] = nc.dram_tensor("tap", shape, dtype, kind="ExternalOutput").ap()
            barrier()

            def _f(sy):
                sy.dma_start(tap_dram[0], src_ap).then_inc(s_out, 16)
                sy.wait_ge(s_out, ctx.bump("s_out", 16))
            block.sync(_f)

        def ts1(eng, o, i0, s1, op):
            return eng.tensor_scalar(o, i0, s1, 0.0, op, ALU.add)

        def vsync(v):
            v.tensor_copy(jnk_v[:, 2:3], jnk_v[:, 2:3]).then_inc(s_v, 1)
            v.wait_ge(s_v, ctx.bump("s_v"))

        def ssync(s):
            s.activation(jnk_s[:, 0:1], jnk_s[:, 0:1], AF.Identity).then_inc(s_s, 1)
            s.wait_ge(s_s, ctx.bump("s_s"))

        def p0_sync(sy):
            for dst, src in ((cos_sb[:], cosT), (sin_sb[:], sinT), (aA_sb[:], alphaA),
                             (aB_sb[:], alphaB), (mb_sb[:], maskb), (idf_sb[:], identf),
                             (eb_sb[32:36, :], ebias)):
                sy.dma_start(dst, src).then_inc(s_x, 16)
                ctx.bump("s_x", 16)
        block.sync(p0_sync)

        def p0_g(g):
            for dst, src in ((idb_sb[:], identf), (pt_sb[:], ptmat), (ee_sb[:], eexp),
                             (wk_sb, wk_t), (wv_sb, wv_t), (sh_sb[:], sh_t),
                             (bfl_sb[:], bflat)):
                g.dma_start(dst, src).then_inc(s_w, 16)
                ctx.bump("s_w", 16)
            g.memset(vaug_sb[:, :, 128:129], 1.0)
            g.memset(jnk_v[:], 0.0)
            g.memset(jnk_s[:], 0.0).then_inc(s_g, 1)
            ctx.bump("s_g", 1)
        block.gpsimd(p0_g)
        P0_W = ctx.val("s_w")
        P0_G = ctx.val("s_g")

        def ln_sq(src_ap, col, dump_ap, pre_s=None):
            c = slice(col, col + 1)

            def _sc(s):
                if pre_s:
                    for sm, v in pre_s:
                        s.wait_ge(sm, v)
                s.activation(dump_ap, src_ap, AF.Square,
                             accum_out=st_ssq[:, c]).then_inc(s_s, 1)
                s.wait_ge(s_s, ctx.bump("s_s"))
                s.activation(jnk_s[:, 0:1], st_ssq[:, c], AF.Identity).then_inc(s_s, 1)
            block.scalar(_sc)
            return ctx.bump("s_s")

        def ln_vstats(src_ap, col, ev_ssq, pre_v=None):
            c = slice(col, col + 1)

            def _v(v, ev_ssq=ev_ssq):
                if pre_v:
                    for sm, vv in pre_v:
                        v.wait_ge(sm, vv)
                v.reduce_sum(st_sum[:, c], src_ap, axis=AX)
                vsync(v)
                ts1(v, st_mu[:, c], st_sum[:, c], 1.0 / D, ALU.mult)
                ts1(v, st_nmu[:, c], st_sum[:, c], -1.0 / D, ALU.mult)
                v.wait_ge(s_s, ev_ssq)
                v.tensor_scalar(st_var[:, c], st_ssq[:, c], 1.0 / D, EPS,
                                ALU.mult, ALU.add)
                v.tensor_tensor(st_musq[:, c], st_mu[:, c], st_mu[:, c], ALU.mult)
                vsync(v)
                v.scalar_tensor_tensor(st_var2[:, c], st_musq[:, c], -1.0,
                                       st_var[:, c], ALU.mult, ALU.add)
                vsync(v)
                v.reciprocal(st_rvar[:, c], st_var2[:, c])
                v.tensor_copy(jnk_v[:, 0:1], st_rvar[:, c]).then_inc(s_v, 1)
            block.vector(_v)
            return ctx.bump("s_v")

        def ln_fin(src_ap, col, dst_ap, ev_rv, pre_s3=None):
            c = slice(col, col + 1)

            def _sc3(s, ev=ev_rv):
                s.wait_ge(s_v, ev)
                if pre_s3:
                    for sm, vv in pre_s3:
                        s.wait_ge(sm, vv)
                s.activation(st_rstd[:, c], st_rvar[:, c], AF.Sqrt)
                ssync(s)
                s.activation(st_nb[:, c], st_rstd[:, c], AF.Identity,
                             scale=st_nmu[:, c])
                ssync(s)
                s.activation(dst_ap, src_ap, AF.Identity,
                             bias=st_nb[:, c], scale=st_rstd[:, c])
                s.activation(jnk_s[:, 3:4], st_nb[:, c], AF.Identity).then_inc(s_s, 1)
            block.scalar(_sc3)
            return ctx.bump("s_s")

        def ln_tile(src_ap, col, dst_ap, dump_ap, pre_s=None, pre_v=None, pre_s3=None):
            ev_ssq = ln_sq(src_ap, col, dump_ap, pre_s=pre_s)
            ev_rv = ln_vstats(src_ap, col, ev_ssq, pre_v=pre_v)
            return ln_fin(src_ap, col, dst_ap, ev_rv, pre_s3=pre_s3)

        with ExitStack() as ps_st:
            pe = ps_st.enter_context
            ps_k = pe(nc.psum_tensor("ps_k", [128, 128], F32))
            ps_vv = pe(nc.psum_tensor("ps_vv", [128, 128], F32))
            ps_r = pe(nc.psum_tensor("ps_r", [128, 128], F32))
            ps_tp = pe(nc.psum_tensor("ps_tp", [128, 128], BF16))

            HALO = {0: 0, 1: 1, 10: 0, 11: 1}
            halo_kv_done = {}
            norm_prev2 = [0, 0, 0]
            tr_prev = {}
            kevac = 0
            vevac = 0
            rope_prev = 0
            vcp_prev = 0
            ln_tr_ev = [0] * NT_SLAB
            st_rv = [0] * NT_SLAB
            a_ld_ev = [0] * NT_SLAB

            ld0_ev = [0, 0, 0]

            def _ld0(g):
                for i in range(3):
                    g.dma_start(xload2[i], xslab[128 * i:128 * (i + 1), :]).then_inc(s_x, 16)
                    ld0_ev[i] = ctx.bump("s_x", 16)
            block.gpsimd(_ld0)

            def a_front(i):
                if i <= 2:
                    wx = (s_x, ld0_ev[i])
                else:
                    wx = (s_x2, a_ld_ev[i])
                ev_ssq = ln_sq(xload2[i % 3], i % 4, dumpA, pre_s=[wx])
                st_rv[i] = ln_vstats(xload2[i % 3], i % 4, ev_ssq, pre_v=[wx])

            def a_back(i):
                nonlocal kevac, vevac, rope_prev, vcp_prev
                slot = i % 2
                pre3 = [(s_tr, tr_prev[i - 2])] if i >= 2 else None
                ev_norm = ln_fin(xload2[i % 3], i % 4, lnb[slot], st_rv[i], pre_s3=pre3)
                norm_prev2[i % 3] = ev_norm

                if i in HALO:
                    hs = HALO[i]
                    dest = halotmp[hs]
                else:
                    dest = ln1T[:, :, 128 * (i - 2):128 * (i - 1)]
                    hs = None

                def _tr(s2, i=i, slot=slot, hs=hs, dest=dest, ev=ev_norm):
                    s2.wait_ge(s_s, ev)
                    if hs is not None and hs in halo_kv_done:
                        s2.wait_ge(s_mm, halo_kv_done[hs])
                    s2.dma_start_transpose(dest, lnb[slot]).then_inc(s_tr, 16)
                    nxt = i + 3
                    if nxt < NT_SLAB:
                        s2.dma_start(xload2[nxt % 3],
                                     xslab[128 * nxt:128 * (nxt + 1), :]).then_inc(s_x2, 16)
                        a_ld_ev[nxt] = ctx.bump("s_x2", 16)
                block.scalar(_tr)
                ln_tr_ev[i] = ctx.bump("s_tr", 16)
                tr_prev[i] = ln_tr_ev[i]

                def _t(t, i=i, dest=dest, ev_tr=ln_tr_ev[i], ke=kevac, ve=vevac):
                    if i == 0:
                        t.wait_ge(s_w, P0_W)
                        t.wait_ge(s_g, P0_G)
                    t.wait_ge(s_tr, ev_tr)
                    if ke:
                        t.wait_ge(s_s, ke)
                    for fc in range(FC):
                        m = t.matmul(ps_k[:], wk_sb[:, fc, :], dest[:, fc, :],
                                     start=(fc == 0), stop=(fc == FC - 1))
                    m.then_inc(s_mm, 1)
                    if ve:
                        t.wait_ge(s_s, ve)
                    for fc in range(FC):
                        m = t.matmul(ps_vv[:], wv_sb[:, fc, :], dest[:, fc, :],
                                     start=(fc == 0), stop=(fc == FC - 1))
                    m.then_inc(s_mm, 1)
                block.tensor(_t)
                ev_k = ctx.bump("s_mm")
                ev_vmm = ctx.bump("s_mm")
                if i in HALO:
                    halo_kv_done[HALO[i]] = ev_vmm

                def _s(s, i=i, slot=slot, evk=ev_k, evv=ev_vmm, rp=rope_prev, vc=vcp_prev):
                    s.wait_ge(s_mm, evk)
                    if rp:
                        s.wait_ge(s_v, rp)
                    s.activation(kstage[slot], ps_k[:], AF.Identity)
                    s.activation(jnk_s[:, 1:2], kstage[slot][:, 0:1],
                                 AF.Identity).then_inc(s_s, 1)
                    s.wait_ge(s_mm, evv)
                    if vc:
                        s.wait_ge(s_v, vc)
                    s.activation(vstage[slot], ps_vv[:], AF.Identity)
                    s.activation(jnk_s[:, 2:3], vstage[slot][:, 0:1],
                                 AF.Identity).then_inc(s_s, 1)
                block.scalar(_s)
                kevac = ctx.bump("s_s")
                vevac = ctx.bump("s_s")

                def _t2(t, slot=slot, ke=kevac, ve=vevac, rp=rope_prev, vc=vcp_prev):
                    t.wait_ge(s_s, ke)
                    if rp:
                        t.wait_ge(s_v, rp)
                    t.matmul(ps_r[:], pt_sb[:], kstage[slot],
                             start=True, stop=True).then_inc(s_mm, 1)
                    t.wait_ge(s_s, ve)
                    if vc:
                        t.wait_ge(s_v, vc)
                    t.transpose(ps_tp[:], vstage[slot], idb_sb[:]).then_inc(s_mm, 1)
                block.tensor(_t2)
                ev_rot = ctx.bump("s_mm")
                ev_vtp = ctx.bump("s_mm")

                def _v(v, i=i, slot=slot, er=ev_rot, et=ev_vtp):
                    cols = slice(128 * i, 128 * (i + 1))
                    v.wait_ge(s_mm, er)
                    v.tensor_tensor(ropesc[:, 0:128], kstage[slot], cos_sb[:, cols], ALU.mult)
                    v.tensor_tensor(ropesc[:, 128:256], ps_r[:], sin_sb[:, cols], ALU.mult)
                    vsync(v)
                    v.tensor_tensor(krT_sb[:, cols], ropesc[:, 0:128], ropesc[:, 128:256],
                                    ALU.add)
                    v.tensor_copy(jnk_v[:, 3:4], krT_sb[:, cols][:, 0:1]).then_inc(s_v, 1)
                    v.wait_ge(s_mm, et)
                    v.tensor_copy(vaug_sb[:, i, 0:128], ps_tp[:])
                    v.tensor_copy(jnk_v[:, 3:4], vaug_sb[:, i, 0:1]).then_inc(s_v, 1)
                block.vector(_v)
                rope_prev = ctx.bump("s_v")
                vcp_prev = ctx.bump("s_v")

            for i in range(NT_SLAB):
                a_front(i)
                if i >= 2:
                    a_back(i - 2)
            a_back(NT_SLAB - 2)
            a_back(NT_SLAB - 1)

            if tap == "ln1T":
                finish_tap(ln1T, [128, FC, T_OWN], BF16)
                return nc, tap_dram[0]
            if tap == "kv":
                finish_tap(krT_sb[:], [128, T_SLAB], BF16)
                return nc, tap_dram[0]
            if tap == "vaug":
                finish_tap(vaug_sb[:], [128, NT_SLAB, 132], BF16)
                return nc, tap_dram[0]
            barrier()

        with ExitStack() as ps_st:
            pe = ps_st.enter_context
            ps_q = [[pe(nc.psum_tensor(f"ps_q{p}{i}", [128, 512], F32))
                     for i in range(2)] for p in range(2)]
            ps_r2 = [[pe(nc.psum_tensor(f"ps_r2{p}{i}", [128, 512], F32))
                      for i in range(2)] for p in range(2)]

            wbuf_free = [0, 0]
            psq_free = [[0, 0], [0, 0]]
            psr_free = [[0, 0], [0, 0]]
            qst_free = [[0, 0], [0, 0]]
            for ot in range(NH):
                slot = ot % 2

                def _g(g, ot=ot, slot=slot, w=wbuf_free[slot]):
                    if w:
                        g.wait_ge(s_mm, w)
                    g.dma_start(wbuf[slot], wq_t[ot]).then_inc(s_w, 16)
                block.gpsimd(_g)
                ev_w = ctx.bump("s_w", 16)

                last_mm = None
                for th in range(2):
                    pq = ps_q[slot][th]
                    pr = ps_r2[slot][th]

                    def _t(t, th=th, slot=slot, ev_w=ev_w, pf=psq_free[slot][th], pq=pq):
                        t.wait_ge(s_w, ev_w)
                        if pf:
                            t.wait_ge(s_s, pf)
                        for fc in range(FC):
                            m = t.matmul(pq[:], wbuf[slot][:, fc, :],
                                         ln1T[:, fc, 512 * th:512 * (th + 1)],
                                         start=(fc == 0), stop=(fc == FC - 1))
                        m.then_inc(s_mm, 1)
                    block.tensor(_t)
                    ev_mm = ctx.bump("s_mm")
                    last_mm = ev_mm

                    def _s(s, th=th, slot=slot, ev=ev_mm, qf=qst_free[slot][th], pq=pq):
                        s.wait_ge(s_mm, ev)
                        if qf:
                            s.wait_ge(s_v, qf)
                        s.activation(rstage[:, 512 * th:512 * (th + 1)], pq[:],
                                     AF.Identity)
                        s.activation(jnk_s[:, 1:2], rstage[:, 512 * th:512 * th + 1],
                                     AF.Identity).then_inc(s_s, 1)
                    block.scalar(_s)
                    ev_ev = ctx.bump("s_s")
                    psq_free[slot][th] = ev_ev

                    def _t2(t, th=th, slot=slot, ev=ev_ev, rf=psr_free[slot][th], pr=pr):
                        t.wait_ge(s_s, ev)
                        if rf:
                            t.wait_ge(s_v, rf)
                        t.matmul(pr[:], pt_sb[:], rstage[:, 512 * th:512 * (th + 1)],
                                 start=True, stop=True).then_inc(s_mm, 1)
                    block.tensor(_t2)
                    ev_rot = ctx.bump("s_mm")

                    def _v(v, ot=ot, th=th, ev=ev_rot, pr=pr):
                        v.wait_ge(s_mm, ev)
                        ccols = slice(256 + 512 * th, 256 + 512 * (th + 1))
                        v.tensor_tensor(ropesc[:, 0:512], rstage[:, 512 * th:512 * (th + 1)],
                                        cos_sb[:, ccols], ALU.mult)
                        v.tensor_tensor(ropesc[:, 512:1024], pr[:],
                                        sin_sb[:, ccols], ALU.mult)
                        v.tensor_tensor(qrT[:, ot, 512 * th:512 * (th + 1)],
                                        ropesc[:, 0:512], ropesc[:, 512:1024],
                                        ALU.add)
                        v.tensor_copy(jnk_v[:, 3:4],
                                      qrT[:, ot, 512 * th:512 * th + 1]).then_inc(s_v, 1)
                    block.vector(_v)
                    ev_rope = ctx.bump("s_v")
                    psr_free[slot][th] = ev_rope
                    qst_free[slot][th] = ev_rope
                wbuf_free[slot] = last_mm

            if tap == "qrT":
                finish_tap(qrT, [128, NH, T_OWN], BF16)
                return nc, tap_dram[0]
            barrier()

        with ExitStack() as ps_st:
            pe = ps_st.enter_context
            ps_scA = pe(nc.psum_tensor("ps_scA", [128, 1024], F32))
            ps_scB = pe(nc.psum_tensor("ps_scB", [128, 1024], F32))
            ps_wA = pe(nc.psum_tensor("ps_wA", [128, 512], F32))
            ps_wB = pe(nc.psum_tensor("ps_wB", [128, 512], F32))

            sigA_done = 0
            sigB_done = 0
            blend_done = 0
            anat_guard = 0
            anat_tr = {}
            heads = [(s_i, h) for s_i in range(SEGS) for h in range(NH)]

            def emit_scores(idx):
                s_i, h = heads[idx]
                qcols = slice(256 * s_i, 256 * (s_i + 1))

                def _t(t, s_i=s_i, h=h, qcols=qcols, sa=sigA_done, sb=sigB_done):
                    if sa:
                        t.wait_ge(s_s, sa)
                    for j in range(4):
                        t.matmul(ps_scA[:, 256 * j:256 * (j + 1)],
                                 krT_sb[:, 128 * (2 * s_i + j):128 * (2 * s_i + j + 1)],
                                 qrT[:, h, qcols], start=True,
                                 stop=True).then_inc(s_mm, 1) if j == 3 else \
                        t.matmul(ps_scA[:, 256 * j:256 * (j + 1)],
                                 krT_sb[:, 128 * (2 * s_i + j):128 * (2 * s_i + j + 1)],
                                 qrT[:, h, qcols], start=True, stop=True)
                    if sb:
                        t.wait_ge(s_s, sb)
                    for j in range(4):
                        m = t.matmul(ps_scB[:, 256 * j:256 * (j + 1)],
                                     krT_sb[:, 128 * (2 * s_i + 2 + j):128 * (2 * s_i + 3 + j)],
                                     qrT[:, h, qcols], start=True, stop=True)
                    m.then_inc(s_mm, 1)
                block.tensor(_t)
                evA = ctx.bump("s_mm")
                evB = ctx.bump("s_mm")
                return evA, evB

            def emit_sig(idx, evA, evB):
                nonlocal sigA_done, sigB_done
                s_i, h = heads[idx]

                sl = sigAB[idx % 2]
                sigA = sl[:, 0:1024]
                sigB = sl[:, 1024:2048]

                def _s(s, s_i=s_i, evA=evA, evB=evB, sigA=sigA, sigB=sigB):
                    s.wait_ge(s_mm, evA)
                    s.activation(sigA, ps_scA[:], AF.Sigmoid,
                                 scale=SCALE).then_inc(s_s, 1)
                    s.wait_ge(s_mm, evB)
                    if s_i == SEGS - 1:
                        s.activation(sigB[:, 0:512], ps_scB[:, 0:512],
                                     AF.Sigmoid, scale=SCALE)
                        s.activation(sigB[:, 512:768], ps_scB[:, 512:768], AF.Sigmoid,
                                     scale=SCALE, bias=mb_sb[:, 0:1])
                        s.activation(sigB[:, 768:1024], ps_scB[:, 768:1024], AF.Sigmoid,
                                     scale=SCALE, bias=mb_sb[:, 1:2])
                    else:
                        s.activation(sigB, ps_scB[:], AF.Sigmoid, scale=SCALE)
                    s.activation(jnk_s[:, 1:2], sigB[:, 1023:1024],
                                 AF.Sigmoid).then_inc(s_s, 1)
                block.scalar(_s)
                sigA_done = ctx.bump("s_s")
                sigB_done = ctx.bump("s_s")

            def emit_wout(idx, sd):
                s_i, h = heads[idx]
                sl = sigAB[idx % 2]
                sigA = sl[:, 0:1024]
                sigB = sl[:, 1024:2048]

                def _t2(t, s_i=s_i, sd=sd, bd=blend_done, sigA=sigA, sigB=sigB):
                    t.wait_ge(s_s, sd)
                    if bd:
                        t.wait_ge(s_v, bd)
                    for qc in range(2):
                        off = 132 * qc
                        for j in range(4):
                            t.matmul(ps_wA[:, off:off + 129],
                                     sigA[:, 256 * j + 128 * qc:256 * j + 128 * (qc + 1)],
                                     vaug_sb[:, 2 * s_i + j, 0:129],
                                     start=(j == 0), stop=(j == 3))
                        for j in range(4):
                            m = t.matmul(ps_wB[:, off:off + 129],
                                         sigB[:, 256 * j + 128 * qc:256 * j + 128 * (qc + 1)],
                                         vaug_sb[:, 2 * s_i + 2 + j, 0:129],
                                         start=(j == 0), stop=(j == 3))
                    m.then_inc(s_mm, 1)
                block.tensor(_t2)
                return ctx.bump("s_mm")

            def emit_blend(idx, ev_w2):
                nonlocal blend_done
                s_i, h = heads[idx]

                def _v(v, s_i=s_i, h=h, ev=ev_w2, ag=anat_guard):
                    v.wait_ge(s_mm, ev)
                    if ag and h == 0:
                        v.wait_ge(s_tr, ag)
                    sA2 = ps_wA[:, 0:264].rearrange("p (a b) -> p a b", a=2, b=132)[
                        :, :, 128:129].rearrange("p a b -> p (a b)")
                    sB2 = ps_wB[:, 0:264].rearrange("p (a b) -> p a b", a=2, b=132)[
                        :, :, 128:129].rearrange("p a b -> p (a b)")
                    v.reciprocal(bl_r[:, 0:2], sA2)
                    v.reciprocal(bl_r[:, 2:4], sB2)
                    vsync(v)
                    acols = slice(2 * s_i, 2 * s_i + 2)
                    v.tensor_tensor(bl_r[:, 4:6], bl_r[:, 0:2], aA_sb[:, acols], ALU.mult)
                    v.tensor_tensor(bl_r[:, 6:8], bl_r[:, 2:4], aB_sb[:, acols], ALU.mult)
                    vsync(v)
                    for qc in range(2):
                        ts1(v, bl_t[:, 128 * qc:128 * (qc + 1)],
                            ps_wA[:, 132 * qc:132 * qc + 128],
                            bl_r[:, 4 + qc:5 + qc], ALU.mult)
                    for qc in range(2):
                        v.scalar_tensor_tensor(
                            anat[qc][:, 128 * h:128 * (h + 1)],
                            ps_wB[:, 132 * qc:132 * qc + 128],
                            bl_r[:, 6 + qc:7 + qc],
                            bl_t[:, 128 * qc:128 * (qc + 1)],
                            ALU.mult, ALU.add)
                    v.tensor_copy(jnk_v[:, 3:4],
                                  anat[1][:, 128 * h:128 * h + 1]).then_inc(s_v, 1)
                block.vector(_v)
                blend_done = ctx.bump("s_v")

            pend = None
            for idx in range(len(heads)):
                evA, evB = emit_scores(idx)
                emit_sig(idx, evA, evB)
                my_sig = sigB_done
                if pend is not None:
                    ev_w2 = emit_wout(pend[0], pend[1])
                    emit_blend(pend[0], ev_w2)
                pend = (idx, my_sig)
                s_i, h = heads[idx]
                if h == NH - 1 and pend is not None:
                    ev_w2 = emit_wout(pend[0], pend[1])
                    emit_blend(pend[0], ev_w2)
                    pend = None
                    for qc in range(2):
                        ti = 2 * s_i + qc

                        def _tr(sy, ti=ti, qc=qc, bd=blend_done):
                            sy.wait_ge(s_v, bd)
                            sy.dma_start_transpose(
                                attnT[:, :, 128 * ti:128 * (ti + 1)],
                                anat[qc]).then_inc(s_tr, 16)
                        block.sync(_tr)
                        anat_tr[ti] = ctx.bump("s_tr", 16)
                    anat_guard = anat_tr[2 * s_i + 1]

            if tap == "wlast":
                wdbg = vf(RE + 8192, T_OWN)

                def _vdbg(v, bd=blend_done):
                    v.wait_ge(s_v, bd)
                    v.tensor_copy(wdbg[:, 0:264], ps_wA.bitcast(F32)[:, 0:264])
                    v.tensor_copy(wdbg[:, 264:528], ps_wB.bitcast(F32)[:, 0:264]).then_inc(s_v, 1)
                block.vector(_vdbg)
                ctx.bump("s_v")
                finish_tap(wdbg, [128, T_OWN], F32)
                return nc, tap_dram[0]
            if tap == "sclast":
                wdbg2 = vf(RE + 8192, T_OWN)

                def _vdbg2(v, ev=ev_sc):
                    v.wait_ge(s_mm, ev)
                    v.tensor_copy(wdbg2[:], ps_scB[:]).then_inc(s_v, 1)
                block.vector(_vdbg2)
                ctx.bump("s_v")
                finish_tap(wdbg2, [128, T_OWN], F32)
                return nc, tap_dram[0]
            if tap == "siglast":
                finish_tap(sigB, [128, T_OWN], BF16)
                return nc, tap_dram[0]
            if tap == "attnT":
                finish_tap(attnT, [128, NH, T_OWN], BF16)
                return nc, tap_dram[0]
            attn_tr_last = anat_tr[2 * SEGS - 1]
            barrier()

        def big_proj(wdram, rhs_fn, dst, delta_fn=None, first_wait=None):
            with ExitStack() as ps_st2:
                pe2 = ps_st2.enter_context
                ps_p = [pe2(nc.psum_tensor(f"ps_p{ctx.bump('psn')}", [128, 512], F32))
                        for _ in range(2)]
                wb_free = [0, 0]
                ps_free = [0, 0]
                for ot in range(FC):
                    slot = ot % 2

                    def _g(g, ot=ot, slot=slot, w=wb_free[slot]):
                        if w:
                            g.wait_ge(s_mm, w)
                        g.dma_start(wbuf[slot], wdram[ot]).then_inc(s_w, 16)
                    block.gpsimd(_g)
                    ev_w = ctx.bump("s_w", 16)

                    last = None
                    for th in range(2):
                        def _t(t, ot=ot, th=th, slot=slot, ev_w=ev_w, pf=ps_free[th]):
                            if ot == 0 and th == 0 and first_wait:
                                for sm, vv in first_wait:
                                    t.wait_ge(sm, vv)
                            t.wait_ge(s_w, ev_w)
                            if pf:
                                t.wait_ge(s_s, pf)
                            first = True
                            if delta_fn is not None:
                                delta_fn(t, ps_p[th], ot, th)
                                first = False
                            for fc in range(FC):
                                m = t.matmul(ps_p[th][:], wbuf[slot][:, fc, :],
                                             rhs_fn(fc, th),
                                             start=(first and fc == 0),
                                             stop=(fc == FC - 1))
                            m.then_inc(s_mm, 1)
                        block.tensor(_t)
                        ev_mm = ctx.bump("s_mm")
                        last = ev_mm

                        def _s(s, ot=ot, th=th, ev=ev_mm):
                            s.wait_ge(s_mm, ev)
                            s.activation(dst[:, ot, 512 * th:512 * (th + 1)],
                                         ps_p[th][:], AF.Identity)
                            s.activation(jnk_s[:, 1:2],
                                         dst[:, ot, 512 * th:512 * th + 1],
                                         AF.Identity).then_inc(s_s, 1)
                        block.scalar(_s)
                        ps_free[th] = ctx.bump("s_s")
                    wb_free[slot] = last

        big_proj(wo_t,
                 lambda fc, th: attnT[:, fc, 512 * th:512 * (th + 1)],
                 projT, first_wait=[(s_tr, attn_tr_last)])
        if tap == "projT":
            finish_tap(projT, [128, NH, T_OWN], BF16)
            return nc, tap_dram[0]
        barrier()

        def resid_phase(srcT, natv, is_final, tr_done=None):
            if tr_done is not None:
                ptr = tr_done
            else:
                ptr = 0
                for ot in range(FC):
                    def _tr(sy, ot=ot):
                        sy.dma_start_transpose(natv[:, :, 128 * ot:128 * (ot + 1)],
                                              srcT[:, ot, :]).then_inc(s_tr, 16)
                    block.sync(_tr)
                    ptr = ctx.bump("s_tr", 16)

            def _ld0(g, ptr=ptr):
                g.wait_ge(s_tr, ptr)
                if is_final:
                    g.wait_ge(s_res, ctx.val("s_res"))
                src = (xres[0:128, :] if is_final else xslab[256:384, :])
                g.dma_start(xnf, src).then_inc(s_x2, 16)
            block.gpsimd(_ld0)
            ld_ev = ctx.bump("s_x2", 16)

            prev_add = 0
            ln2_tr_local = {}
            dumpR = vb(RC + 8192, D) if not is_final else vb(RC, D)
            ln2_last = 0
            for i in range(NT_OWN):
                slot = 0
                pre = [(s_tr, ptr)] if i == 0 else None
                pre_s = [(s_x2, ld_ev)] + (pre or [])
                pre3 = [(s_v, prev_add)] if prev_add else None
                ev_ssq = ln_sq(natv[:, i, :], i % 4, dumpR, pre_s=pre or None)
                ev_rv = ln_vstats(natv[:, i, :], i % 4, ev_ssq, pre_v=pre)
                ev_t1 = ln_fin(natv[:, i, :], i % 4, t1f, ev_rv, pre_s3=pre3)

                def _v3(v, ev_t1=ev_t1, lev=ld_ev):
                    v.wait_ge(s_s, ev_t1)
                    v.wait_ge(s_x2, lev)
                    v.tensor_tensor(xnf, xnf, t1f, ALU.add)
                    v.tensor_copy(jnk_v[:, 3:4], xnf[:, 0:1]).then_inc(s_v, 1)
                block.vector(_v3)
                ev_xn = ctx.bump("s_v")
                prev_add = ev_xn

                if is_final:
                    def _st(s2, i=i, ev=ev_xn):
                        s2.wait_ge(s_v, ev)
                        s2.dma_start(out[128 * i:128 * (i + 1), :],
                                     xnf).then_inc(s_out, 16)
                        nxt = i + 1
                        if nxt < NT_OWN:
                            s2.wait_ge(s_out, ctx.val("s_out") + 16)
                            s2.dma_start(xnf, xres[128 * nxt:128 * (nxt + 1), :]
                                         ).then_inc(s_x2, 16)
                    block.scalar(_st)
                    ctx.bump("s_out", 16)
                    if i + 1 < NT_OWN:
                        ld_ev = ctx.bump("s_x2", 16)
                else:
                    def _st(s2, i=i, ev=ev_xn):
                        s2.wait_ge(s_v, ev)
                        s2.dma_start(xres[128 * i:128 * (i + 1), :],
                                     xnf).then_inc(s_res, 16)
                    block.scalar(_st)
                    ev_res = ctx.bump("s_res", 16)

                    pre3b = [(s_tr, ln2_tr_local[i - 1])] if i >= 1 else None
                    ev_ssq2 = ln_sq(xnf, 4 + i % 4, dumpR, pre_s=[(s_v, ev_xn)])
                    ev_rv2 = ln_vstats(xnf, 4 + i % 4, ev_ssq2, pre_v=[(s_v, ev_xn)])
                    ev_n2 = ln_fin(xnf, 4 + i % 4, lnb[slot], ev_rv2, pre_s3=pre3b)

                    def _tr2(s2, i=i, slot=slot, evr=ev_res, ev=ev_n2):
                        s2.wait_ge(s_s, ev)
                        s2.dma_start_transpose(ln2T[:, :, 128 * i:128 * (i + 1)],
                                               lnb[slot]).then_inc(s_tr, 16)
                        nxt = i + 1
                        if nxt < NT_OWN:
                            s2.wait_ge(s_res, evr)
                            s2.dma_start(xnf, xslab[256 + 128 * nxt:256 + 128 * (nxt + 1), :]
                                         ).then_inc(s_x2, 16)
                    block.scalar(_tr2)
                    ln2_tr_local[i] = ctx.bump("s_tr", 16)
                    ln2_last = ln2_tr_local[i]
                    if i + 1 < NT_OWN:
                        ld_ev = ctx.bump("s_x2", 16)
            return ln2_last

        ln2_tr_last = resid_phase(projT, nat8, is_final=False)
        if tap == "ln2T":
            finish_tap(ln2T, [128, FC, T_OWN], BF16)
            return nc, tap_dram[0]
        barrier()

        with ExitStack() as ps_st:
            pe = ps_st.enter_context
            ps_sh = pe(nc.psum_tensor("ps_sh", [64, T_OWN], F32))
            ps_tpf = pe(nc.psum_tensor("ps_tpf", [128, 128], F32))
            ps_oh = pe(nc.psum_tensor("ps_oh", [4, T_OWN], F32))
            ps_ex = pe(nc.psum_tensor("ps_ex", [32, T_OWN], F32))

            def _gms(g):
                g.memset(lgT, 0.0).then_inc(s_g, 1)
            block.gpsimd(_gms)
            ev_ms = ctx.bump("s_g", 1)

            def _tf1(t):
                t.wait_ge(s_tr, ln2_tr_last)
                for th in range(2):
                    for fc in range(FC):
                        m = t.matmul(ps_sh[:, 512 * th:512 * (th + 1)], sh_sb[:, fc, :],
                                     ln2T[:, fc, 512 * th:512 * (th + 1)],
                                     start=(fc == 0), stop=(fc == FC - 1))
                m.then_inc(s_mm, 1)
            block.tensor(_tf1)
            ev_sh = ctx.bump("s_mm")

            def _vf2(v, ev=ev_sh, evm=ev_ms):
                v.wait_ge(s_mm, ev)
                v.wait_ge(s_g, evm)
                ts1(v, lgT[32:36, :], ps_sh[32:36, :], eb_sb[32:36, :], ALU.add)
                v.tensor_copy(jnk_v[0:4, 3:4], lgT[32:36, 0:1]).then_inc(s_v, 1)
            block.vector(_vf2)
            ev_lg = ctx.bump("s_v")

            cp_ev = 0
            for j in range(NT_OWN):
                def _t(t, j=j, ev=ev_lg, ce=cp_ev):
                    if j == 0:
                        t.wait_ge(s_v, ev)
                    if ce:
                        t.wait_ge(s_v, ce)
                    t.transpose(ps_tpf[:], lgT[:, 128 * j:128 * (j + 1)],
                                idf_sb[:]).then_inc(s_mm, 1)
                block.tensor(_t)
                ev_tp = ctx.bump("s_mm")

                def _v(v, j=j, ev=ev_tp):
                    v.wait_ge(s_mm, ev)
                    v.tensor_copy(lgN[:, j, :], ps_tpf[:, 32:36])
                    v.tensor_copy(jnk_v[:, 3:4], lgN[:, j, 0:1]).then_inc(s_v, 1)
                block.vector(_v)
                cp_ev = ctx.bump("s_v")

            def _vcas(v, ce=cp_ev):
                v.wait_ge(s_v, ce)
                for j in range(NT_OWN):
                    v.reduce_max(mcol[:, j:j + 1], lgN[:, j, :], axis=AX)
                vsync(v)
                for j in range(NT_OWN):
                    ts1(v, fflg8[:, j, :], lgN[:, j, :], mcol[:, j:j + 1], ALU.is_equal)
                vsync(v)
                for j in range(NT_OWN):
                    v.tensor_copy(ohn[:, j, 0:1], fflg8[:, j, 0:1])
                for j in range(NT_OWN):
                    v.tensor_scalar(ntA[:, j:j + 1], fflg8[:, j, 0:1], -1.0, 1.0,
                                    ALU.mult, ALU.add)
                vsync(v)
                for j in range(NT_OWN):
                    v.tensor_tensor(ohn[:, j, 1:2], fflg8[:, j, 1:2], ntA[:, j:j + 1],
                                    ALU.mult)
                for j in range(NT_OWN):
                    v.tensor_scalar(ntB[:, j:j + 1], fflg8[:, j, 1:2], -1.0, 1.0,
                                    ALU.mult, ALU.add)
                vsync(v)
                for j in range(NT_OWN):
                    v.tensor_tensor(ntC[:, j:j + 1], ntA[:, j:j + 1], ntB[:, j:j + 1],
                                    ALU.mult)
                vsync(v)
                for j in range(NT_OWN):
                    v.tensor_tensor(ohn[:, j, 2:3], fflg8[:, j, 2:3], ntC[:, j:j + 1],
                                    ALU.mult)
                for j in range(NT_OWN):
                    v.tensor_scalar(ntB[:, j:j + 1], fflg8[:, j, 2:3], -1.0, 1.0,
                                    ALU.mult, ALU.add)
                vsync(v)
                for j in range(NT_OWN):
                    v.tensor_tensor(ntA[:, j:j + 1], ntC[:, j:j + 1], ntB[:, j:j + 1],
                                    ALU.mult)
                vsync(v)
                for j in range(NT_OWN):
                    v.tensor_tensor(ohn[:, j, 3:4], fflg8[:, j, 3:4], ntA[:, j:j + 1],
                                    ALU.mult)
                v.tensor_copy(jnk_v[:, 1:2], ohn[:, NT_OWN - 1, 3:4]).then_inc(s_v, 1)
            block.vector(_vcas)
            oh_ev = ctx.bump("s_v")

            tpb_last = 0
            for j in range(NT_OWN):
                def _t(t, j=j, ev=oh_ev):
                    if j == 0:
                        t.wait_ge(s_v, ev)
                    t.transpose(ps_oh[:, 128 * j:128 * (j + 1)], ohn[:, j, :],
                                idf_sb[:]).then_inc(s_mm, 1)
                block.tensor(_t)
                tpb_last = ctx.bump("s_mm")

            def _vf3(v, ev=tpb_last):
                v.wait_ge(s_mm, ev)
                v.tensor_copy(ohT_sb[0:4, :], ps_oh[:])
                v.tensor_copy(jnk_v[0:4, 3:4], ohT_sb[0:4, 0:1]).then_inc(s_v, 1)
            block.vector(_vf3)
            ev_ohT = ctx.bump("s_v")

            def _tf4(t, ev=ev_ohT):
                t.wait_ge(s_v, ev)
                for th in range(2):
                    m = t.matmul(ps_ex[:, 512 * th:512 * (th + 1)], ee_sb[:],
                                 ohT_sb[0:4, 512 * th:512 * (th + 1)], start=True, stop=True)
                m.then_inc(s_mm, 1)
            block.tensor(_tf4)
            ev_ex = ctx.bump("s_mm")

            def _vf5(v, ev=ev_ex):
                v.wait_ge(s_mm, ev)
                v.tensor_copy(ex_sb[0:32, :], ps_ex[:])
                vsync(v)
                v.tensor_tensor(zmT[0:32, :], ps_sh[0:32, :], ex_sb[0:32, :], ALU.mult)
                v.tensor_copy(jnk_v[0:32, 3:4], zmT[0:32, 0:1]).then_inc(s_v, 1)
            block.vector(_vf5)
            ev_zm = ctx.bump("s_v")

            def _tf6(t, ev=ev_zm):
                t.wait_ge(s_v, ev)
            block.tensor(_tf6)

            if tap == "zmT":
                finish_tap(zmT[0:32, :], [32, T_OWN], BF16)
                return nc, tap_dram[0]

        def delta_fn(t, ps, ot, th):
            t.matmul(ps[:], bfl_sb[:, 128 * ot:128 * (ot + 1)],
                     zmT[0:32, 512 * th:512 * (th + 1)], start=True, stop=False)

        big_proj(wup_t,
                 lambda fc, th: ln2T[:, fc, 512 * th:512 * (th + 1)],
                 ffnT, delta_fn=delta_fn, first_wait=[(s_tr, ln2_tr_last)])

        if tap == "ffnT":
            finish_tap(ffnT, [128, NH, T_OWN], BF16)
            return nc, tap_dram[0]
        barrier()

        resid_phase(ffnT, natG, is_final=True)

        def _fin(sy):
            sy.wait_ge(s_out, ctx.val("s_out"))
        block.sync(_fin)

    return nc, tap_dram[0]



def _rope_cache(S_, D_, base=10000.0):
    inv_freq = 1.0 / base ** (np.arange(0, D_, 2, dtype=np.float32) / D_)
    t = np.arange(S_, dtype=np.float32)
    f = np.outer(t, inv_freq)
    cos = np.zeros((S_, D_), dtype=np.float32)
    sin = np.zeros((S_, D_), dtype=np.float32)
    cos[:, 0::2] = np.cos(f)
    cos[:, 1::2] = np.cos(f)
    sin[:, 0::2] = np.sin(f)
    sin[:, 1::2] = np.sin(f)
    return cos, sin


def _tile_w(wT):
    f, o = wT.shape
    if o <= 128:
        return np.ascontiguousarray(wT.reshape(FC, 128, o).transpose(1, 0, 2))
    return np.ascontiguousarray(
        wT.reshape(FC, 128, o // 128, 128).transpose(2, 1, 0, 3))


def make_inputs(x, Wq, Wk, Wv, Wo, W_up, gate_w, expert_bias, lora_A, lora_B):
    cos, sin = _rope_cache(S, DH)
    wq_t = _tile_w(np.ascontiguousarray(Wq.T))
    wo_t = _tile_w(np.ascontiguousarray(Wo.T))
    wup_t = _tile_w(np.ascontiguousarray(W_up.T))
    wk_t = _tile_w(np.ascontiguousarray(Wk.T))
    wv_t = _tile_w(np.ascontiguousarray(Wv.T))
    shW = np.zeros((D, 64), np.float32)
    shW[:, 0:32] = lora_A.reshape(32, D).T
    shW[:, 32:36] = gate_w.T
    sh_t = _tile_w(shW)
    bflat = np.ascontiguousarray(lora_B.transpose(0, 2, 1).reshape(32, D))
    eexp = np.zeros((4, 32), np.float32)
    for ei in range(4):
        eexp[ei, ei * 8:(ei + 1) * 8] = 1.0
    identf = np.eye(128, dtype=np.float32)
    pmat = np.zeros((128, 128), np.float32)
    for d_ in range(64):
        pmat[d_, d_ + 64] = -1.0
        pmat[d_ + 64, d_] = 1.0
    ptmat = np.ascontiguousarray(pmat.T)
    lin = np.linspace(0.0, 1.0, 256, dtype=np.float32)

    maps = []
    for c in range(8):
        b, j = divmod(c, 4)
        g0 = 1024 * j - 256
        xs = np.zeros((T_SLAB, D), np.float32)
        lo, hi = max(0, g0), min(S, g0 + T_SLAB)
        xs[lo - g0:hi - g0, :] = x[b, lo:hi, :]
        ct = np.zeros((128, T_SLAB), np.float32)
        sn = np.zeros((128, T_SLAB), np.float32)
        ct[:, lo - g0:hi - g0] = cos[lo:hi].T
        sn[:, lo - g0:hi - g0] = sin[lo:hi].T
        aA = np.zeros((128, 2 * SEGS), np.float32)
        aB = np.zeros((128, 2 * SEGS), np.float32)
        for s_i in range(SEGS):
            g = 4 * j + s_i
            ab = np.ones(256, np.float32) if g == 0 else lin
            for qc in range(2):
                aB[:, 2 * s_i + qc] = ab[128 * qc:128 * (qc + 1)]
                aA[:, 2 * s_i + qc] = np.float32(1.0) - ab[128 * qc:128 * (qc + 1)]
        mb = np.zeros((128, 2), np.float32)
        if j == 3:
            mb[:] = -200.0
        maps.append(dict(
            xslab=xs, wq_t=wq_t, wo_t=wo_t, wup_t=wup_t, wk_t=wk_t, wv_t=wv_t,
            sh_t=sh_t, bflat=bflat,
            ebias=np.asarray(expert_bias, np.float32).reshape(4, 1),
            eexp=eexp, identf=identf, ptmat=ptmat, cosT=ct, sinT=sn,
            alphaA=aA, alphaB=aB, maskb=mb))
    return maps


def kernel(L0_attn_x, L0_attn_Wq, L0_attn_Wk, L0_attn_Wv, L0_attn_Wo,
           L0_ffn_W_up, L0_ffn_gate_w, L0_ffn_expert_bias, L0_ffn_lora_A,
           L0_ffn_lora_B):
    x = np.asarray(L0_attn_x, np.float32)
    maps = make_inputs(x, np.asarray(L0_attn_Wq, np.float32),
                       np.asarray(L0_attn_Wk, np.float32),
                       np.asarray(L0_attn_Wv, np.float32),
                       np.asarray(L0_attn_Wo, np.float32),
                       np.asarray(L0_ffn_W_up, np.float32),
                       np.asarray(L0_ffn_gate_w, np.float32),
                       np.asarray(L0_ffn_expert_bias, np.float32),
                       np.asarray(L0_ffn_lora_A, np.float32),
                       np.asarray(L0_ffn_lora_B, np.float32))
    nc, _ = build_program()
    res = run_bass_kernel_spmd(nc, maps, core_ids=list(range(8)))
    full = np.empty((B, S, D), np.float32)
    for c in range(8):
        b, j = divmod(c, 4)
        full[b, 1024 * j:1024 * (j + 1), :] = res.results[c]["out"]
    return full


# version 20 (final)
# speedup vs baseline: 1.4587x; 1.0648x over previous
import math
import numpy as np
from contextlib import ExitStack

import concourse.bass as bass
import concourse.mybir as mybir
from concourse.bass_utils import run_bass_kernel_spmd

F32 = mybir.dt.float32
BF16 = mybir.dt.bfloat16
AF = mybir.ActivationFunctionType
ALU = mybir.AluOpType
AX = mybir.AxisListType.X

B, S, D = 2, 4096, 4096
NH, DH = 32, 128
FC = D // 128
T_OWN = 1024
T_SLAB = 1536
NT_OWN = 8
NT_SLAB = 12
SEGS = 4
SCALE = 1.0 / math.sqrt(DH)
EPS = 1e-5

RA = 0
RB = 65536
RC = 131072
RE = 147456
RF = 163840
ARENA_BYTES = 180224


class Ctx:
    def __init__(self):
        self.cnt = {}

    def bump(self, name, k=1):
        self.cnt[name] = self.cnt.get(name, 0) + k
        return self.cnt[name]

    def val(self, name):
        return self.cnt.get(name, 0)


def build_program(tap=None):
    nc = bass.Bass("TRN2", target_bir_lowering=False)

    xslab = nc.dram_tensor("xslab", [T_SLAB, D], F32, kind="ExternalInput").ap()
    wq_t = nc.dram_tensor("wq_t", [FC, 128, FC, 128], F32, kind="ExternalInput").ap()
    wo_t = nc.dram_tensor("wo_t", [FC, 128, FC, 128], F32, kind="ExternalInput").ap()
    wup_t = nc.dram_tensor("wup_t", [FC, 128, FC, 128], F32, kind="ExternalInput").ap()
    wk_t = nc.dram_tensor("wk_t", [128, FC, 128], F32, kind="ExternalInput").ap()
    wv_t = nc.dram_tensor("wv_t", [128, FC, 128], F32, kind="ExternalInput").ap()
    sh_t = nc.dram_tensor("sh_t", [128, FC, 64], F32, kind="ExternalInput").ap()
    bflat = nc.dram_tensor("bflat", [32, D], F32, kind="ExternalInput").ap()
    ebias = nc.dram_tensor("ebias", [4, 1], F32, kind="ExternalInput").ap()
    eexp = nc.dram_tensor("eexp", [4, 32], F32, kind="ExternalInput").ap()
    identf = nc.dram_tensor("identf", [128, 128], F32, kind="ExternalInput").ap()
    ptmat = nc.dram_tensor("ptmat", [128, 128], F32, kind="ExternalInput").ap()
    cosT = nc.dram_tensor("cosT", [128, T_SLAB], F32, kind="ExternalInput").ap()
    sinT = nc.dram_tensor("sinT", [128, T_SLAB], F32, kind="ExternalInput").ap()
    alphaA = nc.dram_tensor("alphaA", [128, 2 * SEGS], F32, kind="ExternalInput").ap()
    alphaB = nc.dram_tensor("alphaB", [128, 2 * SEGS], F32, kind="ExternalInput").ap()
    maskb = nc.dram_tensor("maskb", [128, 2], F32, kind="ExternalInput").ap()

    out = nc.dram_tensor("out", [T_OWN, D], F32, kind="ExternalOutput").ap()
    xres = nc.dram_tensor("xres", [T_OWN, D], F32, kind="Internal").ap()
    tap_dram = [None]

    ctx = Ctx()
    st = ExitStack()
    e = st.enter_context

    with st:
        s_x = e(nc.semaphore("s_x"))
        s_w = e(nc.semaphore("s_w"))
        s_tr = e(nc.semaphore("s_tr"))
        s_mm = e(nc.semaphore("s_mm"))
        s_v = e(nc.semaphore("s_v"))
        s_s = e(nc.semaphore("s_s"))
        s_res = e(nc.semaphore("s_res"))
        s_out = e(nc.semaphore("s_out"))
        s_bar = e(nc.semaphore("s_bar"))
        s_g = e(nc.semaphore("s_g"))
        s_x2 = e(nc.semaphore("s_x2"))

        cos_sb = e(nc.sbuf_tensor("cos_sb", [128, T_SLAB], F32))
        sin_sb = e(nc.sbuf_tensor("sin_sb", [128, T_SLAB], F32))
        aA_sb = e(nc.sbuf_tensor("aA_sb", [128, 2 * SEGS], F32))
        aB_sb = e(nc.sbuf_tensor("aB_sb", [128, 2 * SEGS], F32))
        mb_sb = e(nc.sbuf_tensor("mb_sb", [128, 2], F32))
        idb_sb = e(nc.sbuf_tensor("idb_sb", [128, 128], BF16))
        idf_sb = e(nc.sbuf_tensor("idf_sb", [128, 128], F32))
        pt_sb = e(nc.sbuf_tensor("pt_sb", [128, 128], BF16))
        eb_sb = e(nc.sbuf_tensor("eb_sb", [36, 1], F32))
        ee_sb = e(nc.sbuf_tensor("ee_sb", [4, 32], BF16))
        sh_sb = e(nc.sbuf_tensor("sh_sb", [128, FC, 64], BF16))
        bfl_sb = e(nc.sbuf_tensor("bfl_sb", [32, D], BF16))
        krT_sb = e(nc.sbuf_tensor("krT_sb", [128, T_SLAB], BF16))
        vaug_sb = e(nc.sbuf_tensor("vaug_sb", [128, NT_SLAB, 132], BF16))
        st_sum = e(nc.sbuf_tensor("st_sum", [128, 8], F32))
        st_ssq = e(nc.sbuf_tensor("st_ssq", [128, 8], F32))
        st_mu = e(nc.sbuf_tensor("st_mu", [128, 8], F32))
        st_var = e(nc.sbuf_tensor("st_var", [128, 8], F32))
        st_rstd = e(nc.sbuf_tensor("st_rstd", [128, 8], F32))
        st_nb = e(nc.sbuf_tensor("st_nb", [128, 8], F32))
        st_nmu = e(nc.sbuf_tensor("st_nmu", [128, 8], F32))
        st_musq = e(nc.sbuf_tensor("st_musq", [128, 8], F32))
        st_var2 = e(nc.sbuf_tensor("st_var2", [128, 8], F32))
        st_rvar = e(nc.sbuf_tensor("st_rvar", [128, 8], F32))
        jnk_v = e(nc.sbuf_tensor("jnk_v", [128, 4], F32))
        jnk_s = e(nc.sbuf_tensor("jnk_s", [128, 4], F32))
        bl_r = e(nc.sbuf_tensor("bl_r", [128, 8], F32))

        arena = e(nc.sbuf_tensor("arena", [128, ARENA_BYTES // 2], BF16))

        def vb(off, *dims):
            n = 1
            for d_ in dims:
                n *= d_
            ap = arena[:, off // 2: off // 2 + n]
            if len(dims) == 2:
                ap = ap.rearrange("p (a b) -> p a b", a=dims[0], b=dims[1])
            return ap

        def vf(off, *dims):
            n = 1
            for d_ in dims:
                n *= d_
            ap = arena[:, off // 2: off // 2 + 2 * n].bitcast(F32)
            if len(dims) == 2:
                ap = ap.rearrange("p (a b) -> p a b", a=dims[0], b=dims[1])
            return ap

        ln1T = vb(RA, FC, T_OWN)
        attnT = vb(RA, NH, T_OWN)
        nat8 = vb(RA, NT_OWN, D)
        ffnT = vb(RA, NH, T_OWN)
        wk_sb = vb(RB, FC, 128)
        wv_sb = vb(RB + 8192, FC, 128)
        xload2 = [vf(RB + 16384, D), vf(RB + 32768, D), vf(RB + 49152, D)]
        qrT = vb(RB, NH, T_OWN)
        projT = vb(RB, NH, T_OWN)
        ln2T = vb(RB, FC, T_OWN)
        natG = vb(RB, NT_OWN, D)
        lnb = [vb(RC, D), vb(RC + 8192, D)]
        wbuf = [vb(RC, FC, 128), vb(RC + 8192, FC, 128)]
        anat = [vb(RC, D), vb(RC + 8192, D)]
        halotmp = [vb(RE, FC, 128), vb(RE + 8192, FC, 128)]
        rstage = vb(RE, T_OWN)
        sigAB = [vb(RE, 2 * T_OWN), vb(RE + 4096, 2 * T_OWN)]
        xnf = vf(RE, D)
        lgT = vf(RE, T_OWN)
        ohT_sb = vb(RE + 4096, T_OWN)
        ex_sb = vf(RE + 6144, T_OWN)
        zmT = vb(RE + 10240, T_OWN)
        mcol = vf(RE + 12288, NT_OWN)
        fflg = vf(RE + 12320, 4)
        ntmp = vf(RE + 12336, 4)
        ohn = vf(RE + 12352, NT_OWN, 4)
        lgN = vf(RE + 12480, NT_OWN, 4)
        fflg8 = vf(RE + 12608, NT_OWN, 4)
        ntA = vf(RE + 12736, NT_OWN)
        ntB = vf(RE + 12768, NT_OWN)
        ntC = vf(RE + 12800, NT_OWN)
        bl_t = vf(RE + 8192, 256)
        dumpA = vb(RF, D)
        vstage = [vb(RF + 8192, 128), vb(RF + 8448, 128)]
        kstage = [vb(RF + 8704, 128), vb(RF + 8960, 128)]
        ropesc = vf(RF + 9216, T_OWN)
        t1f = vf(RF, D)

        block = e(nc.Block())
        ENGS = ("tensor", "vector", "scalar", "sync", "gpsimd")

        def barrier():
            tgt = ctx.bump("s_bar", len(ENGS))
            for en in ENGS:
                def _f(eng, tgt=tgt):
                    eng.sem_inc(s_bar, 1)
                    eng.wait_ge(s_bar, tgt)
                getattr(block, en)(_f)

        def finish_tap(src_ap, shape, dtype):
            tap_dram[0] = nc.dram_tensor("tap", shape, dtype, kind="ExternalOutput").ap()
            barrier()

            def _f(sy):
                sy.dma_start(tap_dram[0], src_ap).then_inc(s_out, 16)
                sy.wait_ge(s_out, ctx.bump("s_out", 16))
            block.sync(_f)

        def ts1(eng, o, i0, s1, op):
            return eng.tensor_scalar(o, i0, s1, 0.0, op, ALU.add)

        def vsync(v):
            v.tensor_copy(jnk_v[:, 2:3], jnk_v[:, 2:3]).then_inc(s_v, 1)
            v.wait_ge(s_v, ctx.bump("s_v"))

        def ssync(s):
            s.activation(jnk_s[:, 0:1], jnk_s[:, 0:1], AF.Identity).then_inc(s_s, 1)
            s.wait_ge(s_s, ctx.bump("s_s"))

        def p0_sync(sy):
            for dst, src in ((cos_sb[:], cosT), (sin_sb[:], sinT), (aA_sb[:], alphaA),
                             (aB_sb[:], alphaB), (mb_sb[:], maskb), (idf_sb[:], identf),
                             (eb_sb[32:36, :], ebias)):
                sy.dma_start(dst, src).then_inc(s_x, 16)
                ctx.bump("s_x", 16)
        block.sync(p0_sync)

        def p0_g(g):
            for dst, src in ((idb_sb[:], identf), (pt_sb[:], ptmat), (ee_sb[:], eexp),
                             (wk_sb, wk_t), (wv_sb, wv_t), (sh_sb[:], sh_t),
                             (bfl_sb[:], bflat)):
                g.dma_start(dst, src).then_inc(s_w, 16)
                ctx.bump("s_w", 16)
            g.memset(vaug_sb[:, :, 128:129], 1.0)
            g.memset(jnk_v[:], 0.0)
            g.memset(jnk_s[:], 0.0).then_inc(s_g, 1)
            ctx.bump("s_g", 1)
        block.gpsimd(p0_g)
        P0_W = ctx.val("s_w")
        P0_G = ctx.val("s_g")

        def ln_sq(src_ap, col, dump_ap, pre_s=None):
            c = slice(col, col + 1)

            def _sc(s):
                if pre_s:
                    for sm, v in pre_s:
                        s.wait_ge(sm, v)
                s.activation(dump_ap, src_ap, AF.Square,
                             accum_out=st_ssq[:, c]).then_inc(s_s, 1)
                s.wait_ge(s_s, ctx.bump("s_s"))
                s.activation(jnk_s[:, 0:1], st_ssq[:, c], AF.Identity).then_inc(s_s, 1)
            block.scalar(_sc)
            return ctx.bump("s_s")

        def ln_vstats(src_ap, col, ev_ssq, pre_v=None):
            c = slice(col, col + 1)

            def _v(v, ev_ssq=ev_ssq):
                if pre_v:
                    for sm, vv in pre_v:
                        v.wait_ge(sm, vv)
                v.reduce_sum(st_sum[:, c], src_ap, axis=AX)
                vsync(v)
                ts1(v, st_mu[:, c], st_sum[:, c], 1.0 / D, ALU.mult)
                ts1(v, st_nmu[:, c], st_sum[:, c], -1.0 / D, ALU.mult)
                v.wait_ge(s_s, ev_ssq)
                v.tensor_scalar(st_var[:, c], st_ssq[:, c], 1.0 / D, EPS,
                                ALU.mult, ALU.add)
                v.tensor_tensor(st_musq[:, c], st_mu[:, c], st_mu[:, c], ALU.mult)
                vsync(v)
                v.scalar_tensor_tensor(st_var2[:, c], st_musq[:, c], -1.0,
                                       st_var[:, c], ALU.mult, ALU.add)
                vsync(v)
                v.reciprocal(st_rvar[:, c], st_var2[:, c])
                v.tensor_copy(jnk_v[:, 0:1], st_rvar[:, c]).then_inc(s_v, 1)
            block.vector(_v)
            return ctx.bump("s_v")

        bn6 = cos_sb[:, 0:48]
        bn2 = cos_sb[:, 48:64]

        bn6A = vf(RF, 48)
        bn2A = vf(RF + 192, 16)

        def ln_bnstats(src_ap, col, pre_v=None, b6=None, b2=None):
            b6 = bn6 if b6 is None else b6
            b2 = bn2 if b2 is None else b2
            c = slice(col, col + 1)

            def _v(v):
                if pre_v:
                    for sm, vv in pre_v:
                        v.wait_ge(sm, vv)
                for k in range(8):
                    v.bn_stats(b6[:, 6 * k:6 * (k + 1)],
                               src_ap[:, 512 * k:512 * (k + 1)])
                vsync(v)
                v.bn_aggr(b2[:, 2 * col:2 * col + 2], b6[:])
                vsync(v)
                ts1(v, st_nmu[:, c], b2[:, 2 * col:2 * col + 1], -1.0, ALU.mult)
                ts1(v, st_var[:, c], b2[:, 2 * col + 1:2 * col + 2], EPS, ALU.add)
                vsync(v)
                v.reciprocal(st_rvar[:, c], st_var[:, c])
                v.tensor_copy(jnk_v[:, 0:1], st_rvar[:, c]).then_inc(s_v, 1)
            block.vector(_v)
            return ctx.bump("s_v")

        def ln_fin(src_ap, col, dst_ap, ev_rv, pre_s3=None):
            c = slice(col, col + 1)

            def _sc3(s, ev=ev_rv):
                s.wait_ge(s_v, ev)
                if pre_s3:
                    for sm, vv in pre_s3:
                        s.wait_ge(sm, vv)
                s.activation(st_rstd[:, c], st_rvar[:, c], AF.Sqrt)
                ssync(s)
                s.activation(st_nb[:, c], st_rstd[:, c], AF.Identity,
                             scale=st_nmu[:, c])
                ssync(s)
                s.activation(dst_ap, src_ap, AF.Identity,
                             bias=st_nb[:, c], scale=st_rstd[:, c])
                s.activation(jnk_s[:, 3:4], st_nb[:, c], AF.Identity).then_inc(s_s, 1)
            block.scalar(_sc3)
            return ctx.bump("s_s")

        def ln_tile(src_ap, col, dst_ap, dump_ap, pre_s=None, pre_v=None, pre_s3=None):
            ev_ssq = ln_sq(src_ap, col, dump_ap, pre_s=pre_s)
            ev_rv = ln_vstats(src_ap, col, ev_ssq, pre_v=pre_v)
            return ln_fin(src_ap, col, dst_ap, ev_rv, pre_s3=pre_s3)

        with ExitStack() as ps_st:
            pe = ps_st.enter_context
            ps_k = pe(nc.psum_tensor("ps_k", [128, 128], F32))
            ps_vv = pe(nc.psum_tensor("ps_vv", [128, 128], F32))
            ps_r = pe(nc.psum_tensor("ps_r", [128, 128], F32))
            ps_tp = pe(nc.psum_tensor("ps_tp", [128, 128], BF16))

            HALO = {0: 0, 1: 1, 10: 0, 11: 1}
            halo_kv_done = {}
            norm_prev2 = [0, 0, 0]
            tr_prev = {}
            kevac = 0
            vevac = 0
            rope_prev = 0
            vcp_prev = 0
            ln_tr_ev = [0] * NT_SLAB
            st_rv = [0] * NT_SLAB
            a_ld_ev = [0] * NT_SLAB

            ld0_ev = [0, 0, 0]

            def _ld0(g):
                for i in range(3):
                    g.dma_start(xload2[i], xslab[128 * i:128 * (i + 1), :]).then_inc(s_x, 16)
                    ld0_ev[i] = ctx.bump("s_x", 16)
            block.gpsimd(_ld0)

            def a_front(i):
                if i <= 2:
                    wx = (s_x, ld0_ev[i])
                else:
                    wx = (s_x2, a_ld_ev[i])
                st_rv[i] = ln_bnstats(xload2[i % 3], i % 4, pre_v=[wx],
                                      b6=bn6A, b2=bn2A)

            def a_back(i):
                nonlocal kevac, vevac, rope_prev, vcp_prev
                slot = i % 2
                pre3 = [(s_tr, tr_prev[i - 2])] if i >= 2 else None
                ev_norm = ln_fin(xload2[i % 3], i % 4, lnb[slot], st_rv[i], pre_s3=pre3)
                norm_prev2[i % 3] = ev_norm

                if i in HALO:
                    hs = HALO[i]
                    dest = halotmp[hs]
                else:
                    dest = ln1T[:, :, 128 * (i - 2):128 * (i - 1)]
                    hs = None

                def _tr(s2, i=i, slot=slot, hs=hs, dest=dest, ev=ev_norm):
                    s2.wait_ge(s_s, ev)
                    if hs is not None and hs in halo_kv_done:
                        s2.wait_ge(s_mm, halo_kv_done[hs])
                    s2.dma_start_transpose(dest, lnb[slot]).then_inc(s_tr, 16)
                    nxt = i + 3
                    if nxt < NT_SLAB:
                        s2.dma_start(xload2[nxt % 3],
                                     xslab[128 * nxt:128 * (nxt + 1), :]).then_inc(s_x2, 16)
                        a_ld_ev[nxt] = ctx.bump("s_x2", 16)
                block.scalar(_tr)
                ln_tr_ev[i] = ctx.bump("s_tr", 16)
                tr_prev[i] = ln_tr_ev[i]

                def _t(t, i=i, dest=dest, ev_tr=ln_tr_ev[i], ke=kevac, ve=vevac):
                    if i == 0:
                        t.wait_ge(s_w, P0_W)
                        t.wait_ge(s_g, P0_G)
                    t.wait_ge(s_tr, ev_tr)
                    if ke:
                        t.wait_ge(s_s, ke)
                    for fc in range(FC):
                        m = t.matmul(ps_k[:], wk_sb[:, fc, :], dest[:, fc, :],
                                     start=(fc == 0), stop=(fc == FC - 1))
                    m.then_inc(s_mm, 1)
                    if ve:
                        t.wait_ge(s_s, ve)
                    for fc in range(FC):
                        m = t.matmul(ps_vv[:], wv_sb[:, fc, :], dest[:, fc, :],
                                     start=(fc == 0), stop=(fc == FC - 1))
                    m.then_inc(s_mm, 1)
                block.tensor(_t)
                ev_k = ctx.bump("s_mm")
                ev_vmm = ctx.bump("s_mm")
                if i in HALO:
                    halo_kv_done[HALO[i]] = ev_vmm

                def _s(s, i=i, slot=slot, evk=ev_k, evv=ev_vmm, rp=rope_prev, vc=vcp_prev):
                    s.wait_ge(s_mm, evk)
                    if rp:
                        s.wait_ge(s_v, rp)
                    s.activation(kstage[slot], ps_k[:], AF.Identity)
                    s.activation(jnk_s[:, 1:2], kstage[slot][:, 0:1],
                                 AF.Identity).then_inc(s_s, 1)
                    s.wait_ge(s_mm, evv)
                    if vc:
                        s.wait_ge(s_v, vc)
                    s.activation(vstage[slot], ps_vv[:], AF.Identity)
                    s.activation(jnk_s[:, 2:3], vstage[slot][:, 0:1],
                                 AF.Identity).then_inc(s_s, 1)
                block.scalar(_s)
                kevac = ctx.bump("s_s")
                vevac = ctx.bump("s_s")

                def _t2(t, slot=slot, ke=kevac, ve=vevac, rp=rope_prev, vc=vcp_prev):
                    t.wait_ge(s_s, ke)
                    if rp:
                        t.wait_ge(s_v, rp)
                    t.matmul(ps_r[:], pt_sb[:], kstage[slot],
                             start=True, stop=True).then_inc(s_mm, 1)
                    t.wait_ge(s_s, ve)
                    if vc:
                        t.wait_ge(s_v, vc)
                    t.transpose(ps_tp[:], vstage[slot], idb_sb[:]).then_inc(s_mm, 1)
                block.tensor(_t2)
                ev_rot = ctx.bump("s_mm")
                ev_vtp = ctx.bump("s_mm")

                def _v(v, i=i, slot=slot, er=ev_rot, et=ev_vtp):
                    cols = slice(128 * i, 128 * (i + 1))
                    v.wait_ge(s_mm, er)
                    v.tensor_tensor(ropesc[:, 0:128], kstage[slot], cos_sb[:, cols], ALU.mult)
                    v.tensor_tensor(ropesc[:, 128:256], ps_r[:], sin_sb[:, cols], ALU.mult)
                    vsync(v)
                    v.tensor_tensor(krT_sb[:, cols], ropesc[:, 0:128], ropesc[:, 128:256],
                                    ALU.add)
                    v.tensor_copy(jnk_v[:, 3:4], krT_sb[:, cols][:, 0:1]).then_inc(s_v, 1)
                    v.wait_ge(s_mm, et)
                    v.tensor_copy(vaug_sb[:, i, 0:128], ps_tp[:])
                    v.tensor_copy(jnk_v[:, 3:4], vaug_sb[:, i, 0:1]).then_inc(s_v, 1)
                block.vector(_v)
                rope_prev = ctx.bump("s_v")
                vcp_prev = ctx.bump("s_v")

            for i in range(NT_SLAB):
                a_front(i)
                if i >= 2:
                    a_back(i - 2)
            a_back(NT_SLAB - 2)
            a_back(NT_SLAB - 1)

            if tap == "ln1T":
                finish_tap(ln1T, [128, FC, T_OWN], BF16)
                return nc, tap_dram[0]
            if tap == "kv":
                finish_tap(krT_sb[:], [128, T_SLAB], BF16)
                return nc, tap_dram[0]
            if tap == "vaug":
                finish_tap(vaug_sb[:], [128, NT_SLAB, 132], BF16)
                return nc, tap_dram[0]
            barrier()

        with ExitStack() as ps_st:
            pe = ps_st.enter_context
            ps_q = [[pe(nc.psum_tensor(f"ps_q{p}{i}", [128, 512], F32))
                     for i in range(2)] for p in range(2)]
            ps_r2 = [[pe(nc.psum_tensor(f"ps_r2{p}{i}", [128, 512], F32))
                      for i in range(2)] for p in range(2)]

            wbuf_free = [0, 0]
            psq_free = [[0, 0], [0, 0]]
            psr_free = [[0, 0], [0, 0]]
            qst_free = [[0, 0], [0, 0]]
            for ot in range(NH):
                slot = ot % 2

                def _g(g, ot=ot, slot=slot, w=wbuf_free[slot]):
                    if w:
                        g.wait_ge(s_mm, w)
                    g.dma_start(wbuf[slot], wq_t[ot]).then_inc(s_w, 16)
                block.gpsimd(_g)
                ev_w = ctx.bump("s_w", 16)

                last_mm = None
                for th in range(2):
                    pq = ps_q[slot][th]
                    pr = ps_r2[slot][th]

                    def _t(t, th=th, slot=slot, ev_w=ev_w, pf=psq_free[slot][th], pq=pq):
                        t.wait_ge(s_w, ev_w)
                        if pf:
                            t.wait_ge(s_s, pf)
                        for fc in range(FC):
                            m = t.matmul(pq[:], wbuf[slot][:, fc, :],
                                         ln1T[:, fc, 512 * th:512 * (th + 1)],
                                         start=(fc == 0), stop=(fc == FC - 1))
                        m.then_inc(s_mm, 1)
                    block.tensor(_t)
                    ev_mm = ctx.bump("s_mm")
                    last_mm = ev_mm

                    def _s(s, th=th, slot=slot, ev=ev_mm, qf=qst_free[slot][th], pq=pq):
                        s.wait_ge(s_mm, ev)
                        if qf:
                            s.wait_ge(s_v, qf)
                        s.activation(rstage[:, 512 * th:512 * (th + 1)], pq[:],
                                     AF.Identity)
                        s.activation(jnk_s[:, 1:2], rstage[:, 512 * th:512 * th + 1],
                                     AF.Identity).then_inc(s_s, 1)
                    block.scalar(_s)
                    ev_ev = ctx.bump("s_s")
                    psq_free[slot][th] = ev_ev

                    def _t2(t, th=th, slot=slot, ev=ev_ev, rf=psr_free[slot][th], pr=pr):
                        t.wait_ge(s_s, ev)
                        if rf:
                            t.wait_ge(s_v, rf)
                        t.matmul(pr[:], pt_sb[:], rstage[:, 512 * th:512 * (th + 1)],
                                 start=True, stop=True).then_inc(s_mm, 1)
                    block.tensor(_t2)
                    ev_rot = ctx.bump("s_mm")

                    def _v(v, ot=ot, th=th, ev=ev_rot, pr=pr):
                        v.wait_ge(s_mm, ev)
                        ccols = slice(256 + 512 * th, 256 + 512 * (th + 1))
                        v.tensor_tensor(ropesc[:, 0:512], rstage[:, 512 * th:512 * (th + 1)],
                                        cos_sb[:, ccols], ALU.mult)
                        v.tensor_tensor(ropesc[:, 512:1024], pr[:],
                                        sin_sb[:, ccols], ALU.mult)
                        v.tensor_tensor(qrT[:, ot, 512 * th:512 * (th + 1)],
                                        ropesc[:, 0:512], ropesc[:, 512:1024],
                                        ALU.add)
                        v.tensor_copy(jnk_v[:, 3:4],
                                      qrT[:, ot, 512 * th:512 * th + 1]).then_inc(s_v, 1)
                    block.vector(_v)
                    ev_rope = ctx.bump("s_v")
                    psr_free[slot][th] = ev_rope
                    qst_free[slot][th] = ev_rope
                wbuf_free[slot] = last_mm

            if tap == "qrT":
                finish_tap(qrT, [128, NH, T_OWN], BF16)
                return nc, tap_dram[0]
            barrier()

        with ExitStack() as ps_st:
            pe = ps_st.enter_context
            ps_scA = pe(nc.psum_tensor("ps_scA", [128, 1024], F32))
            ps_scB = pe(nc.psum_tensor("ps_scB", [128, 1024], F32))
            ps_wA = pe(nc.psum_tensor("ps_wA", [128, 512], F32))
            ps_wB = pe(nc.psum_tensor("ps_wB", [128, 512], F32))

            sigA_done = 0
            sigB_done = 0
            blend_done = 0
            anat_guard = 0
            anat_tr = {}
            heads = [(s_i, h) for s_i in range(SEGS) for h in range(NH)]

            def emit_scores(idx):
                s_i, h = heads[idx]
                qcols = slice(256 * s_i, 256 * (s_i + 1))

                def _t(t, s_i=s_i, h=h, qcols=qcols, sa=sigA_done, sb=sigB_done):
                    if sa:
                        t.wait_ge(s_s, sa)
                    for j in range(4):
                        t.matmul(ps_scA[:, 256 * j:256 * (j + 1)],
                                 krT_sb[:, 128 * (2 * s_i + j):128 * (2 * s_i + j + 1)],
                                 qrT[:, h, qcols], start=True,
                                 stop=True).then_inc(s_mm, 1) if j == 3 else \
                        t.matmul(ps_scA[:, 256 * j:256 * (j + 1)],
                                 krT_sb[:, 128 * (2 * s_i + j):128 * (2 * s_i + j + 1)],
                                 qrT[:, h, qcols], start=True, stop=True)
                    if sb:
                        t.wait_ge(s_s, sb)
                    for j in range(4):
                        m = t.matmul(ps_scB[:, 256 * j:256 * (j + 1)],
                                     krT_sb[:, 128 * (2 * s_i + 2 + j):128 * (2 * s_i + 3 + j)],
                                     qrT[:, h, qcols], start=True, stop=True)
                    m.then_inc(s_mm, 1)
                block.tensor(_t)
                evA = ctx.bump("s_mm")
                evB = ctx.bump("s_mm")
                return evA, evB

            def emit_sig(idx, evA, evB):
                nonlocal sigA_done, sigB_done
                s_i, h = heads[idx]

                sl = sigAB[idx % 2]
                sigA = sl[:, 0:1024]
                sigB = sl[:, 1024:2048]

                def _s(s, s_i=s_i, evA=evA, evB=evB, sigA=sigA, sigB=sigB):
                    s.wait_ge(s_mm, evA)
                    s.activation(sigA, ps_scA[:], AF.Sigmoid,
                                 scale=SCALE).then_inc(s_s, 1)
                    s.wait_ge(s_mm, evB)
                    if s_i == SEGS - 1:
                        s.activation(sigB[:, 0:512], ps_scB[:, 0:512],
                                     AF.Sigmoid, scale=SCALE)
                        s.activation(sigB[:, 512:768], ps_scB[:, 512:768], AF.Sigmoid,
                                     scale=SCALE, bias=mb_sb[:, 0:1])
                        s.activation(sigB[:, 768:1024], ps_scB[:, 768:1024], AF.Sigmoid,
                                     scale=SCALE, bias=mb_sb[:, 1:2])
                    else:
                        s.activation(sigB, ps_scB[:], AF.Sigmoid, scale=SCALE)
                    s.activation(jnk_s[:, 1:2], sigB[:, 1023:1024],
                                 AF.Sigmoid).then_inc(s_s, 1)
                block.scalar(_s)
                sigA_done = ctx.bump("s_s")
                sigB_done = ctx.bump("s_s")

            def emit_wout(idx, sd):
                s_i, h = heads[idx]
                sl = sigAB[idx % 2]
                sigA = sl[:, 0:1024]
                sigB = sl[:, 1024:2048]

                def _t2(t, s_i=s_i, sd=sd, bd=blend_done, sigA=sigA, sigB=sigB):
                    t.wait_ge(s_s, sd)
                    if bd:
                        t.wait_ge(s_v, bd)
                    for qc in range(2):
                        off = 132 * qc
                        for j in range(4):
                            t.matmul(ps_wA[:, off:off + 129],
                                     sigA[:, 256 * j + 128 * qc:256 * j + 128 * (qc + 1)],
                                     vaug_sb[:, 2 * s_i + j, 0:129],
                                     start=(j == 0), stop=(j == 3))
                        for j in range(4):
                            m = t.matmul(ps_wB[:, off:off + 129],
                                         sigB[:, 256 * j + 128 * qc:256 * j + 128 * (qc + 1)],
                                         vaug_sb[:, 2 * s_i + 2 + j, 0:129],
                                         start=(j == 0), stop=(j == 3))
                    m.then_inc(s_mm, 1)
                block.tensor(_t2)
                return ctx.bump("s_mm")

            def emit_blend(idx, ev_w2):
                nonlocal blend_done
                s_i, h = heads[idx]

                def _v(v, s_i=s_i, h=h, ev=ev_w2, ag=anat_guard):
                    v.wait_ge(s_mm, ev)
                    if ag and h == 0:
                        v.wait_ge(s_tr, ag)
                    sA2 = ps_wA[:, 0:264].rearrange("p (a b) -> p a b", a=2, b=132)[
                        :, :, 128:129].rearrange("p a b -> p (a b)")
                    sB2 = ps_wB[:, 0:264].rearrange("p (a b) -> p a b", a=2, b=132)[
                        :, :, 128:129].rearrange("p a b -> p (a b)")
                    v.reciprocal(bl_r[:, 0:2], sA2)
                    v.reciprocal(bl_r[:, 2:4], sB2)
                    vsync(v)
                    acols = slice(2 * s_i, 2 * s_i + 2)
                    v.tensor_tensor(bl_r[:, 4:6], bl_r[:, 0:2], aA_sb[:, acols], ALU.mult)
                    v.tensor_tensor(bl_r[:, 6:8], bl_r[:, 2:4], aB_sb[:, acols], ALU.mult)
                    vsync(v)
                    for qc in range(2):
                        ts1(v, bl_t[:, 128 * qc:128 * (qc + 1)],
                            ps_wA[:, 132 * qc:132 * qc + 128],
                            bl_r[:, 4 + qc:5 + qc], ALU.mult)
                    for qc in range(2):
                        v.scalar_tensor_tensor(
                            anat[qc][:, 128 * h:128 * (h + 1)],
                            ps_wB[:, 132 * qc:132 * qc + 128],
                            bl_r[:, 6 + qc:7 + qc],
                            bl_t[:, 128 * qc:128 * (qc + 1)],
                            ALU.mult, ALU.add)
                    v.tensor_copy(jnk_v[:, 3:4],
                                  anat[1][:, 128 * h:128 * h + 1]).then_inc(s_v, 1)
                block.vector(_v)
                blend_done = ctx.bump("s_v")

            pend = None
            for idx in range(len(heads)):
                evA, evB = emit_scores(idx)
                emit_sig(idx, evA, evB)
                my_sig = sigB_done
                if pend is not None:
                    ev_w2 = emit_wout(pend[0], pend[1])
                    emit_blend(pend[0], ev_w2)
                pend = (idx, my_sig)
                s_i, h = heads[idx]
                if h == NH - 1 and pend is not None:
                    ev_w2 = emit_wout(pend[0], pend[1])
                    emit_blend(pend[0], ev_w2)
                    pend = None
                    for qc in range(2):
                        ti = 2 * s_i + qc

                        def _tr(sy, ti=ti, qc=qc, bd=blend_done):
                            sy.wait_ge(s_v, bd)
                            sy.dma_start_transpose(
                                attnT[:, :, 128 * ti:128 * (ti + 1)],
                                anat[qc]).then_inc(s_tr, 16)
                        block.sync(_tr)
                        anat_tr[ti] = ctx.bump("s_tr", 16)
                    anat_guard = anat_tr[2 * s_i + 1]

            if tap == "wlast":
                wdbg = vf(RE + 8192, T_OWN)

                def _vdbg(v, bd=blend_done):
                    v.wait_ge(s_v, bd)
                    v.tensor_copy(wdbg[:, 0:264], ps_wA.bitcast(F32)[:, 0:264])
                    v.tensor_copy(wdbg[:, 264:528], ps_wB.bitcast(F32)[:, 0:264]).then_inc(s_v, 1)
                block.vector(_vdbg)
                ctx.bump("s_v")
                finish_tap(wdbg, [128, T_OWN], F32)
                return nc, tap_dram[0]
            if tap == "sclast":
                wdbg2 = vf(RE + 8192, T_OWN)

                def _vdbg2(v, ev=ev_sc):
                    v.wait_ge(s_mm, ev)
                    v.tensor_copy(wdbg2[:], ps_scB[:]).then_inc(s_v, 1)
                block.vector(_vdbg2)
                ctx.bump("s_v")
                finish_tap(wdbg2, [128, T_OWN], F32)
                return nc, tap_dram[0]
            if tap == "siglast":
                finish_tap(sigB, [128, T_OWN], BF16)
                return nc, tap_dram[0]
            if tap == "attnT":
                finish_tap(attnT, [128, NH, T_OWN], BF16)
                return nc, tap_dram[0]
            attn_tr_last = anat_tr[2 * SEGS - 1]
            barrier()

        def big_proj(wdram, rhs_fn, dst, delta_fn=None, first_wait=None):
            with ExitStack() as ps_st2:
                pe2 = ps_st2.enter_context
                ps_p = [pe2(nc.psum_tensor(f"ps_p{ctx.bump('psn')}", [128, 512], F32))
                        for _ in range(2)]
                wb_free = [0, 0]
                ps_free = [0, 0]
                for ot in range(FC):
                    slot = ot % 2

                    def _g(g, ot=ot, slot=slot, w=wb_free[slot]):
                        if w:
                            g.wait_ge(s_mm, w)
                        g.dma_start(wbuf[slot], wdram[ot]).then_inc(s_w, 16)
                    block.gpsimd(_g)
                    ev_w = ctx.bump("s_w", 16)

                    last = None
                    for th in range(2):
                        def _t(t, ot=ot, th=th, slot=slot, ev_w=ev_w, pf=ps_free[th]):
                            if ot == 0 and th == 0 and first_wait:
                                for sm, vv in first_wait:
                                    t.wait_ge(sm, vv)
                            t.wait_ge(s_w, ev_w)
                            if pf:
                                t.wait_ge(s_s, pf)
                            first = True
                            if delta_fn is not None:
                                delta_fn(t, ps_p[th], ot, th)
                                first = False
                            for fc in range(FC):
                                m = t.matmul(ps_p[th][:], wbuf[slot][:, fc, :],
                                             rhs_fn(fc, th),
                                             start=(first and fc == 0),
                                             stop=(fc == FC - 1))
                            m.then_inc(s_mm, 1)
                        block.tensor(_t)
                        ev_mm = ctx.bump("s_mm")
                        last = ev_mm

                        def _s(s, ot=ot, th=th, ev=ev_mm):
                            s.wait_ge(s_mm, ev)
                            s.activation(dst[:, ot, 512 * th:512 * (th + 1)],
                                         ps_p[th][:], AF.Identity)
                            s.activation(jnk_s[:, 1:2],
                                         dst[:, ot, 512 * th:512 * th + 1],
                                         AF.Identity).then_inc(s_s, 1)
                        block.scalar(_s)
                        ps_free[th] = ctx.bump("s_s")
                    wb_free[slot] = last

        big_proj(wo_t,
                 lambda fc, th: attnT[:, fc, 512 * th:512 * (th + 1)],
                 projT, first_wait=[(s_tr, attn_tr_last)])
        if tap == "projT":
            finish_tap(projT, [128, NH, T_OWN], BF16)
            return nc, tap_dram[0]
        barrier()

        def resid_phase(srcT, natv, is_final, tr_done=None):
            if tr_done is not None:
                ptr = tr_done
            else:
                ptr = 0
                for ot in range(FC):
                    def _tr(sy, ot=ot):
                        sy.dma_start_transpose(natv[:, :, 128 * ot:128 * (ot + 1)],
                                              srcT[:, ot, :]).then_inc(s_tr, 16)
                    block.sync(_tr)
                    ptr = ctx.bump("s_tr", 16)

            if is_final:
                xnfs = [xnf, vf(RA + 16384, D)]
                t1fs = [t1f, vf(RC, D)]
            else:
                xnfs = [xnf, xnf]
                t1fs = [t1f, t1f]

            def _ld0(g, ptr=ptr):
                g.wait_ge(s_tr, ptr)
                if is_final:
                    g.wait_ge(s_res, ctx.val("s_res"))
                src = (xres[0:128, :] if is_final else xslab[256:384, :])
                g.dma_start(xnfs[0], src).then_inc(s_x2, 16)
            block.gpsimd(_ld0)
            ld_ev = [ctx.bump("s_x2", 16), 0]

            add_ev = [0] * NT_OWN
            ln2_tr_local = {}
            ln2_last = 0
            for i in range(NT_OWN):
                sl = i % 2
                pre = [(s_tr, ptr)] if i == 0 else None
                if is_final:
                    pre3 = [(s_v, add_ev[i - 2])] if i >= 2 else None
                else:
                    pre3 = [(s_v, add_ev[i - 1])] if i >= 1 else None
                ev_rv = ln_bnstats(natv[:, i, :], i % 4, pre_v=pre)
                ev_t1 = ln_fin(natv[:, i, :], i % 4, t1fs[sl], ev_rv, pre_s3=pre3)

                def _v3(v, sl=sl, ev_t1=ev_t1, lev=ld_ev[sl]):
                    v.wait_ge(s_s, ev_t1)
                    v.wait_ge(s_x2, lev)
                    v.tensor_tensor(xnfs[sl], xnfs[sl], t1fs[sl], ALU.add)
                    v.tensor_copy(jnk_v[:, 3:4], xnfs[sl][:, 0:1]).then_inc(s_v, 1)
                block.vector(_v3)
                add_ev[i] = ctx.bump("s_v")

                if is_final:
                    def _st(s2, i=i, sl=sl, ev=add_ev[i]):
                        s2.wait_ge(s_v, ev)
                        s2.dma_start(out[128 * i:128 * (i + 1), :],
                                     xnfs[sl]).then_inc(s_out, 16)
                        nxt = i + 1
                        if nxt < NT_OWN:
                            s2.wait_ge(s_out, 16 * i)
                            s2.dma_start(xnfs[nxt % 2],
                                         xres[128 * nxt:128 * (nxt + 1), :]
                                         ).then_inc(s_x2, 16)
                    block.scalar(_st)
                    ctx.bump("s_out", 16)
                    if i + 1 < NT_OWN:
                        ld_ev[(i + 1) % 2] = ctx.bump("s_x2", 16)
                else:
                    def _st(s2, i=i, ev=add_ev[i]):
                        s2.wait_ge(s_v, ev)
                        s2.dma_start(xres[128 * i:128 * (i + 1), :],
                                     xnf).then_inc(s_res, 16)
                    block.scalar(_st)
                    ev_res = ctx.bump("s_res", 16)

                    pre3b = [(s_tr, ln2_tr_local[i - 2])] if i >= 2 else None
                    ev_rv2 = ln_bnstats(xnf, 4 + i % 4,
                                        pre_v=[(s_v, add_ev[i])])
                    ev_n2 = ln_fin(xnf, 4 + i % 4, lnb[sl], ev_rv2, pre_s3=pre3b)

                    def _tr2(s2, i=i, sl=sl, evr=ev_res, ev=ev_n2):
                        s2.wait_ge(s_s, ev)
                        s2.dma_start_transpose(ln2T[:, :, 128 * i:128 * (i + 1)],
                                               lnb[sl]).then_inc(s_tr, 16)
                        nxt = i + 1
                        if nxt < NT_OWN:
                            s2.wait_ge(s_res, evr)
                            s2.dma_start(xnf, xslab[256 + 128 * nxt:256 + 128 * (nxt + 1), :]
                                         ).then_inc(s_x2, 16)
                    block.scalar(_tr2)
                    ln2_tr_local[i] = ctx.bump("s_tr", 16)
                    ln2_last = ln2_tr_local[i]
                    if i + 1 < NT_OWN:
                        ld_ev[(i + 1) % 2] = ctx.bump("s_x2", 16)
            return ln2_last

        ln2_tr_last = resid_phase(projT, nat8, is_final=False)
        if tap == "ln2T":
            finish_tap(ln2T, [128, FC, T_OWN], BF16)
            return nc, tap_dram[0]
        barrier()

        with ExitStack() as ps_st:
            pe = ps_st.enter_context
            ps_sh = pe(nc.psum_tensor("ps_sh", [64, T_OWN], F32))
            ps_tpf = pe(nc.psum_tensor("ps_tpf", [128, 128], F32))
            ps_oh = pe(nc.psum_tensor("ps_oh", [4, T_OWN], F32))
            ps_ex = pe(nc.psum_tensor("ps_ex", [32, T_OWN], F32))

            def _gms(g):
                g.memset(lgT, 0.0).then_inc(s_g, 1)
            block.gpsimd(_gms)
            ev_ms = ctx.bump("s_g", 1)

            def _tf1(t):
                t.wait_ge(s_tr, ln2_tr_last)
                for th in range(2):
                    for fc in range(FC):
                        m = t.matmul(ps_sh[:, 512 * th:512 * (th + 1)], sh_sb[:, fc, :],
                                     ln2T[:, fc, 512 * th:512 * (th + 1)],
                                     start=(fc == 0), stop=(fc == FC - 1))
                m.then_inc(s_mm, 1)
            block.tensor(_tf1)
            ev_sh = ctx.bump("s_mm")

            def _vf2(v, ev=ev_sh, evm=ev_ms):
                v.wait_ge(s_mm, ev)
                v.wait_ge(s_g, evm)
                ts1(v, lgT[32:36, :], ps_sh[32:36, :], eb_sb[32:36, :], ALU.add)
                v.tensor_copy(jnk_v[0:4, 3:4], lgT[32:36, 0:1]).then_inc(s_v, 1)
            block.vector(_vf2)
            ev_lg = ctx.bump("s_v")

            cp_ev = 0
            for j in range(NT_OWN):
                def _t(t, j=j, ev=ev_lg, ce=cp_ev):
                    if j == 0:
                        t.wait_ge(s_v, ev)
                    if ce:
                        t.wait_ge(s_v, ce)
                    t.transpose(ps_tpf[:], lgT[:, 128 * j:128 * (j + 1)],
                                idf_sb[:]).then_inc(s_mm, 1)
                block.tensor(_t)
                ev_tp = ctx.bump("s_mm")

                def _v(v, j=j, ev=ev_tp):
                    v.wait_ge(s_mm, ev)
                    v.tensor_copy(lgN[:, j, :], ps_tpf[:, 32:36])
                    v.tensor_copy(jnk_v[:, 3:4], lgN[:, j, 0:1]).then_inc(s_v, 1)
                block.vector(_v)
                cp_ev = ctx.bump("s_v")

            def _vcas(v, ce=cp_ev):
                v.wait_ge(s_v, ce)
                for j in range(NT_OWN):
                    v.reduce_max(mcol[:, j:j + 1], lgN[:, j, :], axis=AX)
                vsync(v)
                for j in range(NT_OWN):
                    ts1(v, fflg8[:, j, :], lgN[:, j, :], mcol[:, j:j + 1], ALU.is_equal)
                vsync(v)
                for j in range(NT_OWN):
                    v.tensor_copy(ohn[:, j, 0:1], fflg8[:, j, 0:1])
                for j in range(NT_OWN):
                    v.tensor_scalar(ntA[:, j:j + 1], fflg8[:, j, 0:1], -1.0, 1.0,
                                    ALU.mult, ALU.add)
                vsync(v)
                for j in range(NT_OWN):
                    v.tensor_tensor(ohn[:, j, 1:2], fflg8[:, j, 1:2], ntA[:, j:j + 1],
                                    ALU.mult)
                for j in range(NT_OWN):
                    v.tensor_scalar(ntB[:, j:j + 1], fflg8[:, j, 1:2], -1.0, 1.0,
                                    ALU.mult, ALU.add)
                vsync(v)
                for j in range(NT_OWN):
                    v.tensor_tensor(ntC[:, j:j + 1], ntA[:, j:j + 1], ntB[:, j:j + 1],
                                    ALU.mult)
                vsync(v)
                for j in range(NT_OWN):
                    v.tensor_tensor(ohn[:, j, 2:3], fflg8[:, j, 2:3], ntC[:, j:j + 1],
                                    ALU.mult)
                for j in range(NT_OWN):
                    v.tensor_scalar(ntB[:, j:j + 1], fflg8[:, j, 2:3], -1.0, 1.0,
                                    ALU.mult, ALU.add)
                vsync(v)
                for j in range(NT_OWN):
                    v.tensor_tensor(ntA[:, j:j + 1], ntC[:, j:j + 1], ntB[:, j:j + 1],
                                    ALU.mult)
                vsync(v)
                for j in range(NT_OWN):
                    v.tensor_tensor(ohn[:, j, 3:4], fflg8[:, j, 3:4], ntA[:, j:j + 1],
                                    ALU.mult)
                v.tensor_copy(jnk_v[:, 1:2], ohn[:, NT_OWN - 1, 3:4]).then_inc(s_v, 1)
            block.vector(_vcas)
            oh_ev = ctx.bump("s_v")

            tpb_last = 0
            for j in range(NT_OWN):
                def _t(t, j=j, ev=oh_ev):
                    if j == 0:
                        t.wait_ge(s_v, ev)
                    t.transpose(ps_oh[:, 128 * j:128 * (j + 1)], ohn[:, j, :],
                                idf_sb[:]).then_inc(s_mm, 1)
                block.tensor(_t)
                tpb_last = ctx.bump("s_mm")

            def _vf3(v, ev=tpb_last):
                v.wait_ge(s_mm, ev)
                v.tensor_copy(ohT_sb[0:4, :], ps_oh[:])
                v.tensor_copy(jnk_v[0:4, 3:4], ohT_sb[0:4, 0:1]).then_inc(s_v, 1)
            block.vector(_vf3)
            ev_ohT = ctx.bump("s_v")

            def _tf4(t, ev=ev_ohT):
                t.wait_ge(s_v, ev)
                for th in range(2):
                    m = t.matmul(ps_ex[:, 512 * th:512 * (th + 1)], ee_sb[:],
                                 ohT_sb[0:4, 512 * th:512 * (th + 1)], start=True, stop=True)
                m.then_inc(s_mm, 1)
            block.tensor(_tf4)
            ev_ex = ctx.bump("s_mm")

            def _vf5(v, ev=ev_ex):
                v.wait_ge(s_mm, ev)
                v.tensor_copy(ex_sb[0:32, :], ps_ex[:])
                vsync(v)
                v.tensor_tensor(zmT[0:32, :], ps_sh[0:32, :], ex_sb[0:32, :], ALU.mult)
                v.tensor_copy(jnk_v[0:32, 3:4], zmT[0:32, 0:1]).then_inc(s_v, 1)
            block.vector(_vf5)
            ev_zm = ctx.bump("s_v")

            def _tf6(t, ev=ev_zm):
                t.wait_ge(s_v, ev)
            block.tensor(_tf6)

            if tap == "zmT":
                finish_tap(zmT[0:32, :], [32, T_OWN], BF16)
                return nc, tap_dram[0]

        def delta_fn(t, ps, ot, th):
            t.matmul(ps[:], bfl_sb[:, 128 * ot:128 * (ot + 1)],
                     zmT[0:32, 512 * th:512 * (th + 1)], start=True, stop=False)

        big_proj(wup_t,
                 lambda fc, th: ln2T[:, fc, 512 * th:512 * (th + 1)],
                 ffnT, delta_fn=delta_fn, first_wait=[(s_tr, ln2_tr_last)])

        if tap == "ffnT":
            finish_tap(ffnT, [128, NH, T_OWN], BF16)
            return nc, tap_dram[0]
        barrier()

        resid_phase(ffnT, natG, is_final=True)

        def _fin(sy):
            sy.wait_ge(s_out, ctx.val("s_out"))
        block.sync(_fin)

    return nc, tap_dram[0]



def _rope_cache(S_, D_, base=10000.0):
    inv_freq = 1.0 / base ** (np.arange(0, D_, 2, dtype=np.float32) / D_)
    t = np.arange(S_, dtype=np.float32)
    f = np.outer(t, inv_freq)
    cos = np.zeros((S_, D_), dtype=np.float32)
    sin = np.zeros((S_, D_), dtype=np.float32)
    cos[:, 0::2] = np.cos(f)
    cos[:, 1::2] = np.cos(f)
    sin[:, 0::2] = np.sin(f)
    sin[:, 1::2] = np.sin(f)
    return cos, sin


def _tile_w(wT):
    f, o = wT.shape
    if o <= 128:
        return np.ascontiguousarray(wT.reshape(FC, 128, o).transpose(1, 0, 2))
    return np.ascontiguousarray(
        wT.reshape(FC, 128, o // 128, 128).transpose(2, 1, 0, 3))


def make_inputs(x, Wq, Wk, Wv, Wo, W_up, gate_w, expert_bias, lora_A, lora_B):
    cos, sin = _rope_cache(S, DH)
    wq_t = _tile_w(np.ascontiguousarray(Wq.T))
    wo_t = _tile_w(np.ascontiguousarray(Wo.T))
    wup_t = _tile_w(np.ascontiguousarray(W_up.T))
    wk_t = _tile_w(np.ascontiguousarray(Wk.T))
    wv_t = _tile_w(np.ascontiguousarray(Wv.T))
    shW = np.zeros((D, 64), np.float32)
    shW[:, 0:32] = lora_A.reshape(32, D).T
    shW[:, 32:36] = gate_w.T
    sh_t = _tile_w(shW)
    bflat = np.ascontiguousarray(lora_B.transpose(0, 2, 1).reshape(32, D))
    eexp = np.zeros((4, 32), np.float32)
    for ei in range(4):
        eexp[ei, ei * 8:(ei + 1) * 8] = 1.0
    identf = np.eye(128, dtype=np.float32)
    pmat = np.zeros((128, 128), np.float32)
    for d_ in range(64):
        pmat[d_, d_ + 64] = -1.0
        pmat[d_ + 64, d_] = 1.0
    ptmat = np.ascontiguousarray(pmat.T)
    lin = np.linspace(0.0, 1.0, 256, dtype=np.float32)

    maps = []
    for c in range(8):
        b, j = divmod(c, 4)
        g0 = 1024 * j - 256
        xs = np.zeros((T_SLAB, D), np.float32)
        lo, hi = max(0, g0), min(S, g0 + T_SLAB)
        xs[lo - g0:hi - g0, :] = x[b, lo:hi, :]
        ct = np.zeros((128, T_SLAB), np.float32)
        sn = np.zeros((128, T_SLAB), np.float32)
        ct[:, lo - g0:hi - g0] = cos[lo:hi].T
        sn[:, lo - g0:hi - g0] = sin[lo:hi].T
        aA = np.zeros((128, 2 * SEGS), np.float32)
        aB = np.zeros((128, 2 * SEGS), np.float32)
        for s_i in range(SEGS):
            g = 4 * j + s_i
            ab = np.ones(256, np.float32) if g == 0 else lin
            for qc in range(2):
                aB[:, 2 * s_i + qc] = ab[128 * qc:128 * (qc + 1)]
                aA[:, 2 * s_i + qc] = np.float32(1.0) - ab[128 * qc:128 * (qc + 1)]
        mb = np.zeros((128, 2), np.float32)
        if j == 3:
            mb[:] = -200.0
        maps.append(dict(
            xslab=xs, wq_t=wq_t, wo_t=wo_t, wup_t=wup_t, wk_t=wk_t, wv_t=wv_t,
            sh_t=sh_t, bflat=bflat,
            ebias=np.asarray(expert_bias, np.float32).reshape(4, 1),
            eexp=eexp, identf=identf, ptmat=ptmat, cosT=ct, sinT=sn,
            alphaA=aA, alphaB=aB, maskb=mb))
    return maps


def kernel(L0_attn_x, L0_attn_Wq, L0_attn_Wk, L0_attn_Wv, L0_attn_Wo,
           L0_ffn_W_up, L0_ffn_gate_w, L0_ffn_expert_bias, L0_ffn_lora_A,
           L0_ffn_lora_B):
    x = np.asarray(L0_attn_x, np.float32)
    maps = make_inputs(x, np.asarray(L0_attn_Wq, np.float32),
                       np.asarray(L0_attn_Wk, np.float32),
                       np.asarray(L0_attn_Wv, np.float32),
                       np.asarray(L0_attn_Wo, np.float32),
                       np.asarray(L0_ffn_W_up, np.float32),
                       np.asarray(L0_ffn_gate_w, np.float32),
                       np.asarray(L0_ffn_expert_bias, np.float32),
                       np.asarray(L0_ffn_lora_A, np.float32),
                       np.asarray(L0_ffn_lora_B, np.float32))
    nc, _ = build_program()
    res = run_bass_kernel_spmd(nc, maps, core_ids=list(range(8)))
    full = np.empty((B, S, D), np.float32)
    for c in range(8):
        b, j = divmod(c, 4)
        full[b, 1024 * j:1024 * (j + 1), :] = res.results[c]["out"]
    return full
